# version 01
# baseline (speedup 1.0000x reference)
import numpy as np
from contextlib import ExitStack

import concourse.bass as bass
import concourse.mybir as mybir
from concourse.bass_utils import run_bass_kernel_spmd

F32 = mybir.dt.float32
AF = mybir.ActivationFunctionType
AL = mybir.AluOpType
AX = mybir.AxisListType

D = 1024
H = 16
DH = 64
E = 8
DFF = 4096
FC = D // 128
GDFF = 512
NG = DFF // GDFF
LN_EPS = 1e-5
NORM_EPS = 1e-12


class Ctr:
    def __init__(self, sem):
        self.sem = sem
        self.n = 0


def build(T=2048, debug=False):
    NT = T // 128
    FTT = min(512, T)
    NFT = T // FTT
    NQC = max(1, T // 512)
    QCW = min(512, T)
    NKC = T // 128
    NH1 = GDFF // 128

    nc = bass.Bass("TRN2", target_bir_lowering=False)

    dx = nc.dram_tensor("x", [T, D], F32, kind="ExternalInput").ap()
    dwq = nc.dram_tensor("wq", [D, D], F32, kind="ExternalInput").ap()
    dwk = nc.dram_tensor("wk", [D, D], F32, kind="ExternalInput").ap()
    dwv = nc.dram_tensor("wv", [D, D], F32, kind="ExternalInput").ap()
    dwo = nc.dram_tensor("wo", [D, D], F32, kind="ExternalInput").ap()
    dbo = nc.dram_tensor("bo", [1, D], F32, kind="ExternalInput").ap()
    dgamma = nc.dram_tensor("gamma_fm", [128, FC], F32, kind="ExternalInput").ap()
    dbeta = nc.dram_tensor("beta_fm", [128, FC], F32, kind="ExternalInput").ap()
    dgq = nc.dram_tensor("gq_rep", [128, DH], F32, kind="ExternalInput").ap()
    dgk = nc.dram_tensor("gk_rep", [128, DH], F32, kind="ExternalInput").ap()
    dwr = nc.dram_tensor("wr_sb", [128, FC * E], F32, kind="ExternalInput").ap()
    dbr = nc.dram_tensor("br", [1, E], F32, kind="ExternalInput").ap()
    dW1 = nc.dram_tensor("W1", [E, D, DFF], F32, kind="ExternalInput").ap()
    db1 = nc.dram_tensor("b1_fm", [128, E * (DFF // 128)], F32, kind="ExternalInput").ap()
    dW2 = nc.dram_tensor("W2", [E, DFF, D], F32, kind="ExternalInput").ap()
    db2 = nc.dram_tensor("b2", [1, E * D], F32, kind="ExternalInput").ap()
    deye = nc.dram_tensor("eye", [128, 128], F32, kind="ExternalInput").ap()
    dout = nc.dram_tensor("out", [T, D], F32, kind="ExternalOutput").ap()
    dqn = nc.dram_tensor("qn_s", [FC, 128, T], F32).ap()
    dkn = nc.dram_tensor("kn_s", [FC, 128, T], F32).ap()
    dvg = nc.dram_tensor("vg_s", [T, D], F32).ap()
    dbg = {}
    if debug:
        for name, shp in [("dbg_g", [T, D]), ("dbg_oh", [T, E]), ("dbg_gate", [T, 1]),
                          ("dbg_q", [T, D]), ("dbg_k", [T, D]),
                          ("dbg_gq", [T, H]), ("dbg_gk", [T, H]),
                          ("dbg_fused", [128, FC * T]), ("dbg_csum", [1, D]),
                          ("dbg_xn", [T, D])]:
            dbg[name] = nc.dram_tensor(name, shp, F32, kind="ExternalOutput").ap()

    plan = {k: [] for k in ("sync", "vector", "scalar", "tensor", "gpsimd")}

    def op(eng, fn, inc=None, k=1, waits=()):
        waits = [w for w in waits if w is not None]
        for (c, v) in waits:
            assert v <= c.n, f"wait on future value: {v} > {c.n}"
        if inc is not None:
            inc.n += k
        def run(e, fn=fn, inc=inc, k=k, waits=tuple(waits)):
            for (c, v) in waits:
                if v > 0:
                    e.wait_ge(c.sem, v)
            ins = fn(e)
            if inc is not None:
                ins.then_inc(inc.sem, k)
        plan[eng].append(run)

    def drain(eng):
        plan[eng].append(lambda e: e.drain())

    ctx = ExitStack()
    with ctx:
        sb = lambda n, s: ctx.enter_context(nc.sbuf_tensor(n, s, F32))
        xnT = sb("xnT", [128, FC * T])
        yb = sb("yb", [128, NT * D])
        Wbuf = sb("Wbuf", [128, 8 * D])
        xt = sb("xt", [128, D])
        dt_ = sb("dt_", [128, D])
        sqt = sb("sqt", [128, D])
        qt = sb("qt", [128, D])
        s2a = sb("s2a", [128, D])
        s2b = sb("s2b", [128, D])
        h1g = sb("h1g", [128, NH1 * FTT])
        stats = sb("stats", [128, 32])
        nhead = sb("nhead", [128, 4 * H])
        gq_tm = sb("gq_tm", [128, NT * H])
        gk_tm = sb("gk_tm", [128, NT * H])
        oh_tm = sb("oh_tm", [128, NT * E])
        gatev = sb("gatev", [128, NT])
        gammat = sb("gammat", [128, FC])
        betat = sb("betat", [128, FC])
        gqrep = sb("gqrep", [128, DH])
        gkrep = sb("gkrep", [128, DH])
        wr_sb = sb("wr_sbuf", [128, FC * E])
        b1_sb = sb("b1_sbuf", [128, E * (DFF // 128)])
        eyet = sb("eyet", [128, 128])
        cst = sb("cst", [128, 8])
        onesr = sb("onesr", [1, 512])
        ones1 = sb("ones128", [1, 128])
        onesc = sb("onescol", [128, 1])
        brow = sb("brow", [1, D])
        br_sb = sb("br_sbuf", [1, E])
        csum = sb("csum_row", [1, D])
        QN_OFF, KN_OFF = 0, T
        VG_OFF = 2 * T
        TS_OFF = VG_OFF + NKC * DH
        GB_OFF = TS_OFF + 1024
        def qnh_(a, b):
            return Wbuf[0:64, QN_OFF + a:QN_OFF + b]
        def knh_(a, b):
            return Wbuf[0:64, KN_OFF + a:KN_OFF + b]
        def vgh_(a, b):
            return Wbuf[0:128, VG_OFF + a:VG_OFF + b]
        def tsb_(a, b):
            return Wbuf[0:128, TS_OFF + a:TS_OFF + b]
        gqbc = Wbuf[0:128, GB_OFF:GB_OFF + 512]
        gqrow = Wbuf[0:1, GB_OFF + 512:GB_OFF + 1024]
        pb = [ctx.enter_context(nc.psum_tensor(f"pb{i}", [128, 512], F32))
              for i in range(8)]

        s_ld = Ctr(ctx.enter_context(nc.semaphore("s_ld")))
        s_mm = Ctr(ctx.enter_context(nc.semaphore("s_mm")))
        s_ve = Ctr(ctx.enter_context(nc.semaphore("s_ve")))
        s_sc = Ctr(ctx.enter_context(nc.semaphore("s_sc")))
        s_sst = Ctr(ctx.enter_context(nc.semaphore("s_sst")))
        s_out = Ctr(ctx.enter_context(nc.semaphore("s_out")))

        CI1024 = cst[:, 0:1]
        CEPS = cst[:, 1:2]
        C15 = cst[:, 2:3]
        CM405 = cst[:, 3:4]
        fusedT = xnT

        xnT3 = xnT[:].rearrange("p (c t) -> p c t", t=T)

        def ldi(dst, src):
            op("sync", lambda e, d=dst, s=src: e.dma_start(out=d, in_=s),
               inc=s_ld, k=16)

        ldi(gammat[:], dgamma)
        ldi(betat[:], dbeta)
        ldi(gqrep[:], dgq)
        ldi(gkrep[:], dgk)
        ldi(wr_sb[:], dwr)
        ldi(b1_sb[:], db1)
        ldi(eyet[:], deye)
        ldi(br_sb[:], dbr)
        for aptgt, val in [(cst[:, 0:1], 1.0 / D), (cst[:, 1:2], LN_EPS),
                           (cst[:, 2:3], 15.0), (cst[:, 3:4], -4.05),
                           (onesr[:], 1.0), (ones1[:], 1.0), (onesc[:], 1.0)]:
            op("vector", lambda e, a=aptgt, v=val: e.memset(a, v), inc=s_ve)
        op("vector", lambda e: e.memset(yb[:], 0.0), inc=s_ve)
        setup_ve = s_ve.n

        def ln_tile(src, out, ld_wait=None):
            drain("vector")
            op("vector", lambda e: e.reduce_sum(stats[:, 0:1], src, axis=AX.X),
               waits=[ld_wait] if ld_wait else [])
            drain("vector")
            op("vector", lambda e: e.tensor_scalar_mul(stats[:, 1:2], stats[:, 0:1], 1.0 / D))
            drain("vector")
            op("vector", lambda e: e.tensor_scalar(dt_[:], src, stats[:, 1:2],
                                                   None, AL.subtract), inc=s_ve)
            op("scalar", lambda e: e.activation(sqt[:], dt_[:], AF.Square,
                                                accum_out=stats[:, 2:3]),
               waits=[(s_ve, s_ve.n)], inc=s_sc)
            drain("scalar")
            op("scalar", lambda e: e.activation(stats[:, 3:4], stats[:, 2:3],
                                                AF.Sqrt, scale=CI1024, bias=CEPS),
               inc=s_sc)
            op("vector", lambda e: e.reciprocal(stats[:, 4:5], stats[:, 3:4]),
               waits=[(s_sc, s_sc.n)])
            drain("vector")
            op("vector", lambda e, out=out: e.tensor_scalar(out, dt_[:], stats[:, 4:5],
                                                            None, AL.mult), inc=s_ve)

        tr_evac = {6: 0, 7: 0}

        def transpose8(src, evac_fn_for_half):
            src_ready = s_ve.n
            for half in (0, 1):
                pbi = 6 + half
                def tmf(e, half=half, pbi=pbi, src=src):
                    ins = None
                    for c in range(4):
                        cc = half * 4 + c
                        ins = e.transpose(pb[pbi][:, c * 128:(c + 1) * 128],
                                          src[:, cc * 128:(cc + 1) * 128], eyet[:])
                    return ins
                op("tensor", tmf, inc=s_mm,
                   waits=[(s_ve, src_ready),
                          ((s_ve, tr_evac[pbi]) if tr_evac[pbi] else None)])
                evac_fn_for_half(half, pb[pbi])
                tr_evac[pbi] = s_ve.n

        for tl in range(NT):
            op("sync", lambda e, tl=tl: e.dma_start(out=xt[:],
                                                    in_=dx[tl * 128:(tl + 1) * 128, :]),
               inc=s_ld, k=16, waits=[(s_ve, s_ve.n if tl else setup_ve)])
            ln_tile(xt[:], qt[:], ld_wait=(s_ld, s_ld.n))
            def mk_evac(half, pbk, tl=tl):
                def evf(e, half=half, pbk=pbk, tl=tl):
                    dstv = xnT3[:, half * 4:(half + 1) * 4, tl * 128:(tl + 1) * 128]
                    return e.tensor_copy(dstv, pbk[:].rearrange("p (c t) -> p c t", t=128))
                op("vector", evf, waits=[(s_mm, s_mm.n)], inc=s_ve)
            transpose8(qt[:], mk_evac)
            if debug:
                op("scalar", lambda e, tl=tl: e.dma_start(
                    out=dbg["dbg_xn"][tl * 128:(tl + 1) * 128, :], in_=qt[:]),
                   waits=[(s_ve, s_ve.n)], inc=s_sst, k=16)
        xnT_ready = s_ve.n

        for tl in range(NT):
            def rmm(e, tl=tl):
                ins = None
                for c in range(FC):
                    ins = e.matmul(pb[5][:, 0:E], xnT3[:, c, tl * 128:(tl + 1) * 128],
                                   wr_sb[:, c * E:(c + 1) * E],
                                   start=(c == 0), stop=False)
                ins = e.matmul(pb[5][:, 0:E], ones1[0:1, :], br_sb[0:1, :],
                               start=False, stop=True)
                return ins
            op("tensor", rmm, inc=s_mm, waits=[(s_ve, s_ve.n), (s_ld, s_ld.n)])
            op("vector", lambda e: e.tensor_copy(stats[:, 8:8 + E], pb[5][:, 0:E]),
               waits=[(s_mm, s_mm.n), (s_sc, s_sc.n)], inc=s_ve)
            drain("vector")
            op("vector", lambda e: e.reduce_max(stats[:, 16:17], stats[:, 8:8 + E],
                                                axis=AX.X, negate=True))
            drain("vector")
            op("vector", lambda e: e.tensor_scalar_mul(stats[:, 17:18],
                                                       stats[:, 16:17], -1.0))
            drain("vector")
            op("vector", lambda e, tl=tl: e.tensor_scalar(oh_tm[:, tl * E:(tl + 1) * E],
                                                          stats[:, 8:8 + E],
                                                          stats[:, 17:18], None,
                                                          AL.is_equal), inc=s_ve)
            op("scalar", lambda e: e.activation(stats[:, 18:18 + E], stats[:, 8:8 + E],
                                                AF.Exp, bias=stats[:, 16:17],
                                                accum_out=stats[:, 26:27]),
               waits=[(s_ve, s_ve.n)], inc=s_sc)
            op("vector", lambda e, tl=tl: e.reciprocal(gatev[:, tl:tl + 1],
                                                       stats[:, 26:27]),
               waits=[(s_sc, s_sc.n)], inc=s_ve)
        router_ve = s_ve.n

        last_w_use_mm = 0
        pb_h_evac = {0: 0, 1: 0}
        pb_y_evac = {4: 0, 5: router_ve}
        for e_i in range(E):
            op("sync", lambda e, e_i=e_i: e.dma_start(
                out=brow[0:1, :], in_=db2[0:1, e_i * D:(e_i + 1) * D]),
               inc=s_ld, k=16,
               waits=[(s_mm, last_w_use_mm) if last_w_use_mm else None])
            b2_ld = s_ld.n
            for g in range(NG):
                def l1(en, e_i=e_i, g=g):
                    dstv = Wbuf[:, 0:FC * GDFF].rearrange("p (c f) -> p c f", f=GDFF)
                    srcv = dW1[e_i, :, g * GDFF:(g + 1) * GDFF].rearrange(
                        "(c p) f -> p c f", p=128)
                    return en.dma_start(out=dstv, in_=srcv)
                def l2(en, e_i=e_i, g=g):
                    dstv = Wbuf[:, FC * GDFF:FC * GDFF + NH1 * D].rearrange(
                        "p (c j) -> p c j", j=D)
                    srcv = dW2[e_i, g * GDFF:(g + 1) * GDFF, :].rearrange(
                        "(c p) j -> p c j", p=128)
                    return en.dma_start(out=dstv, in_=srcv)
                op("sync", l1, inc=s_ld, k=16,
                   waits=[(s_mm, last_w_use_mm) if last_w_use_mm else None])
                op("sync", l2, inc=s_ld, k=16)
                w_ld = s_ld.n
                for tt in range(NFT):
                    h1_all = 0
                    for fc in range(NH1):
                        pbk = fc % 2
                        def h1mm(e, g=g, tt=tt, fc=fc, pbk=pbk):
                            ins = None
                            for c in range(FC):
                                lhsT = Wbuf[:, c * GDFF + fc * 128:
                                            c * GDFF + (fc + 1) * 128]
                                rhs = xnT3[:, c, tt * FTT:(tt + 1) * FTT]
                                ins = e.matmul(pb[pbk][:, 0:FTT], lhsT, rhs,
                                               start=(c == 0), stop=(c == FC - 1))
                            return ins
                        op("tensor", h1mm, inc=s_mm,
                           waits=[(s_ld, w_ld),
                                  ((s_sc, pb_h_evac[pbk]) if pb_h_evac[pbk] else None)])
                        bidx = e_i * (DFF // 128) + g * NH1 + fc
                        op("scalar", lambda e, fc=fc, bidx=bidx, pbk=pbk:
                           e.activation(h1g[:, fc * FTT:(fc + 1) * FTT],
                                        pb[pbk][:, 0:FTT], AF.Relu,
                                        bias=b1_sb[:, bidx:bidx + 1]),
                           waits=[(s_mm, s_mm.n), (s_ve, setup_ve)], inc=s_sc)
                        pb_h_evac[pbk] = s_sc.n
                        h1_all = s_sc.n
                    for ts in range(FTT // 128):
                        for jd in range(2):
                            pbk = 4 + (ts * 2 + jd) % 2
                            def ymm(e, e_i=e_i, g=g, ts=ts, jd=jd, pbk=pbk):
                                ins = None
                                for fc in range(NH1):
                                    lhsT = h1g[:, fc * FTT + ts * 128:
                                               fc * FTT + (ts + 1) * 128]
                                    rhs = Wbuf[:, FC * GDFF + fc * D + jd * 512:
                                               FC * GDFF + fc * D + (jd + 1) * 512]
                                    ins = e.matmul(pb[pbk][:], lhsT, rhs,
                                                   start=(fc == 0),
                                                   stop=(g > 0 and fc == NH1 - 1))
                                if g == 0:
                                    ins = e.matmul(
                                        pb[pbk][:], ones1[0:1, :],
                                        brow[0:1, jd * 512:(jd + 1) * 512],
                                        start=False, stop=True)
                                return ins
                            op("tensor", ymm, inc=s_mm,
                               waits=[(s_sc, h1_all), (s_ld, b2_ld),
                                      ((s_ve, pb_y_evac[pbk]) if pb_y_evac[pbk] else None)])
                            tg = tt * (FTT // 128) + ts
                            op("vector", lambda e, e_i=e_i, tg=tg, jd=jd, pbk=pbk:
                               e.scalar_tensor_tensor(
                                   yb[:, tg * D + jd * 512:tg * D + (jd + 1) * 512],
                                   pb[pbk][:],
                                   oh_tm[:, tg * E + e_i:tg * E + e_i + 1],
                                   yb[:, tg * D + jd * 512:tg * D + (jd + 1) * 512],
                                   AL.mult, AL.add),
                               waits=[(s_mm, s_mm.n)], inc=s_ve)
                            pb_y_evac[pbk] = s_ve.n
                last_w_use_mm = s_mm.n
        for tl in range(NT):
            op("vector", lambda e, tl=tl: e.tensor_scalar(
                yb[:, tl * D:(tl + 1) * D], yb[:, tl * D:(tl + 1) * D],
                gatev[:, tl:tl + 1], None, AL.mult), inc=s_ve)
            op("scalar", lambda e, tl=tl: e.activation(
                yb[:, tl * D:(tl + 1) * D], yb[:, tl * D:(tl + 1) * D], AF.Silu),
               waits=[(s_ve, s_ve.n)], inc=s_sc)
        g_done_sc = s_sc.n
        if debug:
            op("scalar", lambda e: e.dma_start(
                out=dbg["dbg_g"].rearrange("(tl p) j -> p tl j", p=128),
                in_=yb[:].rearrange("p (tl j) -> p tl j", j=D)),
               inc=s_sst, k=16, waits=[(s_sc, g_done_sc)])
            op("scalar", lambda e: e.dma_start(
                out=dbg["dbg_oh"].rearrange("(tl p) j -> p tl j", p=128),
                in_=oh_tm[:].rearrange("p (tl j) -> p tl j", j=E)),
               inc=s_sst, k=16)
            op("scalar", lambda e: e.dma_start(
                out=dbg["dbg_gate"].rearrange("(tl p) j -> p tl j", p=128),
                in_=gatev[:].rearrange("p (tl j) -> p tl j", j=1),
                allow_slow_non_contiguous=True),
               inc=s_sst, k=16)

        def wload_fold(dw, fold):
            def lw(e, dw=dw):
                dstv = Wbuf[:].rearrange("p (c j) -> p c j", j=D)
                srcv = dw.rearrange("(c p) j -> p c j", p=128)
                return e.dma_start(out=dstv, in_=srcv)
            op("sync", lw, inc=s_ld, k=16, waits=[(s_mm, s_mm.n)])
            wl = s_ld.n
            if fold:
                for jd in range(2):
                    def bmm(e, jd=jd):
                        ins = None
                        for c in range(FC):
                            ins = e.matmul(pb[2][0:1, :], betat[:, c:c + 1],
                                           Wbuf[:, c * D + jd * 512:
                                                c * D + (jd + 1) * 512],
                                           start=(c == 0), stop=(c == FC - 1))
                        return ins
                    op("tensor", bmm, inc=s_mm,
                       waits=[(s_ld, wl), (s_ve, s_ve.n)])
                    op("vector", lambda e, jd=jd: e.tensor_copy(
                        brow[0:1, jd * 512:(jd + 1) * 512], pb[2][0:1, :]),
                       waits=[(s_mm, s_mm.n)], inc=s_ve)
                for c in range(FC):
                    op("vector", lambda e, c=c: e.tensor_scalar(
                        Wbuf[:, c * D:(c + 1) * D], Wbuf[:, c * D:(c + 1) * D],
                        gammat[:, c:c + 1], None, AL.mult),
                       waits=[(s_mm, s_mm.n)] if c == 0 else [], inc=s_ve)
            return s_ld.n

        def proj_phase(dw, fold, grep, g_tm, scratch_dram, do_norm, tag):
            wl = wload_fold(dw, fold)
            for tl in range(NT):
                if tag != "v":
                    op("sync", lambda e, tl=tl: e.dma_start(
                        out=xt[:], in_=dx[tl * 128:(tl + 1) * 128, :]),
                       inc=s_ld, k=16, waits=[(s_ve, s_ve.n)])
                    ln_tile(xt[:], qt[:], ld_wait=(s_ld, s_ld.n))
                    ln_tile(qt[:], qt[:])
                    def mk_ev_a(half, pbk):
                        op("vector", lambda e, half=half, pbk=pbk: e.tensor_copy(
                            s2a[:, half * 512:(half + 1) * 512], pbk[:]),
                           waits=[(s_mm, s_mm.n)], inc=s_ve)
                    transpose8(qt[:], mk_ev_a)
                    act_fm = s2a
                else:
                    act_fm = None
                for jd in range(2):
                    def pmm(e, tl=tl, jd=jd, act_fm=act_fm, fold=fold):
                        ins = None
                        for c in range(FC):
                            if act_fm is None:
                                lhsT = xnT3[:, c, tl * 128:(tl + 1) * 128]
                            else:
                                lhsT = act_fm[:, c * 128:(c + 1) * 128]
                            ins = e.matmul(pb[jd][:], lhsT,
                                           Wbuf[:, c * D + jd * 512:
                                                c * D + (jd + 1) * 512],
                                           start=(c == 0),
                                           stop=(not fold and c == FC - 1))
                        if fold:
                            ins = e.matmul(pb[jd][:], ones1[0:1, :],
                                           brow[0:1, jd * 512:(jd + 1) * 512],
                                           start=False, stop=True)
                        return ins
                    op("tensor", pmm, inc=s_mm,
                       waits=[(s_ld, wl), (s_ve, s_ve.n)])
                    if tag == "v":
                        for hh in range(jd * 8, (jd + 1) * 8):
                            op("vector", lambda e, tl=tl, jd=jd, hh=hh: e.tensor_scalar(
                                qt[:, hh * DH:(hh + 1) * DH],
                                pb[jd][:, (hh - jd * 8) * DH:(hh - jd * 8 + 1) * DH],
                                gk_tm[:, tl * H + hh:tl * H + hh + 1], 0.5,
                                AL.mult, AL.mult),
                               waits=([(s_mm, s_mm.n), (s_sst, s_sst.n)]
                                      if hh == jd * 8 else []),
                               inc=(s_ve if hh == (jd + 1) * 8 - 1 else None))
                    else:
                        op("vector", lambda e, jd=jd: e.tensor_copy(
                            qt[:, jd * 512:(jd + 1) * 512], pb[jd][:]),
                           waits=[(s_mm, s_mm.n),
                                  ((s_sst, s_sst.n) if s_sst.n else None)],
                           inc=s_ve)
                tile_ready = s_ve.n
                if do_norm:
                    op("scalar", lambda e: e.activation(sqt[:], qt[:], AF.Square),
                       waits=[(s_ve, tile_ready)], inc=s_sc)
                    op("vector", lambda e: e.reduce_sum(
                        nhead[:, 0:H], sqt[:].rearrange("p (h d) -> p h d", d=DH),
                        axis=AX.X), waits=[(s_sc, s_sc.n)], inc=s_ve)
                    op("scalar", lambda e: e.activation(nhead[:, H:2 * H],
                                                        nhead[:, 0:H], AF.Sqrt),
                       waits=[(s_ve, s_ve.n)], inc=s_sc)
                    op("vector", lambda e: e.tensor_scalar_max(
                        nhead[:, 2 * H:3 * H], nhead[:, H:2 * H], NORM_EPS),
                       waits=[(s_sc, s_sc.n)])
                    drain("vector")
                    op("vector", lambda e: e.reciprocal(nhead[:, 3 * H:4 * H],
                                                        nhead[:, 2 * H:3 * H]))
                    drain("vector")
                    for h in range(H):
                        op("vector", lambda e, h=h: e.tensor_scalar(
                            qt[:, h * DH:(h + 1) * DH], qt[:, h * DH:(h + 1) * DH],
                            nhead[:, 3 * H + h:3 * H + h + 1], None, AL.mult))
                    drain("vector")
                    for h in range(H):
                        op("vector", lambda e, h=h, grep=grep: e.scalar_tensor_tensor(
                            sqt[:, 0:DH], qt[:, h * DH:(h + 1) * DH], 1.0, grep,
                            AL.mult, AL.mult, accum_out=nhead[:, h:h + 1]))
                    drain("vector")
                    op("vector", lambda e: e.engine_nop(), inc=s_ve)
                    op("scalar", lambda e: e.activation(nhead[:, H:2 * H],
                                                        nhead[:, 0:H], AF.Sigmoid),
                       waits=[(s_ve, s_ve.n)], inc=s_sc)
                    op("vector", lambda e, g_tm=g_tm, tl=tl: e.tensor_scalar(
                        g_tm[:, tl * H:(tl + 1) * H], nhead[:, H:2 * H], 0.75, 0.25,
                        AL.mult, AL.add),
                       waits=[(s_sc, s_sc.n)], inc=s_ve)
                    def mk_ev_b(half, pbk):
                        op("vector", lambda e, half=half, pbk=pbk: e.tensor_copy(
                            s2b[:, half * 512:(half + 1) * 512], pbk[:]),
                           waits=[(s_mm, s_mm.n),
                                  ((s_sst, s_sst.n) if s_sst.n else None)],
                           inc=s_ve)
                    transpose8(qt[:], mk_ev_b)
                    op("scalar", lambda e, tl=tl, sd=scratch_dram: e.dma_start(
                        out=sd[:, :, tl * 128:(tl + 1) * 128].rearrange(
                            "c p t -> p c t"),
                        in_=s2b[:].rearrange("p (c t) -> p c t", t=128)),
                       waits=[(s_ve, s_ve.n)], inc=s_sst, k=16)
                    if debug:
                        dk = "dbg_q" if tag == "q" else "dbg_k"
                        op("scalar", lambda e, tl=tl, dk=dk: e.dma_start(
                            out=dbg[dk][tl * 128:(tl + 1) * 128, :], in_=qt[:]),
                           inc=s_sst, k=16)
                else:
                    op("scalar", lambda e, tl=tl: e.dma_start(
                        out=dvg[tl * 128:(tl + 1) * 128, :], in_=qt[:]),
                       waits=[(s_ve, tile_ready)], inc=s_sst, k=16)
                    for jd in range(2):
                        op("tensor", lambda e, tl=tl, jd=jd: e.matmul(
                            pb[2 + jd][0:1, :], onesc[:, 0:1],
                            qt[:, jd * 512:(jd + 1) * 512],
                            start=(tl == 0), stop=(tl == NT - 1)),
                           inc=s_mm, waits=[(s_ve, tile_ready)])
                    if tl == NT - 1:
                        for jd in range(2):
                            op("vector", lambda e, jd=jd: e.tensor_copy(
                                csum[0:1, jd * 512:(jd + 1) * 512],
                                pb[2 + jd][0:1, :]),
                               waits=[(s_mm, s_mm.n)], inc=s_ve)

        proj_phase(dwk, True, gkrep[:], gk_tm, dkn, True, "k")
        proj_phase(dwv, False, None, None, None, False, "v")
        proj_phase(dwq, True, gqrep[:], gq_tm, dqn, True, "q")
        if debug:
            op("scalar", lambda e: e.dma_start(
                out=dbg["dbg_gq"].rearrange("(tl p) h -> p tl h", p=128),
                in_=gq_tm[:].rearrange("p (tl h) -> p tl h", h=H)),
               waits=[(s_ve, s_ve.n)], inc=s_sst, k=16)
            op("scalar", lambda e: e.dma_start(
                out=dbg["dbg_gk"].rearrange("(tl p) h -> p tl h", p=128),
                in_=gk_tm[:].rearrange("p (tl h) -> p tl h", h=H)),
               inc=s_sst, k=16)
            op("scalar", lambda e: e.dma_start(out=dbg["dbg_csum"], in_=csum[:]),
               inc=s_sst, k=16)

        ve_at_D = s_ve.n
        sc_at_D = s_sc.n
        tanh_sc = {0: 0, 1: 0}
        fused_evac_ve = ve_at_D
        gqrow_ve = ve_at_D
        gqbc_ve = ve_at_D
        gqtr_done = ve_at_D
        for h in range(H):
            c = h // 2
            off = (h % 2) * 64
            op("sync", lambda e, c=c, off=off: e.dma_start(
                out=qnh_(0, T), in_=dqn[c, off:off + 64, :]),
               inc=s_ld, k=16, waits=[(s_mm, s_mm.n), (s_sst, s_sst.n)])
            op("sync", lambda e, c=c, off=off: e.dma_start(
                out=knh_(0, T), in_=dkn[c, off:off + 64, :]), inc=s_ld, k=16)
            op("sync", lambda e, h=h: e.dma_start(
                out=vgh_(0, NKC * DH).rearrange("p (kc d) -> p kc d", d=DH),
                in_=dvg[:, h * DH:(h + 1) * DH].rearrange("(kc p) d -> p kc d", p=128)),
               inc=s_ld, k=16)
            hd_ld = s_ld.n
            for qc in range(NQC):
                def gqt(e, h=h, qc=qc):
                    ins = None
                    ntl = QCW // 128
                    for i in range(ntl):
                        tl = qc * ntl + i
                        ins = e.transpose(pb[6][0:1, i * 128:(i + 1) * 128],
                                          gq_tm[:, tl * H + h:tl * H + h + 1],
                                          eyet[:])
                    return ins
                op("tensor", gqt, inc=s_mm, waits=[(s_ve, gqrow_ve)])
                op("vector", lambda e: e.tensor_copy(gqrow[0:1, 0:QCW],
                                                     pb[6][0:1, 0:QCW]),
                   waits=[(s_mm, s_mm.n)], inc=s_ve)
                gqrow_ve = s_ve.n
                op("tensor", lambda e: e.matmul(pb[3][:, 0:QCW], ones1[0:1, :],
                                                gqrow[0:1, 0:QCW],
                                                start=True, stop=True),
                   inc=s_mm, waits=[(s_ve, gqrow_ve), (s_ve, gqbc_ve)])
                op("vector", lambda e: e.tensor_copy(gqbc[:, 0:QCW] if False else Wbuf[0:128, GB_OFF:GB_OFF + QCW], pb[3][:, 0:QCW]),
                   waits=[(s_mm, s_mm.n)], inc=s_ve)
                gqbc_ve = s_ve.n
                op("tensor", lambda e, h=h: e.matmul(
                    pb[2][0:64, 0:QCW], csum[0:1, h * DH:(h + 1) * DH],
                    onesr[0:1, 0:QCW], start=True, stop=False),
                   inc=s_mm, waits=[(s_ve, fused_evac_ve), (s_ld, hd_ld)])
                for kc in range(NKC):
                    pbk = kc % 2
                    op("tensor", lambda e, kc=kc, qc=qc, pbk=pbk: e.matmul(
                        pb[pbk][:, 0:QCW], knh_(kc * 128, (kc + 1) * 128),
                        qnh_(qc * QCW, (qc + 1) * QCW), start=True, stop=True),
                       inc=s_mm,
                       waits=[(s_ld, hd_ld),
                              ((s_sc, tanh_sc[pbk]) if tanh_sc[pbk] else
                               (s_ve, ve_at_D))])
                    smm = s_mm.n
                    op("scalar", lambda e, pbk=pbk: e.activation(
                        tsb_(pbk * 512, pbk * 512 + QCW), pb[pbk][:, 0:QCW],
                        AF.Tanh, bias=CM405, scale=C15),
                       waits=[(s_mm, smm)], inc=s_sc)
                    tanh_sc[pbk] = s_sc.n
                    op("tensor", lambda e, kc=kc, pbk=pbk: e.matmul(
                        pb[2][0:64, 0:QCW], vgh_(kc * DH, (kc + 1) * DH),
                        tsb_(pbk * 512, pbk * 512 + QCW),
                        start=False, stop=(kc == NKC - 1)),
                       inc=s_mm, waits=[(s_sc, s_sc.n)])
                def fev(e, h=h, qc=qc):
                    dstv = fusedT[:].rearrange("p (c t) -> p c t", t=T)[
                        (h % 2) * 64:(h % 2) * 64 + 64, h // 2,
                        qc * QCW:(qc + 1) * QCW]
                    return e.tensor_tensor(dstv, pb[2][0:64, 0:QCW],
                                           Wbuf[0:64, GB_OFF:GB_OFF + QCW], AL.mult)
                op("vector", fev, waits=[(s_mm, s_mm.n), (s_ve, gqbc_ve)], inc=s_ve)
                fused_evac_ve = s_ve.n
        if debug:
            op("scalar", lambda e: e.dma_start(out=dbg["dbg_fused"], in_=fusedT[:]),
               waits=[(s_ve, s_ve.n)], inc=s_sst, k=16)

        def lwo(e):
            dstv = Wbuf[:].rearrange("p (c j) -> p c j", j=D)
            return e.dma_start(out=dstv, in_=dwo.rearrange("(c p) j -> p c j", p=128))
        op("sync", lwo, inc=s_ld, k=16, waits=[(s_mm, s_mm.n)])
        op("sync", lambda e: e.dma_start(out=brow[0:1, :], in_=dbo), inc=s_ld, k=16)
        wo_ld = s_ld.n
        sc_at_E = s_sc.n
        pe_evac = {0: 0, 1: 0}
        for tl in range(NT):
            op("sync", lambda e, tl=tl: e.dma_start(
                out=xt[:], in_=dx[tl * 128:(tl + 1) * 128, :]),
               inc=s_ld, k=16, waits=[(s_ve, s_ve.n)])
            x_ld = s_ld.n
            for jd in range(2):
                def omm(e, tl=tl, jd=jd):
                    ins = None
                    for c in range(FC):
                        lhsT = fusedT[:].rearrange("p (c t) -> p c t", t=T)[
                            :, c, tl * 128:(tl + 1) * 128]
                        ins = e.matmul(pb[jd][:], lhsT,
                                       Wbuf[:, c * D + jd * 512:
                                            c * D + (jd + 1) * 512],
                                       start=(c == 0), stop=False)
                    ins = e.matmul(pb[jd][:], ones1[0:1, :],
                                   brow[0:1, jd * 512:(jd + 1) * 512],
                                   start=False, stop=True)
                    return ins
                op("tensor", omm, inc=s_mm,
                   waits=[(s_ld, wo_ld), (s_ve, s_ve.n),
                          ((s_ve, pe_evac[jd]) if pe_evac[jd] else None),
                          ((s_sc, sc_at_E) if tl == 0 else None)])
                op("vector", lambda e, tl=tl, jd=jd: e.scalar_tensor_tensor(
                    s2a[:, jd * 512:(jd + 1) * 512], pb[jd][:], 1.0,
                    yb[:, tl * D + jd * 512:tl * D + (jd + 1) * 512],
                    AL.bypass, AL.mult),
                   waits=[(s_mm, s_mm.n), (s_sc, g_done_sc),
                          ((s_out, s_out.n) if s_out.n else None)])
                drain("vector")
                op("vector", lambda e, jd=jd: e.tensor_tensor(
                    s2a[:, jd * 512:(jd + 1) * 512],
                    s2a[:, jd * 512:(jd + 1) * 512],
                    xt[:, jd * 512:(jd + 1) * 512], AL.add),
                   waits=[(s_ld, x_ld)], inc=s_ve)
                pe_evac[jd] = s_ve.n
            op("scalar", lambda e, tl=tl: e.dma_start(
                out=dout[tl * 128:(tl + 1) * 128, :], in_=s2a[:]),
               waits=[(s_ve, s_ve.n)], inc=s_out, k=16)
        plan["scalar"].append(lambda e, v=s_out.n: e.wait_ge(s_out.sem, v))

        with nc.Block() as block:
            block.sync(lambda e: [f(e) for f in plan["sync"]])
            block.vector(lambda e: [f(e) for f in plan["vector"]])
            block.scalar(lambda e: [f(e) for f in plan["scalar"]])
            block.tensor(lambda e: [f(e) for f in plan["tensor"]])
            block.gpsimd(lambda e: [f(e) for f in plan["gpsimd"]])
    return nc


def make_inputs(xc, wq, wk, wv, g_q, g_k, wo, b_o, gamma, beta, Wr, br,
                W1, b1, W2, b2):
    return {
        "x": np.ascontiguousarray(xc),
        "wq": np.ascontiguousarray(wq),
        "wk": np.ascontiguousarray(wk),
        "wv": np.ascontiguousarray(wv),
        "wo": np.ascontiguousarray(wo),
        "bo": np.ascontiguousarray(b_o.reshape(1, D)),
        "gamma_fm": np.ascontiguousarray(gamma.reshape(FC, 128).T),
        "beta_fm": np.ascontiguousarray(beta.reshape(FC, 128).T),
        "gq_rep": np.ascontiguousarray(np.broadcast_to(g_q, (128, DH))),
        "gk_rep": np.ascontiguousarray(np.broadcast_to(g_k, (128, DH))),
        "wr_sb": np.ascontiguousarray(
            Wr.reshape(FC, 128, E).transpose(1, 0, 2).reshape(128, FC * E)),
        "br": np.ascontiguousarray(br.reshape(1, E)),
        "W1": np.ascontiguousarray(W1),
        "b1_fm": np.ascontiguousarray(
            b1.reshape(E, DFF // 128, 128).transpose(2, 0, 1).reshape(128, -1)),
        "W2": np.ascontiguousarray(W2),
        "b2": np.ascontiguousarray(b2.reshape(1, E * D)),
        "eye": np.eye(128, dtype=np.float32),
    }


_NC_CACHE = {}


def kernel(L0_attn_x, L0_attn_w_q, L0_attn_w_k, L0_attn_w_v, L0_attn_g_q,
           L0_attn_g_k, L0_attn_w_o, L0_attn_b_o, L0_attn_gamma, L0_attn_beta,
           L0_ffn_Wr, L0_ffn_br, L0_ffn_W1, L0_ffn_b1, L0_ffn_W2, L0_ffn_b2):
    x = np.asarray(L0_attn_x, dtype=np.float32)
    B, T, _ = x.shape
    if (T,) not in _NC_CACHE:
        _NC_CACHE[(T,)] = build(T=T)
    nc = _NC_CACHE[(T,)]
    args = [np.asarray(a, dtype=np.float32) for a in
            (L0_attn_w_q, L0_attn_w_k, L0_attn_w_v, L0_attn_g_q, L0_attn_g_k,
             L0_attn_w_o, L0_attn_b_o, L0_attn_gamma, L0_attn_beta,
             L0_ffn_Wr, L0_ffn_br, L0_ffn_W1, L0_ffn_b1, L0_ffn_W2, L0_ffn_b2)]
    in_maps = [make_inputs(x[c], *args) for c in range(B)]
    res = run_bass_kernel_spmd(nc, in_maps, core_ids=list(range(B)))
    return np.stack([np.asarray(r["out"]) for r in res.results], axis=0)


# version 02
# speedup vs baseline: 1.0729x; 1.0729x over previous
import numpy as np
from contextlib import ExitStack

import concourse.bass as bass
import concourse.mybir as mybir
from concourse.bass_utils import run_bass_kernel_spmd

F32 = mybir.dt.float32
AF = mybir.ActivationFunctionType
AL = mybir.AluOpType
AX = mybir.AxisListType

D = 1024
H = 16
DH = 64
E = 8
DFF = 4096
FC = D // 128
GDFF = 512
NG = DFF // GDFF
LN_EPS = 1e-5
NORM_EPS = 1e-12


class Ctr:
    def __init__(self, sem):
        self.sem = sem
        self.n = 0


def build(T=2048, debug=False):
    NT = T // 128
    FTT = min(512, T)
    NFT = T // FTT
    NQC = max(1, T // 512)
    QCW = min(512, T)
    NKC = T // 128
    NH1 = GDFF // 128

    nc = bass.Bass("TRN2", target_bir_lowering=False)

    dx = nc.dram_tensor("x", [T, D], F32, kind="ExternalInput").ap()
    dwq = nc.dram_tensor("wq", [D, D], F32, kind="ExternalInput").ap()
    dwk = nc.dram_tensor("wk", [D, D], F32, kind="ExternalInput").ap()
    dwv = nc.dram_tensor("wv", [D, D], F32, kind="ExternalInput").ap()
    dwo = nc.dram_tensor("wo", [D, D], F32, kind="ExternalInput").ap()
    dbo = nc.dram_tensor("bo", [1, D], F32, kind="ExternalInput").ap()
    dgamma = nc.dram_tensor("gamma_fm", [128, FC], F32, kind="ExternalInput").ap()
    dbeta = nc.dram_tensor("beta_fm", [128, FC], F32, kind="ExternalInput").ap()
    dgq = nc.dram_tensor("gq_rep", [128, DH], F32, kind="ExternalInput").ap()
    dgk = nc.dram_tensor("gk_rep", [128, DH], F32, kind="ExternalInput").ap()
    dwr = nc.dram_tensor("wr_sb", [128, FC * E], F32, kind="ExternalInput").ap()
    dbr = nc.dram_tensor("br", [1, E], F32, kind="ExternalInput").ap()
    dW1 = nc.dram_tensor("W1", [E, D, DFF], F32, kind="ExternalInput").ap()
    db1 = nc.dram_tensor("b1_fm", [128, E * (DFF // 128)], F32, kind="ExternalInput").ap()
    dW2 = nc.dram_tensor("W2", [E, DFF, D], F32, kind="ExternalInput").ap()
    db2 = nc.dram_tensor("b2", [1, E * D], F32, kind="ExternalInput").ap()
    deye = nc.dram_tensor("eye", [128, 128], F32, kind="ExternalInput").ap()
    dout = nc.dram_tensor("out", [T, D], F32, kind="ExternalOutput").ap()
    BF16 = mybir.dt.bfloat16
    dqn_h = nc.dram_tensor("qn_sh", [FC, 128, T], BF16).ap()
    dqn_l = nc.dram_tensor("qn_sl", [FC, 128, T], BF16).ap()
    dkn_h = nc.dram_tensor("kn_sh", [FC, 128, T], BF16).ap()
    dkn_l = nc.dram_tensor("kn_sl", [FC, 128, T], BF16).ap()
    dvg_h = nc.dram_tensor("vg_sh", [T, D], BF16).ap()
    dvg_l = nc.dram_tensor("vg_sl", [T, D], BF16).ap()
    dbg = {}
    if debug:
        for name, shp in [("dbg_g", [T, D]), ("dbg_oh", [T, E]), ("dbg_gate", [T, 1]),
                          ("dbg_q", [T, D]), ("dbg_k", [T, D]),
                          ("dbg_gq", [T, H]), ("dbg_gk", [T, H]),
                          ("dbg_fused", [128, FC * T]), ("dbg_csum", [1, D]),
                          ("dbg_xn", [T, D]), ("dbg_gqbc", [128, 512]),
                          ("dbg_stage", [128, 1024]), ("dbg_rd", [128, 2 * T]),
                          ("dbg_praw", [128, 1024]),
                          ("dbg_t0", [128, 1024]), ("dbg_t1", [128, 1024]),
                          ("dbg_qnh", [128, 2048]), ("dbg_vgh", [128, 2048])]:
            dbg[name] = nc.dram_tensor(name, shp, F32, kind="ExternalOutput").ap()

    plan = {k: [] for k in ("sync", "vector", "scalar", "tensor", "gpsimd")}

    def op(eng, fn, inc=None, k=1, waits=()):
        waits = [w for w in waits if w is not None]
        for (c, v) in waits:
            assert v <= c.n, f"wait on future value: {v} > {c.n}"
        if inc is not None:
            inc.n += k
        def run(e, fn=fn, inc=inc, k=k, waits=tuple(waits)):
            for (c, v) in waits:
                if v > 0:
                    e.wait_ge(c.sem, v)
            ins = fn(e)
            if inc is not None:
                ins.then_inc(inc.sem, k)
        plan[eng].append(run)

    def drain(eng):
        plan[eng].append(lambda e: e.drain())

    ctx = ExitStack()
    with ctx:
        sb = lambda n, s: ctx.enter_context(nc.sbuf_tensor(n, s, F32))
        xnT = sb("xnT", [128, FC * T])
        yb = sb("yb", [128, NT * D])
        Wbuf = sb("Wbuf", [128, 8 * D])
        xt = sb("xt", [128, D])
        dt_ = sb("dt_", [128, D])
        sqt = sb("sqt", [128, D])
        qt = sb("qt", [128, D])
        s2a = sb("s2a", [128, D])
        s2b = sb("s2b", [128, D])
        h1g = sb("h1g", [128, NH1 * FTT])
        stats = sb("stats", [128, 32])
        nhead = sb("nhead", [128, 4 * H])
        gq_tm = sb("gq_tm", [128, NT * H])
        gk_tm = sb("gk_tm", [128, NT * H])
        oh_tm = sb("oh_tm", [128, NT * E])
        gatev = sb("gatev", [128, NT])
        gammat = sb("gammat", [128, FC])
        betat = sb("betat", [128, FC])
        gqrep = sb("gqrep", [128, DH])
        gkrep = sb("gkrep", [128, DH])
        wr_sb = sb("wr_sbuf", [128, FC * E])
        b1_sb = sb("b1_sbuf", [128, E * (DFF // 128)])
        eyet = sb("eyet", [128, 128])
        cst = sb("cst", [128, 8])
        onesr = sb("onesr", [1, 512])
        ones1 = sb("ones128", [1, 128])
        onesc = sb("onescol", [128, 1])
        brow = sb("brow", [1, D])
        br_sb = sb("br_sbuf", [1, E])
        csum = sb("csum_row", [1, D])
        Wb16 = Wbuf[:].bitcast(BF16)
        qnh_hi = Wb16[:, 0:T]
        qnh_lo = Wb16[:, T:2 * T]
        knh_hi = Wb16[:, 2 * T:3 * T]
        knh_lo = Wb16[:, 3 * T:4 * T]
        VGO = 4 * T
        vgh_hi = Wb16[:, VGO:VGO + 2 * NKC * DH]
        vgh_lo = Wb16[:, VGO + 2 * NKC * DH:VGO + 4 * NKC * DH]
        pbA = ctx.enter_context(nc.psum_tensor("pbA", [128, 1024], F32))
        pbB = ctx.enter_context(nc.psum_tensor("pbB", [128, 1024], F32))
        pbC = [ctx.enter_context(nc.psum_tensor(f"pbC{i}", [128, 512], F32))
               for i in range(4)]
        pb = {0: pbA[:, 0:512], 1: pbA[:, 512:1024],
              4: pbB[:, 0:512], 5: pbB[:, 512:1024],
              2: pbC[0][:], 3: pbC[1][:], 6: pbC[2][:], 7: pbC[3][:]}

        s_ld = Ctr(ctx.enter_context(nc.semaphore("s_ld")))
        s_mm = Ctr(ctx.enter_context(nc.semaphore("s_mm")))
        s_ve = Ctr(ctx.enter_context(nc.semaphore("s_ve")))
        s_sc = Ctr(ctx.enter_context(nc.semaphore("s_sc")))
        s_sst = Ctr(ctx.enter_context(nc.semaphore("s_sst")))
        s_out = Ctr(ctx.enter_context(nc.semaphore("s_out")))

        CI1024 = cst[:, 0:1]
        CEPS = cst[:, 1:2]
        C15 = cst[:, 2:3]
        CM405 = cst[:, 3:4]
        fusedT = xnT

        xnT3 = xnT[:].rearrange("p (c t) -> p c t", t=T)

        def ldi(dst, src):
            op("sync", lambda e, d=dst, s=src: e.dma_start(out=d, in_=s),
               inc=s_ld, k=16)

        ldi(gammat[:], dgamma)
        ldi(betat[:], dbeta)
        ldi(gqrep[:], dgq)
        ldi(gkrep[:], dgk)
        ldi(wr_sb[:], dwr)
        ldi(b1_sb[:], db1)
        ldi(eyet[:], deye)
        ldi(br_sb[:], dbr)
        for aptgt, val in [(cst[:, 0:1], 1.0 / D), (cst[:, 1:2], LN_EPS),
                           (cst[:, 2:3], 15.0), (cst[:, 3:4], -4.05),
                           (onesr[:], 1.0), (ones1[:], 1.0), (onesc[:], 1.0)]:
            op("vector", lambda e, a=aptgt, v=val: e.memset(a, v), inc=s_ve)
        op("vector", lambda e: e.memset(yb[:], 0.0), inc=s_ve)
        setup_ve = s_ve.n

        def ln_tile(src, out, ld_wait=None):
            drain("vector")
            op("vector", lambda e: e.reduce_sum(stats[:, 0:1], src, axis=AX.X),
               waits=[ld_wait] if ld_wait else [])
            drain("vector")
            op("vector", lambda e: e.tensor_scalar_mul(stats[:, 1:2], stats[:, 0:1], 1.0 / D))
            drain("vector")
            op("vector", lambda e: e.tensor_scalar(dt_[:], src, stats[:, 1:2],
                                                   None, AL.subtract), inc=s_ve)
            op("scalar", lambda e: e.activation(sqt[:], dt_[:], AF.Square,
                                                accum_out=stats[:, 2:3]),
               waits=[(s_ve, s_ve.n)], inc=s_sc)
            drain("scalar")
            op("scalar", lambda e: e.activation(stats[:, 3:4], stats[:, 2:3],
                                                AF.Sqrt, scale=CI1024, bias=CEPS),
               inc=s_sc)
            op("vector", lambda e: e.reciprocal(stats[:, 4:5], stats[:, 3:4]),
               waits=[(s_sc, s_sc.n)])
            drain("vector")
            op("vector", lambda e, out=out: e.tensor_scalar(out, dt_[:], stats[:, 4:5],
                                                            None, AL.mult), inc=s_ve)

        tr_evac = {6: 0, 7: 0}

        def transpose8(src, evac_fn_for_half):
            src_ready = s_ve.n
            for half in (0, 1):
                pbi = 6 + half
                def tmf(e, half=half, pbi=pbi, src=src):
                    ins = None
                    for c in range(4):
                        cc = half * 4 + c
                        ins = e.transpose(pb[pbi][:, c * 128:(c + 1) * 128],
                                          src[:, cc * 128:(cc + 1) * 128], eyet[:])
                    return ins
                op("tensor", tmf, inc=s_mm,
                   waits=[(s_ve, src_ready),
                          ((s_ve, tr_evac[pbi]) if tr_evac[pbi] else None)])
                evac_fn_for_half(half, pb[pbi])
                tr_evac[pbi] = s_ve.n

        for tl in range(NT):
            op("sync", lambda e, tl=tl: e.dma_start(out=xt[:],
                                                    in_=dx[tl * 128:(tl + 1) * 128, :]),
               inc=s_ld, k=16, waits=[(s_ve, s_ve.n if tl else setup_ve)])
            ln_tile(xt[:], qt[:], ld_wait=(s_ld, s_ld.n))
            def mk_evac(half, pbk, tl=tl):
                def evf(e, half=half, pbk=pbk, tl=tl):
                    dstv = xnT3[:, half * 4:(half + 1) * 4, tl * 128:(tl + 1) * 128]
                    return e.tensor_copy(dstv, pbk[:].rearrange("p (c t) -> p c t", t=128))
                op("vector", evf, waits=[(s_mm, s_mm.n)], inc=s_ve)
            transpose8(qt[:], mk_evac)
            if debug:
                op("scalar", lambda e, tl=tl: e.dma_start(
                    out=dbg["dbg_xn"][tl * 128:(tl + 1) * 128, :], in_=qt[:]),
                   waits=[(s_ve, s_ve.n)], inc=s_sst, k=16)
        xnT_ready = s_ve.n

        for tl in range(NT):
            def rmm(e, tl=tl):
                ins = None
                for c in range(FC):
                    ins = e.matmul(pb[5][:, 0:E], xnT3[:, c, tl * 128:(tl + 1) * 128],
                                   wr_sb[:, c * E:(c + 1) * E],
                                   start=(c == 0), stop=False)
                ins = e.matmul(pb[5][:, 0:E], ones1[0:1, :], br_sb[0:1, :],
                               start=False, stop=True)
                return ins
            op("tensor", rmm, inc=s_mm, waits=[(s_ve, s_ve.n), (s_ld, s_ld.n)])
            op("vector", lambda e: e.tensor_copy(stats[:, 8:8 + E], pb[5][:, 0:E]),
               waits=[(s_mm, s_mm.n), (s_sc, s_sc.n)], inc=s_ve)
            drain("vector")
            op("vector", lambda e: e.reduce_max(stats[:, 16:17], stats[:, 8:8 + E],
                                                axis=AX.X, negate=True))
            drain("vector")
            op("vector", lambda e: e.tensor_scalar_mul(stats[:, 17:18],
                                                       stats[:, 16:17], -1.0))
            drain("vector")
            op("vector", lambda e, tl=tl: e.tensor_scalar(oh_tm[:, tl * E:(tl + 1) * E],
                                                          stats[:, 8:8 + E],
                                                          stats[:, 17:18], None,
                                                          AL.is_equal), inc=s_ve)
            op("scalar", lambda e: e.activation(stats[:, 18:18 + E], stats[:, 8:8 + E],
                                                AF.Exp, bias=stats[:, 16:17],
                                                accum_out=stats[:, 26:27]),
               waits=[(s_ve, s_ve.n)], inc=s_sc)
            op("vector", lambda e, tl=tl: e.reciprocal(gatev[:, tl:tl + 1],
                                                       stats[:, 26:27]),
               waits=[(s_sc, s_sc.n)], inc=s_ve)
        router_ve = s_ve.n

        last_w_use_mm = 0
        pb_h_evac = {0: 0, 1: 0}
        pb_y_evac = {4: 0, 5: router_ve}
        for e_i in range(E):
            op("sync", lambda e, e_i=e_i: e.dma_start(
                out=brow[0:1, :], in_=db2[0:1, e_i * D:(e_i + 1) * D]),
               inc=s_ld, k=16,
               waits=[(s_mm, last_w_use_mm) if last_w_use_mm else None])
            b2_ld = s_ld.n
            for g in range(NG):
                def l1(en, e_i=e_i, g=g):
                    dstv = Wbuf[:, 0:FC * GDFF].rearrange("p (c f) -> p c f", f=GDFF)
                    srcv = dW1[e_i, :, g * GDFF:(g + 1) * GDFF].rearrange(
                        "(c p) f -> p c f", p=128)
                    return en.dma_start(out=dstv, in_=srcv)
                def l2(en, e_i=e_i, g=g):
                    dstv = Wbuf[:, FC * GDFF:FC * GDFF + NH1 * D].rearrange(
                        "p (c j) -> p c j", j=D)
                    srcv = dW2[e_i, g * GDFF:(g + 1) * GDFF, :].rearrange(
                        "(c p) j -> p c j", p=128)
                    return en.dma_start(out=dstv, in_=srcv)
                op("sync", l1, inc=s_ld, k=16,
                   waits=[(s_mm, last_w_use_mm) if last_w_use_mm else None])
                op("sync", l2, inc=s_ld, k=16)
                w_ld = s_ld.n
                for tt in range(NFT):
                    h1_all = 0
                    for fc in range(NH1):
                        pbk = fc % 2
                        def h1mm(e, g=g, tt=tt, fc=fc, pbk=pbk):
                            ins = None
                            for c in range(FC):
                                lhsT = Wbuf[:, c * GDFF + fc * 128:
                                            c * GDFF + (fc + 1) * 128]
                                rhs = xnT3[:, c, tt * FTT:(tt + 1) * FTT]
                                ins = e.matmul(pb[pbk][:, 0:FTT], lhsT, rhs,
                                               start=(c == 0), stop=(c == FC - 1))
                            return ins
                        op("tensor", h1mm, inc=s_mm,
                           waits=[(s_ld, w_ld),
                                  ((s_sc, pb_h_evac[pbk]) if pb_h_evac[pbk] else None)])
                        bidx = e_i * (DFF // 128) + g * NH1 + fc
                        op("scalar", lambda e, fc=fc, bidx=bidx, pbk=pbk:
                           e.activation(h1g[:, fc * FTT:(fc + 1) * FTT],
                                        pb[pbk][:, 0:FTT], AF.Relu,
                                        bias=b1_sb[:, bidx:bidx + 1]),
                           waits=[(s_mm, s_mm.n), (s_ve, setup_ve)], inc=s_sc)
                        pb_h_evac[pbk] = s_sc.n
                        h1_all = s_sc.n
                    for ts in range(FTT // 128):
                        for jd in range(2):
                            pbk = 4 + (ts * 2 + jd) % 2
                            def ymm(e, e_i=e_i, g=g, ts=ts, jd=jd, pbk=pbk):
                                ins = None
                                for fc in range(NH1):
                                    lhsT = h1g[:, fc * FTT + ts * 128:
                                               fc * FTT + (ts + 1) * 128]
                                    rhs = Wbuf[:, FC * GDFF + fc * D + jd * 512:
                                               FC * GDFF + fc * D + (jd + 1) * 512]
                                    ins = e.matmul(pb[pbk][:], lhsT, rhs,
                                                   start=(fc == 0),
                                                   stop=(g > 0 and fc == NH1 - 1))
                                if g == 0:
                                    ins = e.matmul(
                                        pb[pbk][:], ones1[0:1, :],
                                        brow[0:1, jd * 512:(jd + 1) * 512],
                                        start=False, stop=True)
                                return ins
                            op("tensor", ymm, inc=s_mm,
                               waits=[(s_sc, h1_all), (s_ld, b2_ld),
                                      ((s_ve, pb_y_evac[pbk]) if pb_y_evac[pbk] else None)])
                            tg = tt * (FTT // 128) + ts
                            op("vector", lambda e, e_i=e_i, tg=tg, jd=jd, pbk=pbk:
                               e.scalar_tensor_tensor(
                                   yb[:, tg * D + jd * 512:tg * D + (jd + 1) * 512],
                                   pb[pbk][:],
                                   oh_tm[:, tg * E + e_i:tg * E + e_i + 1],
                                   yb[:, tg * D + jd * 512:tg * D + (jd + 1) * 512],
                                   AL.mult, AL.add),
                               waits=[(s_mm, s_mm.n)], inc=s_ve)
                            pb_y_evac[pbk] = s_ve.n
                last_w_use_mm = s_mm.n
        for tl in range(NT):
            op("vector", lambda e, tl=tl: e.tensor_scalar(
                yb[:, tl * D:(tl + 1) * D], yb[:, tl * D:(tl + 1) * D],
                gatev[:, tl:tl + 1], None, AL.mult), inc=s_ve)
            op("scalar", lambda e, tl=tl: e.activation(
                yb[:, tl * D:(tl + 1) * D], yb[:, tl * D:(tl + 1) * D], AF.Silu),
               waits=[(s_ve, s_ve.n)], inc=s_sc)
        g_done_sc = s_sc.n
        if debug:
            op("scalar", lambda e: e.dma_start(
                out=dbg["dbg_g"].rearrange("(tl p) j -> p tl j", p=128),
                in_=yb[:].rearrange("p (tl j) -> p tl j", j=D)),
               inc=s_sst, k=16, waits=[(s_sc, g_done_sc)])
            op("scalar", lambda e: e.dma_start(
                out=dbg["dbg_oh"].rearrange("(tl p) j -> p tl j", p=128),
                in_=oh_tm[:].rearrange("p (tl j) -> p tl j", j=E)),
               inc=s_sst, k=16)
            op("scalar", lambda e: e.dma_start(
                out=dbg["dbg_gate"].rearrange("(tl p) j -> p tl j", p=128),
                in_=gatev[:].rearrange("p (tl j) -> p tl j", j=1),
                allow_slow_non_contiguous=True),
               inc=s_sst, k=16)

        def wload_fold(dw, fold):
            def lw(e, dw=dw):
                dstv = Wbuf[:].rearrange("p (c j) -> p c j", j=D)
                srcv = dw.rearrange("(c p) j -> p c j", p=128)
                return e.dma_start(out=dstv, in_=srcv)
            op("sync", lw, inc=s_ld, k=16, waits=[(s_mm, s_mm.n)])
            wl = s_ld.n
            if fold:
                for jd in range(2):
                    def bmm(e, jd=jd):
                        ins = None
                        for c in range(FC):
                            ins = e.matmul(pb[2][0:1, :], betat[:, c:c + 1],
                                           Wbuf[:, c * D + jd * 512:
                                                c * D + (jd + 1) * 512],
                                           start=(c == 0), stop=(c == FC - 1))
                        return ins
                    op("tensor", bmm, inc=s_mm,
                       waits=[(s_ld, wl), (s_ve, s_ve.n)])
                    op("vector", lambda e, jd=jd: e.tensor_copy(
                        brow[0:1, jd * 512:(jd + 1) * 512], pb[2][0:1, :]),
                       waits=[(s_mm, s_mm.n)], inc=s_ve)
                for c in range(FC):
                    op("vector", lambda e, c=c: e.tensor_scalar(
                        Wbuf[:, c * D:(c + 1) * D], Wbuf[:, c * D:(c + 1) * D],
                        gammat[:, c:c + 1], None, AL.mult),
                       waits=[(s_mm, s_mm.n)] if c == 0 else [], inc=s_ve)
            return s_ld.n

        def proj_phase(dw, fold, grep, g_tm, scratch_dram, do_norm, tag):
            wl = wload_fold(dw, fold)
            for tl in range(NT):
                if tag != "v":
                    op("sync", lambda e, tl=tl: e.dma_start(
                        out=xt[:], in_=dx[tl * 128:(tl + 1) * 128, :]),
                       inc=s_ld, k=16,
                       waits=[(s_ve, s_ve.n),
                              ((s_sst, s_sst.n) if s_sst.n else None)])
                    ln_tile(xt[:], qt[:], ld_wait=(s_ld, s_ld.n))
                    ln_tile(qt[:], qt[:])
                    def mk_ev_a(half, pbk):
                        op("vector", lambda e, half=half, pbk=pbk: e.tensor_copy(
                            s2a[:, half * 512:(half + 1) * 512], pbk[:]),
                           waits=[(s_mm, s_mm.n)], inc=s_ve)
                    transpose8(qt[:], mk_ev_a)
                    act_fm = s2a
                else:
                    act_fm = None
                for jd in range(2):
                    def pmm(e, tl=tl, jd=jd, act_fm=act_fm, fold=fold):
                        ins = None
                        for c in range(FC):
                            if act_fm is None:
                                lhsT = xnT3[:, c, tl * 128:(tl + 1) * 128]
                            else:
                                lhsT = act_fm[:, c * 128:(c + 1) * 128]
                            ins = e.matmul(pb[jd][:], lhsT,
                                           Wbuf[:, c * D + jd * 512:
                                                c * D + (jd + 1) * 512],
                                           start=(c == 0),
                                           stop=(not fold and c == FC - 1))
                        if fold:
                            ins = e.matmul(pb[jd][:], ones1[0:1, :],
                                           brow[0:1, jd * 512:(jd + 1) * 512],
                                           start=False, stop=True)
                        return ins
                    op("tensor", pmm, inc=s_mm,
                       waits=[(s_ld, wl), (s_ve, s_ve.n)])
                    if tag == "v":
                        for hh in range(jd * 8, (jd + 1) * 8):
                            op("vector", lambda e, tl=tl, jd=jd, hh=hh: e.tensor_scalar(
                                qt[:, hh * DH:(hh + 1) * DH],
                                pb[jd][:, (hh - jd * 8) * DH:(hh - jd * 8 + 1) * DH],
                                gk_tm[:, tl * H + hh:tl * H + hh + 1], 0.5,
                                AL.mult, AL.mult),
                               waits=([(s_mm, s_mm.n), (s_sst, s_sst.n)]
                                      if hh == jd * 8 else []),
                               inc=(s_ve if hh == (jd + 1) * 8 - 1 else None))
                    else:
                        op("vector", lambda e, jd=jd: e.tensor_copy(
                            qt[:, jd * 512:(jd + 1) * 512], pb[jd][:]),
                           waits=[(s_mm, s_mm.n),
                                  ((s_sst, s_sst.n) if s_sst.n else None)],
                           inc=s_ve)
                tile_ready = s_ve.n
                if do_norm:
                    op("scalar", lambda e: e.activation(sqt[:], qt[:], AF.Square),
                       waits=[(s_ve, tile_ready)], inc=s_sc)
                    op("vector", lambda e: e.reduce_sum(
                        nhead[:, 0:H], sqt[:].rearrange("p (h d) -> p h d", d=DH),
                        axis=AX.X), waits=[(s_sc, s_sc.n)], inc=s_ve)
                    op("scalar", lambda e: e.activation(nhead[:, H:2 * H],
                                                        nhead[:, 0:H], AF.Sqrt),
                       waits=[(s_ve, s_ve.n)], inc=s_sc)
                    op("vector", lambda e: e.tensor_scalar_max(
                        nhead[:, 2 * H:3 * H], nhead[:, H:2 * H], NORM_EPS),
                       waits=[(s_sc, s_sc.n)])
                    drain("vector")
                    op("vector", lambda e: e.reciprocal(nhead[:, 3 * H:4 * H],
                                                        nhead[:, 2 * H:3 * H]))
                    drain("vector")
                    for h in range(H):
                        op("vector", lambda e, h=h: e.tensor_scalar(
                            qt[:, h * DH:(h + 1) * DH], qt[:, h * DH:(h + 1) * DH],
                            nhead[:, 3 * H + h:3 * H + h + 1], None, AL.mult))
                    drain("vector")
                    for h in range(H):
                        op("vector", lambda e, h=h, grep=grep: e.scalar_tensor_tensor(
                            sqt[:, 0:DH], qt[:, h * DH:(h + 1) * DH], 1.0, grep,
                            AL.mult, AL.mult, accum_out=nhead[:, h:h + 1]))
                    drain("vector")
                    op("vector", lambda e: e.engine_nop(), inc=s_ve)
                    op("scalar", lambda e: e.activation(nhead[:, H:2 * H],
                                                        nhead[:, 0:H], AF.Sigmoid),
                       waits=[(s_ve, s_ve.n)], inc=s_sc)
                    op("vector", lambda e, g_tm=g_tm, tl=tl: e.tensor_scalar(
                        g_tm[:, tl * H:(tl + 1) * H], nhead[:, H:2 * H], 0.75, 0.25,
                        AL.mult, AL.add),
                       waits=[(s_sc, s_sc.n)], inc=s_ve)
                    def mk_ev_b(half, pbk):
                        op("vector", lambda e, half=half, pbk=pbk: e.tensor_copy(
                            s2b[:, half * 512:(half + 1) * 512], pbk[:]),
                           waits=[(s_mm, s_mm.n),
                                  ((s_sst, s_sst.n) if s_sst.n else None)],
                           inc=s_ve)
                    transpose8(qt[:], mk_ev_b)
                    xt16 = xt[:].bitcast(BF16)
                    drain("vector")
                    op("vector", lambda e, xt16=xt16: e.tensor_copy(
                        xt16[:, 0:1024], s2b[:]), waits=[(s_ve, s_ve.n)])
                    drain("vector")
                    op("vector", lambda e, xt16=xt16: e.tensor_tensor(
                        xt16[:, 1024:2048], s2b[:], xt16[:, 0:1024], AL.subtract),
                       inc=s_ve)
                    sdh, sdl = scratch_dram
                    op("scalar", lambda e, tl=tl, sd=sdh, xt16=xt16: e.dma_start(
                        out=sd[:, :, tl * 128:(tl + 1) * 128].rearrange(
                            "c p t -> p c t"),
                        in_=xt16[:, 0:1024].rearrange("p (c t) -> p c t", t=128)),
                       waits=[(s_ve, s_ve.n)], inc=s_sst, k=16)
                    op("scalar", lambda e, tl=tl, sd=sdl, xt16=xt16: e.dma_start(
                        out=sd[:, :, tl * 128:(tl + 1) * 128].rearrange(
                            "c p t -> p c t"),
                        in_=xt16[:, 1024:2048].rearrange("p (c t) -> p c t", t=128)),
                       inc=s_sst, k=16)
                    if debug:
                        dk = "dbg_q" if tag == "q" else "dbg_k"
                        op("scalar", lambda e, tl=tl, dk=dk: e.dma_start(
                            out=dbg[dk][tl * 128:(tl + 1) * 128, :], in_=qt[:]),
                           inc=s_sst, k=16)
                else:
                    xt16 = xt[:].bitcast(BF16)
                    drain("vector")
                    op("vector", lambda e, xt16=xt16: e.tensor_copy(
                        xt16[:, 0:1024], qt[:]),
                       waits=[(s_ve, tile_ready),
                              ((s_sst, s_sst.n) if s_sst.n else None)])
                    drain("vector")
                    op("vector", lambda e, xt16=xt16: e.tensor_tensor(
                        xt16[:, 1024:2048], qt[:], xt16[:, 0:1024], AL.subtract),
                       inc=s_ve)
                    op("scalar", lambda e, tl=tl, xt16=xt16: e.dma_start(
                        out=dvg_h[tl * 128:(tl + 1) * 128, :], in_=xt16[:, 0:1024]),
                       waits=[(s_ve, s_ve.n)], inc=s_sst, k=16)
                    op("scalar", lambda e, tl=tl, xt16=xt16: e.dma_start(
                        out=dvg_l[tl * 128:(tl + 1) * 128, :], in_=xt16[:, 1024:2048]),
                       inc=s_sst, k=16)
                    for jd in range(2):
                        op("tensor", lambda e, tl=tl, jd=jd: e.matmul(
                            pb[2 + jd][0:1, :], onesc[:, 0:1],
                            qt[:, jd * 512:(jd + 1) * 512],
                            start=(tl == 0), stop=(tl == NT - 1)),
                           inc=s_mm, waits=[(s_ve, tile_ready)])
                    if tl == NT - 1:
                        for jd in range(2):
                            op("vector", lambda e, jd=jd: e.tensor_copy(
                                csum[0:1, jd * 512:(jd + 1) * 512],
                                pb[2 + jd][0:1, :]),
                               waits=[(s_mm, s_mm.n)], inc=s_ve)

        proj_phase(dwk, True, gkrep[:], gk_tm, (dkn_h, dkn_l), True, "k")
        proj_phase(dwv, False, None, None, None, False, "v")
        proj_phase(dwq, True, gqrep[:], gq_tm, (dqn_h, dqn_l), True, "q")
        if debug:
            op("scalar", lambda e: e.dma_start(out=dbg["dbg_stage"], in_=xt[:]),
               waits=[(s_ve, s_ve.n), (s_sst, s_sst.n)], inc=s_sst, k=16)
            op("sync", lambda e: e.dma_start(
                out=dbg["dbg_rd"].bitcast(BF16)[:, 0:T], in_=dqn_h[1]),
               waits=[(s_sst, s_sst.n)], inc=s_ld, k=16)
            op("sync", lambda e: e.dma_start(
                out=dbg["dbg_rd"].bitcast(BF16)[:, 2 * T:3 * T], in_=dqn_l[1]),
               inc=s_ld, k=16)
            op("sync", lambda e: e.dma_start(
                out=dbg["dbg_rd"].bitcast(BF16)[:, T:T + NKC * DH].rearrange(
                    "p (kc d) -> p kc d", d=DH), 
                in_=dvg_h[:, (2 * FC - 2) * DH:(2 * FC - 2) * DH + DH].rearrange(
                    "(kc p) d -> p kc d", p=128)),
               inc=s_ld, k=16)
        if debug:
            op("scalar", lambda e: e.dma_start(
                out=dbg["dbg_gq"].rearrange("(tl p) h -> p tl h", p=128),
                in_=gq_tm[:].rearrange("p (tl h) -> p tl h", h=H)),
               waits=[(s_ve, s_ve.n)], inc=s_sst, k=16)
            op("scalar", lambda e: e.dma_start(
                out=dbg["dbg_gk"].rearrange("(tl p) h -> p tl h", p=128),
                in_=gk_tm[:].rearrange("p (tl h) -> p tl h", h=H)),
               inc=s_sst, k=16)
            op("scalar", lambda e: e.dma_start(out=dbg["dbg_csum"], in_=csum[:]),
               inc=s_sst, k=16)

        ve_at_D = s_ve.n
        sst_at_D = s_sst.n
        dt16 = dt_[:].bitcast(BF16)
        sq16 = sqt[:].bitcast(BF16)
        gqbc = xt[:, 0:512]
        gqrowA = xt[0:1, 512:1024]
        gqrowB = s2b[0:1, 0:512]
        tfs = (qt[:], s2a[:])
        sab_t = (pbA, pbB)
        sab_sc = [0, 0]
        sab_ve = [0, 0]
        fused_evac_ve = ve_at_D
        gq_ve = ve_at_D
        grow_ve = ve_at_D
        fusedT3 = fusedT[:].rearrange("p (c t) -> p c t", t=T)
        dbg_praw_sb = h1g
        for hp in range(FC):
            for (dst, srcd) in ((qnh_hi, dqn_h), (qnh_lo, dqn_l),
                                (knh_hi, dkn_h), (knh_lo, dkn_l)):
                op("sync", lambda e, dst=dst, srcd=srcd, hp=hp: e.dma_start(
                    out=dst, in_=srcd[hp]),
                   inc=s_ld, k=16,
                   waits=([(s_mm, s_mm.n), (s_sst, s_sst.n)] if dst is qnh_hi else []))
            for (dst, srcd) in ((vgh_hi, dvg_h), (vgh_lo, dvg_l)):
                for h2 in (0, 1):
                    op("sync", lambda e, dst=dst, srcd=srcd, hp=hp, h2=h2: e.dma_start(
                        out=dst[:, h2 * NKC * DH:(h2 + 1) * NKC * DH].rearrange(
                            "p (kc d) -> p kc d", d=DH),
                        in_=srcd[:, (2 * hp + h2) * DH:(2 * hp + h2 + 1) * DH].rearrange(
                            "(kc p) d -> p kc d", p=128)),
                       inc=s_ld, k=16)
            hd_ld = s_ld.n
            for qc in range(NQC):
                ntl = QCW // 128
                def gqtA(e, hp=hp, qc=qc, ntl=ntl):
                    ins = None
                    for i in range(ntl):
                        tl = qc * ntl + i
                        ins = e.transpose(pbC[2][0:1, i * 128:(i + 1) * 128],
                                          gq_tm[:, tl * H + 2 * hp:tl * H + 2 * hp + 1],
                                          eyet[:])
                    return ins
                def gqtB(e, hp=hp, qc=qc, ntl=ntl):
                    ins = None
                    for i in range(ntl):
                        tl = qc * ntl + i
                        ins = e.transpose(pbC[3][0:1, i * 128:(i + 1) * 128],
                                          gq_tm[:, tl * H + 2 * hp + 1:tl * H + 2 * hp + 2],
                                          eyet[:])
                    return ins
                op("tensor", gqtA, waits=[(s_ve, grow_ve)])
                op("tensor", gqtB, inc=s_mm)
                op("vector", lambda e: e.tensor_copy(gqrowA, pbC[2][0:1, 0:QCW]),
                   waits=[(s_mm, s_mm.n),
                          ((s_sst, sst_at_D) if hp == 0 and qc == 0 else None)])
                op("vector", lambda e: e.tensor_copy(gqrowB, pbC[3][0:1, 0:QCW]),
                   inc=s_ve)
                grow_ve = s_ve.n
                def gbc(e):
                    e.matmul(pbC[1][0:64, 0:QCW], ones1[0:1, 0:64], gqrowA,
                             start=True, stop=True, tile_position=(0, 0))
                    return e.matmul(pbC[1][64:128, 0:QCW], ones1[0:1, 0:64], gqrowB,
                                    start=True, stop=True, tile_position=(0, 64))
                op("tensor", gbc, inc=s_mm, waits=[(s_ve, grow_ve), (s_ve, gq_ve)])
                op("vector", lambda e: e.tensor_copy(gqbc, pbC[1][:, 0:QCW]),
                   waits=[(s_mm, s_mm.n)], inc=s_ve)
                gq_ve = s_ve.n
                op("tensor", lambda e, hp=hp: e.matmul(
                    pbC[0][:, 0:QCW], csum[0:1, hp * 128:(hp + 1) * 128],
                    onesr[0:1, 0:QCW], start=True, stop=False),
                   inc=s_mm, waits=[(s_ve, fused_evac_ve), (s_ld, hd_ld)])
                for kc in range(NKC):
                    sl = kc % 2
                    sab = sab_t[sl]
                    def smm(e, kc=kc, qc=qc, sab=sab):
                        ins = None
                        ops = ((qnh_hi, knh_hi), (qnh_lo, knh_hi), (qnh_hi, knh_lo))
                        for i, (qb, kb) in enumerate(ops):
                            e.matmul(sab[:, 0:QCW],
                                     kb[0:64, kc * 128:(kc + 1) * 128],
                                     qb[0:64, qc * QCW:(qc + 1) * QCW],
                                     start=(i == 0), stop=(i == 2),
                                     tile_position=(0, 0))
                            ins = e.matmul(sab[:, 512:512 + QCW],
                                           kb[64:128, kc * 128:(kc + 1) * 128],
                                           qb[64:128, qc * QCW:(qc + 1) * QCW],
                                           start=(i == 0), stop=(i == 2),
                                           tile_position=(64, 0))
                        return ins
                    op("tensor", smm, inc=s_mm,
                       waits=[(s_ld, hd_ld),
                              ((s_sc, sab_sc[sl]) if sab_sc[sl] else (s_ve, ve_at_D))])
                    smm_n = s_mm.n
                    op("scalar", lambda e, sab=sab, sl=sl: e.activation(
                        tfs[sl], sab[:, 0:1024], AF.Tanh, bias=CM405, scale=C15),
                       waits=[(s_mm, smm_n)], inc=s_sc)
                    sab_sc[sl] = s_sc.n
                    op("vector", lambda e, sl=sl: e.tensor_copy(
                        dt16[:, sl * 1024:(sl + 1) * 1024], tfs[sl]),
                       waits=[(s_sc, s_sc.n), (s_mm, smm_n)])
                    drain("vector")
                    op("vector", lambda e, sl=sl: e.tensor_tensor(
                        sq16[:, sl * 1024:(sl + 1) * 1024], tfs[sl],
                        dt16[:, sl * 1024:(sl + 1) * 1024], AL.subtract),
                       inc=s_ve)
                    sab_ve[sl] = s_ve.n
                    def fmm(e, kc=kc, sl=sl):
                        ins = None
                        ops = ((vgh_hi, dt16), (vgh_lo, dt16), (vgh_hi, sq16))
                        for i, (vb, tb) in enumerate(ops):
                            e.matmul(pbC[0][0:64, 0:QCW],
                                     vb[:, kc * DH:(kc + 1) * DH],
                                     tb[:, sl * 1024:sl * 1024 + QCW],
                                     start=False, stop=(kc == NKC - 1 and i == 2),
                                     tile_position=(0, 0))
                            ins = e.matmul(pbC[0][64:128, 0:QCW],
                                           vb[:, NKC * DH + kc * DH:
                                              NKC * DH + (kc + 1) * DH],
                                           tb[:, sl * 1024 + 512:sl * 1024 + 512 + QCW],
                                           start=False,
                                           stop=(kc == NKC - 1 and i == 2),
                                           tile_position=(0, 64))
                        return ins
                    op("tensor", fmm, inc=s_mm, waits=[(s_ve, sab_ve[sl])])
                if debug and hp == 1 and qc == 0:
                    op("vector", lambda e: e.tensor_copy(
                        dbg_praw_sb[:, 0:QCW], pbC[0][:, 0:QCW]),
                       waits=[(s_mm, s_mm.n)])
                    drain("vector")
                    op("vector", lambda e: e.tensor_copy(
                        dbg_praw_sb[:, 512:512 + QCW], gqbc), inc=s_ve)
                    op("scalar", lambda e: e.dma_start(
                        out=dbg["dbg_praw"], in_=dbg_praw_sb[:, 0:1024]),
                       waits=[(s_ve, s_ve.n)], inc=s_sst, k=16)
                def fev(e, hp=hp, qc=qc):
                    dstv = fusedT3[:, hp, qc * QCW:(qc + 1) * QCW]
                    return e.tensor_tensor(dstv, pbC[0][:, 0:QCW], gqbc, AL.mult)
                op("vector", fev, waits=[(s_mm, s_mm.n), (s_ve, gq_ve)], inc=s_ve)
                fused_evac_ve = s_ve.n
        if debug:
            op("scalar", lambda e: e.dma_start(out=dbg["dbg_fused"], in_=fusedT[:]),
               waits=[(s_ve, s_ve.n)], inc=s_sst, k=16)
            op("scalar", lambda e: e.dma_start(out=dbg["dbg_gqbc"], in_=gqbc),
               inc=s_sst, k=16)
            op("scalar", lambda e: e.dma_start(out=dbg["dbg_t0"], in_=tfs[0]),
               inc=s_sst, k=16)
            op("scalar", lambda e: e.dma_start(out=dbg["dbg_t1"], in_=tfs[1]),
               inc=s_sst, k=16)
            op("scalar", lambda e: e.dma_start(
                out=dbg["dbg_qnh"].rearrange("p j -> p j").bitcast(BF16)[:, 0:T]
                if False else dbg["dbg_qnh"].bitcast(BF16)[:, 0:T], in_=qnh_hi),
               inc=s_sst, k=16)
            op("scalar", lambda e: e.dma_start(
                out=dbg["dbg_vgh"].bitcast(BF16)[:, 0:2 * NKC * DH], in_=vgh_hi),
               inc=s_sst, k=16)

        def lwo(e):
            dstv = Wbuf[:].rearrange("p (c j) -> p c j", j=D)
            return e.dma_start(out=dstv, in_=dwo.rearrange("(c p) j -> p c j", p=128))
        op("sync", lwo, inc=s_ld, k=16,
           waits=[(s_mm, s_mm.n), ((s_sst, s_sst.n) if s_sst.n else None)])
        op("sync", lambda e: e.dma_start(out=brow[0:1, :], in_=dbo), inc=s_ld, k=16)
        wo_ld = s_ld.n
        sc_at_E = s_sc.n
        pe_evac = {0: 0, 1: 0}
        for tl in range(NT):
            op("sync", lambda e, tl=tl: e.dma_start(
                out=xt[:], in_=dx[tl * 128:(tl + 1) * 128, :]),
               inc=s_ld, k=16, waits=[(s_ve, s_ve.n)])
            x_ld = s_ld.n
            for jd in range(2):
                def omm(e, tl=tl, jd=jd):
                    ins = None
                    for c in range(FC):
                        lhsT = fusedT[:].rearrange("p (c t) -> p c t", t=T)[
                            :, c, tl * 128:(tl + 1) * 128]
                        ins = e.matmul(pb[jd][:], lhsT,
                                       Wbuf[:, c * D + jd * 512:
                                            c * D + (jd + 1) * 512],
                                       start=(c == 0), stop=False)
                    ins = e.matmul(pb[jd][:], ones1[0:1, :],
                                   brow[0:1, jd * 512:(jd + 1) * 512],
                                   start=False, stop=True)
                    return ins
                op("tensor", omm, inc=s_mm,
                   waits=[(s_ld, wo_ld), (s_ve, s_ve.n),
                          ((s_ve, pe_evac[jd]) if pe_evac[jd] else None),
                          ((s_sc, sc_at_E) if tl == 0 else None)])
                op("vector", lambda e, tl=tl, jd=jd: e.scalar_tensor_tensor(
                    s2a[:, jd * 512:(jd + 1) * 512], pb[jd][:], 1.0,
                    yb[:, tl * D + jd * 512:tl * D + (jd + 1) * 512],
                    AL.bypass, AL.mult),
                   waits=[(s_mm, s_mm.n), (s_sc, g_done_sc),
                          ((s_out, s_out.n) if s_out.n else None)])
                drain("vector")
                op("vector", lambda e, jd=jd: e.tensor_tensor(
                    s2a[:, jd * 512:(jd + 1) * 512],
                    s2a[:, jd * 512:(jd + 1) * 512],
                    xt[:, jd * 512:(jd + 1) * 512], AL.add),
                   waits=[(s_ld, x_ld)], inc=s_ve)
                pe_evac[jd] = s_ve.n
            op("scalar", lambda e, tl=tl: e.dma_start(
                out=dout[tl * 128:(tl + 1) * 128, :], in_=s2a[:]),
               waits=[(s_ve, s_ve.n)], inc=s_out, k=16)
        plan["scalar"].append(lambda e, v=s_out.n: e.wait_ge(s_out.sem, v))

        with nc.Block() as block:
            block.sync(lambda e: [f(e) for f in plan["sync"]])
            block.vector(lambda e: [f(e) for f in plan["vector"]])
            block.scalar(lambda e: [f(e) for f in plan["scalar"]])
            block.tensor(lambda e: [f(e) for f in plan["tensor"]])
            block.gpsimd(lambda e: [f(e) for f in plan["gpsimd"]])
    return nc


def make_inputs(xc, wq, wk, wv, g_q, g_k, wo, b_o, gamma, beta, Wr, br,
                W1, b1, W2, b2):
    return {
        "x": np.ascontiguousarray(xc),
        "wq": np.ascontiguousarray(wq),
        "wk": np.ascontiguousarray(wk),
        "wv": np.ascontiguousarray(wv),
        "wo": np.ascontiguousarray(wo),
        "bo": np.ascontiguousarray(b_o.reshape(1, D)),
        "gamma_fm": np.ascontiguousarray(gamma.reshape(FC, 128).T),
        "beta_fm": np.ascontiguousarray(beta.reshape(FC, 128).T),
        "gq_rep": np.ascontiguousarray(np.broadcast_to(g_q, (128, DH))),
        "gk_rep": np.ascontiguousarray(np.broadcast_to(g_k, (128, DH))),
        "wr_sb": np.ascontiguousarray(
            Wr.reshape(FC, 128, E).transpose(1, 0, 2).reshape(128, FC * E)),
        "br": np.ascontiguousarray(br.reshape(1, E)),
        "W1": np.ascontiguousarray(W1),
        "b1_fm": np.ascontiguousarray(
            b1.reshape(E, DFF // 128, 128).transpose(2, 0, 1).reshape(128, -1)),
        "W2": np.ascontiguousarray(W2),
        "b2": np.ascontiguousarray(b2.reshape(1, E * D)),
        "eye": np.eye(128, dtype=np.float32),
    }


_NC_CACHE = {}


def kernel(L0_attn_x, L0_attn_w_q, L0_attn_w_k, L0_attn_w_v, L0_attn_g_q,
           L0_attn_g_k, L0_attn_w_o, L0_attn_b_o, L0_attn_gamma, L0_attn_beta,
           L0_ffn_Wr, L0_ffn_br, L0_ffn_W1, L0_ffn_b1, L0_ffn_W2, L0_ffn_b2):
    x = np.asarray(L0_attn_x, dtype=np.float32)
    B, T, _ = x.shape
    if (T,) not in _NC_CACHE:
        _NC_CACHE[(T,)] = build(T=T)
    nc = _NC_CACHE[(T,)]
    args = [np.asarray(a, dtype=np.float32) for a in
            (L0_attn_w_q, L0_attn_w_k, L0_attn_w_v, L0_attn_g_q, L0_attn_g_k,
             L0_attn_w_o, L0_attn_b_o, L0_attn_gamma, L0_attn_beta,
             L0_ffn_Wr, L0_ffn_br, L0_ffn_W1, L0_ffn_b1, L0_ffn_W2, L0_ffn_b2)]
    in_maps = [make_inputs(x[c], *args) for c in range(B)]
    res = run_bass_kernel_spmd(nc, in_maps, core_ids=list(range(B)))
    return np.stack([np.asarray(r["out"]) for r in res.results], axis=0)


# version 03
# speedup vs baseline: 1.3187x; 1.2291x over previous
import numpy as np
from contextlib import ExitStack

import concourse.bass as bass
import concourse.mybir as mybir
from concourse.bass_utils import run_bass_kernel_spmd

F32 = mybir.dt.float32
AF = mybir.ActivationFunctionType
AL = mybir.AluOpType
AX = mybir.AxisListType

D = 1024
H = 16
DH = 64
E = 8
DFF = 4096
FC = D // 128
GDFF = 256
NG = DFF // GDFF
LN_EPS = 1e-5
NORM_EPS = 1e-12


class Ctr:
    def __init__(self, sem):
        self.sem = sem
        self.n = 0


def build(T=2048, debug=False):
    NT = T // 128
    FTT = min(512, T)
    NFT = T // FTT
    NQC = max(1, T // 512)
    QCW = min(512, T)
    NKC = T // 128
    NH1 = GDFF // 128

    nc = bass.Bass("TRN2", target_bir_lowering=False)

    dx = nc.dram_tensor("x", [T, D], F32, kind="ExternalInput").ap()
    dwq = nc.dram_tensor("wq", [D, D], F32, kind="ExternalInput").ap()
    dwk = nc.dram_tensor("wk", [D, D], F32, kind="ExternalInput").ap()
    dwv = nc.dram_tensor("wv", [D, D], F32, kind="ExternalInput").ap()
    dwo = nc.dram_tensor("wo", [D, D], F32, kind="ExternalInput").ap()
    dbo = nc.dram_tensor("bo", [1, D], F32, kind="ExternalInput").ap()
    dgamma = nc.dram_tensor("gamma_fm", [128, FC], F32, kind="ExternalInput").ap()
    dbeta = nc.dram_tensor("beta_fm", [128, FC], F32, kind="ExternalInput").ap()
    dgq = nc.dram_tensor("gq_rep", [128, DH], F32, kind="ExternalInput").ap()
    dgk = nc.dram_tensor("gk_rep", [128, DH], F32, kind="ExternalInput").ap()
    dwr = nc.dram_tensor("wr_sb", [128, FC * E], F32, kind="ExternalInput").ap()
    dbr = nc.dram_tensor("br", [1, E], F32, kind="ExternalInput").ap()
    dW1 = nc.dram_tensor("W1", [E, D, DFF], F32, kind="ExternalInput").ap()
    db1 = nc.dram_tensor("b1_fm", [128, E * (DFF // 128)], F32, kind="ExternalInput").ap()
    dW2 = nc.dram_tensor("W2", [E, DFF, D], F32, kind="ExternalInput").ap()
    db2 = nc.dram_tensor("b2", [1, E * D], F32, kind="ExternalInput").ap()
    deye = nc.dram_tensor("eye", [128, 128], F32, kind="ExternalInput").ap()
    dout = nc.dram_tensor("out", [T, D], F32, kind="ExternalOutput").ap()
    BF16 = mybir.dt.bfloat16
    dqn_h = nc.dram_tensor("qn_sh", [FC, 128, T], BF16).ap()
    dqn_l = nc.dram_tensor("qn_sl", [FC, 128, T], BF16).ap()
    dkn_h = nc.dram_tensor("kn_sh", [FC, 128, T], BF16).ap()
    dkn_l = nc.dram_tensor("kn_sl", [FC, 128, T], BF16).ap()
    dvg_h = nc.dram_tensor("vg_sh", [T, D], BF16).ap()
    dvg_l = nc.dram_tensor("vg_sl", [T, D], BF16).ap()
    dbg = {}
    if debug:
        for name, shp in [("dbg_g", [T, D]), ("dbg_oh", [T, E]), ("dbg_gate", [T, 1]),
                          ("dbg_q", [T, D]), ("dbg_k", [T, D]),
                          ("dbg_gq", [T, H]), ("dbg_gk", [T, H]),
                          ("dbg_fused", [128, FC * T]), ("dbg_csum", [1, D]),
                          ("dbg_xn", [T, D]), ("dbg_gqbc", [128, 512]),
                          ("dbg_stage", [128, 1024]), ("dbg_rd", [128, 2 * T]),
                          ("dbg_praw", [128, 1024]),
                          ("dbg_t0", [128, 1024]), ("dbg_t1", [128, 1024]),
                          ("dbg_qnh", [128, 2048]), ("dbg_vgh", [128, 2048])]:
            dbg[name] = nc.dram_tensor(name, shp, F32, kind="ExternalOutput").ap()

    plan = {k: [] for k in ("sync", "vector", "scalar", "tensor", "gpsimd")}

    def op(eng, fn, inc=None, k=1, waits=()):
        waits = [w for w in waits if w is not None]
        for (c, v) in waits:
            assert v <= c.n, f"wait on future value: {v} > {c.n}"
        if inc is not None:
            inc.n += k
        def run(e, fn=fn, inc=inc, k=k, waits=tuple(waits)):
            for (c, v) in waits:
                if v > 0:
                    e.wait_ge(c.sem, v)
            ins = fn(e)
            if inc is not None:
                ins.then_inc(inc.sem, k)
        plan[eng].append(run)

    def drain(eng):
        plan[eng].append(lambda e: e.drain())

    ctx = ExitStack()
    with ctx:
        sb = lambda n, s: ctx.enter_context(nc.sbuf_tensor(n, s, F32))
        xnT = sb("xnT", [128, FC * T])
        yb = sb("yb", [128, NT * D])
        Wbuf = sb("Wbuf", [128, 8 * D])
        xt = sb("xt", [128, D])
        dt_ = sb("dt_", [128, D])
        qt = sb("qt", [128, D])
        s2a = sb("s2a", [128, D])
        s2b = sb("s2b", [128, D])
        h1g = sb("h1g", [128, 2048])
        sqt = h1g[:, 0:1024]
        stats = sb("stats", [128, 32])
        nhead = sb("nhead", [128, 4 * H])
        gq_tm = sb("gq_tm", [128, NT * H])
        gk_tm = sb("gk_tm", [128, NT * H])
        oh_tm = sb("oh_tm", [128, NT * E])
        gatev = sb("gatev", [128, NT])
        gammat = sb("gammat", [128, FC])
        betat = sb("betat", [128, FC])
        gqrep = sb("gqrep", [128, DH])
        gkrep = sb("gkrep", [128, DH])
        wr_sb = sb("wr_sbuf", [128, FC * E])
        wrhl = sb("wrhl", [128, FC * E])
        b1_sb = sb("b1_sbuf", [128, E * (DFF // 128)])
        eyet = sb("eyet", [128, 128])
        cst = sb("cst", [128, 8])
        onesr = sb("onesr", [1, 512])
        ones1 = sb("ones128", [1, 128])
        onesc = sb("onescol", [128, 1])
        brow = sb("brow", [1, D])
        br_sb = sb("br_sbuf", [1, E])
        csum = sb("csum_row", [1, D])
        Wb16 = Wbuf[:].bitcast(BF16)
        xnT16 = xnT[:].bitcast(BF16)
        XLO = FC * T
        def xh3(c, a, b):
            return xnT16[:, c * T + a:c * T + b]
        def xl3(c, a, b):
            return xnT16[:, XLO + c * T + a:XLO + c * T + b]
        def wph(c, a, b):
            return Wb16[:, c * D + a:c * D + b]
        def wpl(c, a, b):
            return Wb16[:, 8192 + c * D + a:8192 + c * D + b]
        qnh_hi = Wb16[:, 0:T]
        qnh_lo = Wb16[:, T:2 * T]
        knh_hi = Wb16[:, 2 * T:3 * T]
        knh_lo = Wb16[:, 3 * T:4 * T]
        VGO = 4 * T
        vgh_hi = Wb16[:, VGO:VGO + 2 * NKC * DH]
        vgh_lo = Wb16[:, VGO + 2 * NKC * DH:VGO + 4 * NKC * DH]
        pbA = ctx.enter_context(nc.psum_tensor("pbA", [128, 1024], F32))
        pbB = ctx.enter_context(nc.psum_tensor("pbB", [128, 1024], F32))
        pbC = [ctx.enter_context(nc.psum_tensor(f"pbC{i}", [128, 512], F32))
               for i in range(4)]
        pb = {0: pbA[:, 0:512], 1: pbA[:, 512:1024],
              4: pbB[:, 0:512], 5: pbB[:, 512:1024],
              2: pbC[0][:], 3: pbC[1][:], 6: pbC[2][:], 7: pbC[3][:]}

        s_ld = Ctr(ctx.enter_context(nc.semaphore("s_ld")))
        s_mm = Ctr(ctx.enter_context(nc.semaphore("s_mm")))
        s_ve = Ctr(ctx.enter_context(nc.semaphore("s_ve")))
        s_sc = Ctr(ctx.enter_context(nc.semaphore("s_sc")))
        s_sst = Ctr(ctx.enter_context(nc.semaphore("s_sst")))
        s_out = Ctr(ctx.enter_context(nc.semaphore("s_out")))

        CI1024 = cst[:, 0:1]
        CEPS = cst[:, 1:2]
        C15 = cst[:, 2:3]
        CM405 = cst[:, 3:4]
        fusedT = xnT

        xnT3 = xnT[:].rearrange("p (c t) -> p c t", t=T)

        def ldi(dst, src):
            op("sync", lambda e, d=dst, s=src: e.dma_start(out=d, in_=s),
               inc=s_ld, k=16)

        ldi(gammat[:], dgamma)
        ldi(betat[:], dbeta)
        ldi(gqrep[:], dgq)
        ldi(gkrep[:], dgk)
        ldi(wr_sb[:], dwr)
        ldi(b1_sb[:], db1)
        ldi(eyet[:], deye)
        ldi(br_sb[:], dbr)
        for aptgt, val in [(cst[:, 0:1], 1.0 / D), (cst[:, 1:2], LN_EPS),
                           (cst[:, 2:3], 15.0), (cst[:, 3:4], -4.05),
                           (onesr[:], 1.0), (ones1[:], 1.0), (onesc[:], 1.0)]:
            op("vector", lambda e, a=aptgt, v=val: e.memset(a, v), inc=s_ve)
        wr16 = wrhl[:].bitcast(BF16)
        op("vector", lambda e: e.tensor_copy(wr16[:, 0:FC * E], wr_sb[:]),
           waits=[(s_ld, s_ld.n)])
        drain("vector")
        op("vector", lambda e: e.tensor_tensor(wr16[:, FC * E:2 * FC * E], wr_sb[:],
                                               wr16[:, 0:FC * E], AL.subtract))
        op("vector", lambda e: e.memset(yb[:], 0.0), inc=s_ve)
        setup_ve = s_ve.n

        def ln_tile(src, out, ld_wait=None):
            drain("vector")
            op("vector", lambda e: e.reduce_sum(stats[:, 0:1], src, axis=AX.X),
               waits=[ld_wait] if ld_wait else [])
            drain("vector")
            op("vector", lambda e: e.tensor_scalar_mul(stats[:, 1:2], stats[:, 0:1], 1.0 / D))
            drain("vector")
            op("vector", lambda e: e.tensor_scalar(dt_[:], src, stats[:, 1:2],
                                                   None, AL.subtract), inc=s_ve)
            op("scalar", lambda e: e.activation(sqt, dt_[:], AF.Square,
                                                accum_out=stats[:, 2:3]),
               waits=[(s_ve, s_ve.n)], inc=s_sc)
            drain("scalar")
            op("scalar", lambda e: e.activation(stats[:, 3:4], stats[:, 2:3],
                                                AF.Sqrt, scale=CI1024, bias=CEPS),
               inc=s_sc)
            op("vector", lambda e: e.reciprocal(stats[:, 4:5], stats[:, 3:4]),
               waits=[(s_sc, s_sc.n)])
            drain("vector")
            op("vector", lambda e, out=out: e.tensor_scalar(out, dt_[:], stats[:, 4:5],
                                                            None, AL.mult), inc=s_ve)

        tr_evac = {6: 0, 7: 0}

        def transpose8(src, evac_fn_for_half):
            src_ready = s_ve.n
            for half in (0, 1):
                pbi = 6 + half
                def tmf(e, half=half, pbi=pbi, src=src):
                    ins = None
                    for c in range(4):
                        cc = half * 4 + c
                        ins = e.transpose(pb[pbi][:, c * 128:(c + 1) * 128],
                                          src[:, cc * 128:(cc + 1) * 128], eyet[:])
                    return ins
                op("tensor", tmf, inc=s_mm,
                   waits=[(s_ve, src_ready),
                          ((s_ve, tr_evac[pbi]) if tr_evac[pbi] else None)])
                evac_fn_for_half(half, pb[pbi])
                tr_evac[pbi] = s_ve.n

        for tl in range(NT):
            op("sync", lambda e, tl=tl: e.dma_start(out=xt[:],
                                                    in_=dx[tl * 128:(tl + 1) * 128, :]),
               inc=s_ld, k=16, waits=[(s_ve, s_ve.n if tl else setup_ve)])
            ln_tile(xt[:], qt[:], ld_wait=(s_ld, s_ld.n))
            def mk_evac(half, pbk, tl=tl):
                hi3 = xnT16[:, 0:FC * T].rearrange("p (c t) -> p c t", t=T)[
                    :, half * 4:(half + 1) * 4, tl * 128:(tl + 1) * 128]
                lo3 = xnT16[:, XLO:XLO + FC * T].rearrange("p (c t) -> p c t", t=T)[
                    :, half * 4:(half + 1) * 4, tl * 128:(tl + 1) * 128]
                p3 = pbk[:].rearrange("p (c t) -> p c t", t=128)
                op("vector", lambda e, hi3=hi3, p3=p3: e.tensor_copy(hi3, p3),
                   waits=[(s_mm, s_mm.n)])
                drain("vector")
                op("vector", lambda e, hi3=hi3, lo3=lo3, p3=p3: e.tensor_tensor(
                    lo3, p3, hi3, AL.subtract), inc=s_ve)
            transpose8(qt[:], mk_evac)
        xnT_ready = s_ve.n
        if debug:
            op("scalar", lambda e: e.dma_start(
                out=dbg["dbg_xn"].rearrange("(a p) j -> p a j", p=128),
                in_=xnT[:].rearrange("p (a j) -> p a j", j=D)),
               waits=[(s_ve, s_ve.n)], inc=s_sst, k=16)

        for tl in range(NT):
            def rmm(e, tl=tl):
                ins = None
                first = True
                for c in range(FC):
                    xh = xh3(c, tl * 128, (tl + 1) * 128)
                    xl = xl3(c, tl * 128, (tl + 1) * 128)
                    wh = wr16[:, c * E:(c + 1) * E]
                    wl = wr16[:, FC * E + c * E:FC * E + (c + 1) * E]
                    for lhsT, rhs in ((xh, wh), (xl, wh), (xh, wl)):
                        ins = e.matmul(pb[5][:, 0:E], lhsT, rhs,
                                       start=first, stop=False)
                        first = False
                ins = e.matmul(pb[5][:, 0:E], ones1[0:1, :], br_sb[0:1, :],
                               start=False, stop=True)
                return ins
            op("tensor", rmm, inc=s_mm, waits=[(s_ve, s_ve.n), (s_ld, s_ld.n)])
            op("vector", lambda e: e.tensor_copy(stats[:, 8:8 + E], pb[5][:, 0:E]),
               waits=[(s_mm, s_mm.n), (s_sc, s_sc.n)], inc=s_ve)
            drain("vector")
            op("vector", lambda e: e.reduce_max(stats[:, 16:17], stats[:, 8:8 + E],
                                                axis=AX.X, negate=True))
            drain("vector")
            op("vector", lambda e: e.tensor_scalar_mul(stats[:, 17:18],
                                                       stats[:, 16:17], -1.0))
            drain("vector")
            op("vector", lambda e, tl=tl: e.tensor_scalar(oh_tm[:, tl * E:(tl + 1) * E],
                                                          stats[:, 8:8 + E],
                                                          stats[:, 17:18], None,
                                                          AL.is_equal), inc=s_ve)
            op("scalar", lambda e: e.activation(stats[:, 18:18 + E], stats[:, 8:8 + E],
                                                AF.Exp, bias=stats[:, 16:17],
                                                accum_out=stats[:, 26:27]),
               waits=[(s_ve, s_ve.n)], inc=s_sc)
            op("vector", lambda e, tl=tl: e.reciprocal(gatev[:, tl:tl + 1],
                                                       stats[:, 26:27]),
               waits=[(s_sc, s_sc.n)], inc=s_ve)
        router_ve = s_ve.n

        NCH = DFF // 128
        h1g16 = h1g[:].bitcast(BF16)
        NGEN = E * NG
        wsplit_sc = [0, 0]
        wsplit_ve = [0, 0]
        wuse_mm = [0, 0]
        stage_sc = [0, 0, 0, 0]
        stage_ve = [0, 0, 0, 0]
        h1_used = [0, 0]
        pb_h_evac = {0: 0, 1: 0}
        pb_y_evac = {4: 0, 5: router_ve}
        b2_lds = {}

        def w_load_split(gen):
            e_i, g = divmod(gen, NG)
            p = gen % 2
            base = p * 8192
            stgs = (xt, qt, s2a, s2b)
            if e_i not in b2_lds:
                op("sync", lambda e, e_i=e_i: e.dma_start(
                    out=brow[0:1, :], in_=db2[0:1, e_i * D:(e_i + 1) * D]),
                   inc=s_ld, k=16,
                   waits=[(s_mm, wuse_mm[p]) if wuse_mm[p] else None])
                b2_lds[e_i] = s_ld.n
            def l1a(e, e_i=e_i, g=g):
                return e.dma_start(
                    out=xt[:].rearrange("p (c f) -> p c f", f=GDFF),
                    in_=dW1[e_i, 0:512, g * GDFF:(g + 1) * GDFF].rearrange(
                        "(c p) f -> p c f", p=128))
            def l1b(e, e_i=e_i, g=g):
                return e.dma_start(
                    out=qt[:].rearrange("p (c f) -> p c f", f=GDFF),
                    in_=dW1[e_i, 512:1024, g * GDFF:(g + 1) * GDFF].rearrange(
                        "(c p) f -> p c f", p=128))
            def l2a(e, e_i=e_i, g=g):
                return e.dma_start(out=s2a[:],
                                   in_=dW2[e_i, g * GDFF:g * GDFF + 128, :])
            def l2b(e, e_i=e_i, g=g):
                return e.dma_start(out=s2b[:],
                                   in_=dW2[e_i, g * GDFF + 128:g * GDFF + 256, :])
            lds = []
            for i, fn in enumerate((l1a, l1b, l2a, l2b)):
                op("sync", fn, inc=s_ld, k=16,
                   waits=[((s_sc, stage_sc[i]) if stage_sc[i] else None),
                          ((s_ve, stage_ve[i]) if stage_ve[i] else None),
                          ((s_mm, s_mm.n) if gen == 0 else None),
                          ((s_ve, s_ve.n) if gen == 0 else None)])
                lds.append(s_ld.n)
            hidst = (Wb16[:, base + 0:base + 1024],
                     Wb16[:, base + 1024:base + 2048],
                     Wb16[:, base + 4096:base + 5120],
                     Wb16[:, base + 5120:base + 6144])
            lodst = (Wb16[:, base + 2048:base + 3072],
                     Wb16[:, base + 3072:base + 4096],
                     Wb16[:, base + 6144:base + 7168],
                     Wb16[:, base + 7168:base + 8192])
            for i, stg in enumerate(stgs):
                op("scalar", lambda e, d=hidst[i], s=stg: e.copy(d, s[:]),
                   waits=[(s_ld, lds[i]),
                          ((s_mm, wuse_mm[p]) if wuse_mm[p] else None)],
                   inc=s_sc)
                stage_sc[i] = s_sc.n
                op("vector", lambda e, d=lodst[i], s=stg, h=hidst[i]:
                   e.tensor_tensor(d, s[:], h, AL.subtract),
                   waits=[(s_sc, s_sc.n),
                          ((s_mm, wuse_mm[p]) if wuse_mm[p] else None)],
                   inc=s_ve)
                stage_ve[i] = s_ve.n
            wsplit_sc[p] = s_sc.n
            wsplit_ve[p] = s_ve.n

        w_load_split(0)
        for gen in range(NGEN):
            e_i, g = divmod(gen, NG)
            p = gen % 2
            base = p * 8192
            if gen + 1 < NGEN:
                w_load_split(gen + 1)
            b2_ld = b2_lds[e_i]
            for tt in range(NFT):
                gp = (gen * NFT + tt) % 2
                for fc in range(2):
                    pbk = fc % 2
                    def h1mm(e, g=g, tt=tt, fc=fc, pbk=pbk, base=base):
                        ins = None
                        cnt = 0
                        for c in range(FC):
                            w1h = Wb16[:, base + (c * GDFF + fc * 128):
                                       base + (c * GDFF + fc * 128) + 128]
                            w1l = Wb16[:, base + 2048 + (c * GDFF + fc * 128):
                                       base + 2048 + (c * GDFF + fc * 128) + 128]
                            xh = xh3(c, tt * FTT, (tt + 1) * FTT)
                            xl = xl3(c, tt * FTT, (tt + 1) * FTT)
                            for lhsT, rhs in ((w1h, xh), (w1l, xh), (w1h, xl)):
                                ins = e.matmul(pb[pbk][:, 0:FTT], lhsT, rhs,
                                               start=(cnt == 0),
                                               stop=(cnt == 3 * FC - 1))
                                cnt += 1
                        return ins
                    op("tensor", h1mm, inc=s_mm,
                       waits=[(s_sc, wsplit_sc[p]), (s_ve, wsplit_ve[p]),
                              ((s_ve, pb_h_evac[pbk]) if pb_h_evac[pbk] else None),
                              ((s_mm, h1_used[gp]) if fc == 0 and h1_used[gp] else None)])
                    bidx = e_i * NCH + g * 2 + fc
                    op("vector", lambda e, fc=fc, bidx=bidx, pbk=pbk: e.tensor_scalar(
                        dt_[:, (fc % 2) * 512:(fc % 2 + 1) * 512], pb[pbk][:, 0:FTT],
                        b1_sb[:, bidx:bidx + 1], 0.0, AL.add, AL.max),
                       waits=[(s_mm, s_mm.n)], inc=s_ve)
                    pb_h_evac[pbk] = s_ve.n
                    op("scalar", lambda e, fc=fc, gp=gp: e.copy(
                        h1g16[:, gp * 2048 + fc * 512:gp * 2048 + (fc + 1) * 512],
                        dt_[:, (fc % 2) * 512:(fc % 2 + 1) * 512]),
                       waits=[(s_ve, s_ve.n),
                              ((s_mm, h1_used[gp]) if fc == 0 and h1_used[gp] else None)],
                       inc=s_sc)
                    op("vector", lambda e, fc=fc, gp=gp: e.tensor_tensor(
                        h1g16[:, gp * 2048 + 1024 + fc * 512:
                              gp * 2048 + 1024 + (fc + 1) * 512],
                        dt_[:, (fc % 2) * 512:(fc % 2 + 1) * 512],
                        h1g16[:, gp * 2048 + fc * 512:gp * 2048 + (fc + 1) * 512],
                        AL.subtract),
                       waits=[(s_sc, s_sc.n)], inc=s_ve)
                h1_all_sc = s_sc.n
                h1_all_ve = s_ve.n
                for ts in range(FTT // 128):
                    for jd in range(2):
                        pbk = 4 + (ts * 2 + jd) % 2
                        def ymm(e, g=g, ts=ts, jd=jd, pbk=pbk, base=base, gp=gp,
                                e_i=e_i):
                            ins = None
                            cnt = 0
                            nmm = 6 + (1 if g == 0 else 0)
                            for fc in range(2):
                                h1h = h1g16[:, gp * 2048 + fc * 512 + ts * 128:
                                            gp * 2048 + fc * 512 + (ts + 1) * 128]
                                h1l = h1g16[:, gp * 2048 + 1024 + fc * 512 + ts * 128:
                                            gp * 2048 + 1024 + fc * 512 + (ts + 1) * 128]
                                w2h = Wb16[:, base + 4096 + fc * D + jd * 512:
                                           base + 4096 + fc * D + (jd + 1) * 512]
                                w2l = Wb16[:, base + 6144 + fc * D + jd * 512:
                                           base + 6144 + fc * D + (jd + 1) * 512]
                                for lhsT, rhs in ((h1h, w2h), (h1l, w2h), (h1h, w2l)):
                                    ins = e.matmul(pb[pbk][:], lhsT, rhs,
                                                   start=(cnt == 0),
                                                   stop=(cnt == nmm - 1))
                                    cnt += 1
                            if g == 0:
                                ins = e.matmul(
                                    pb[pbk][:], ones1[0:1, :],
                                    brow[0:1, jd * 512:(jd + 1) * 512],
                                    start=False, stop=True)
                            return ins
                        op("tensor", ymm, inc=s_mm,
                           waits=[(s_sc, h1_all_sc), (s_ve, h1_all_ve),
                                  (s_ld, b2_ld),
                                  ((s_ve, pb_y_evac[pbk]) if pb_y_evac[pbk] else None)])
                        tg = tt * (FTT // 128) + ts
                        op("vector", lambda e, e_i=e_i, tg=tg, jd=jd, pbk=pbk:
                           e.scalar_tensor_tensor(
                               yb[:, tg * D + jd * 512:tg * D + (jd + 1) * 512],
                               pb[pbk][:],
                               oh_tm[:, tg * E + e_i:tg * E + e_i + 1],
                               yb[:, tg * D + jd * 512:tg * D + (jd + 1) * 512],
                               AL.mult, AL.add),
                           waits=[(s_mm, s_mm.n)], inc=s_ve)
                        pb_y_evac[pbk] = s_ve.n
                h1_used[gp] = s_mm.n
            wuse_mm[p] = s_mm.n
        for tl in range(NT):
            op("vector", lambda e, tl=tl: e.tensor_scalar(
                yb[:, tl * D:(tl + 1) * D], yb[:, tl * D:(tl + 1) * D],
                gatev[:, tl:tl + 1], None, AL.mult), inc=s_ve)
            op("scalar", lambda e, tl=tl: e.activation(
                yb[:, tl * D:(tl + 1) * D], yb[:, tl * D:(tl + 1) * D], AF.Silu),
               waits=[(s_ve, s_ve.n)], inc=s_sc)
        g_done_sc = s_sc.n
        if debug:
            op("scalar", lambda e: e.dma_start(
                out=dbg["dbg_g"].rearrange("(tl p) j -> p tl j", p=128),
                in_=yb[:].rearrange("p (tl j) -> p tl j", j=D)),
               inc=s_sst, k=16, waits=[(s_sc, g_done_sc)])
            op("scalar", lambda e: e.dma_start(
                out=dbg["dbg_oh"].rearrange("(tl p) j -> p tl j", p=128),
                in_=oh_tm[:].rearrange("p (tl j) -> p tl j", j=E)),
               inc=s_sst, k=16)
            op("scalar", lambda e: e.dma_start(
                out=dbg["dbg_gate"].rearrange("(tl p) j -> p tl j", p=128),
                in_=gatev[:].rearrange("p (tl j) -> p tl j", j=1),
                allow_slow_non_contiguous=True),
               inc=s_sst, k=16)

        def wload_fold(dw, fold):
            st_sc = [0, 0, 0, 0]
            st_ve = [0, 0, 0, 0]
            stgs = (xt, qt, s2a, s2b)
            entry_mm = s_mm.n
            entry_sst = s_sst.n
            for c in range(FC):
                i = c % 4
                op("sync", lambda e, c=c, dw=dw: e.dma_start(
                    out=stgs[c % 4][:], in_=dw[c * 128:(c + 1) * 128, :]),
                   inc=s_ld, k=16,
                   waits=[((s_sc, st_sc[i]) if st_sc[i] else None),
                          ((s_ve, st_ve[i]) if st_ve[i] else None),
                          ((s_mm, entry_mm) if c < 4 else None),
                          ((s_sst, entry_sst) if c < 4 and entry_sst else None)])
                wld = s_ld.n
                if fold:
                    for jd in range(2):
                        op("tensor", lambda e, c=c, jd=jd, i=i: e.matmul(
                            pbC[jd][0:1, :], betat[:, c:c + 1],
                            stgs[i][:, jd * 512:(jd + 1) * 512],
                            start=(c == 0), stop=(c == FC - 1)),
                           inc=s_mm, waits=[(s_ld, wld),
                                            ((s_ve, s_ve.n) if c == 0 and jd == 0 else None)])
                    op("vector", lambda e, c=c, i=i: e.tensor_scalar(
                        stgs[i][:], stgs[i][:], gammat[:, c:c + 1], None, AL.mult),
                       waits=[(s_mm, s_mm.n)], inc=s_ve)
                op("scalar", lambda e, c=c, i=i: e.copy(
                    wph(c, 0, 1024), stgs[i][:]),
                   waits=[((s_ve, s_ve.n) if fold else (s_ld, wld)),
                          ((s_mm, entry_mm) if c < 2 else None)], inc=s_sc)
                st_sc[i] = s_sc.n
                op("vector", lambda e, c=c, i=i: e.tensor_tensor(
                    wpl(c, 0, 1024), stgs[i][:], wph(c, 0, 1024), AL.subtract),
                   waits=[(s_sc, s_sc.n)], inc=s_ve)
                st_ve[i] = s_ve.n
            if fold:
                for jd in range(2):
                    op("vector", lambda e, jd=jd: e.tensor_copy(
                        brow[0:1, jd * 512:(jd + 1) * 512], pbC[jd][0:1, :]),
                       waits=[(s_mm, s_mm.n)], inc=s_ve)
            return s_sc.n, s_ve.n

        def proj_phase(dw, fold, grep, g_tm, scratch_dram, do_norm, tag):
            wsc, wve = wload_fold(dw, fold)
            s2b16 = s2b[:].bitcast(BF16)
            for tl in range(NT):
                if tag != "v":
                    op("sync", lambda e, tl=tl: e.dma_start(
                        out=xt[:], in_=dx[tl * 128:(tl + 1) * 128, :]),
                       inc=s_ld, k=16,
                       waits=[(s_ve, s_ve.n),
                              ((s_sst, s_sst.n) if s_sst.n else None)])
                    ln_tile(xt[:], qt[:], ld_wait=(s_ld, s_ld.n))
                    ln_tile(qt[:], qt[:])
                    def mk_ev_a(half, pbk):
                        op("vector", lambda e, half=half, pbk=pbk: e.tensor_copy(
                            s2a[:, half * 512:(half + 1) * 512], pbk[:]),
                           waits=[(s_mm, s_mm.n)], inc=s_ve)
                    transpose8(qt[:], mk_ev_a)
                    drain("vector")
                    op("vector", lambda e: e.tensor_copy(s2b16[:, 0:1024], s2a[:]),
                       waits=[((s_sst, s_sst.n) if s_sst.n else None)])
                    drain("vector")
                    op("vector", lambda e: e.tensor_tensor(
                        s2b16[:, 1024:2048], s2a[:], s2b16[:, 0:1024], AL.subtract),
                       inc=s_ve)
                    act16 = s2b16
                else:
                    act16 = None
                for jd in range(2):
                    def pmm(e, tl=tl, jd=jd, act16=act16, fold=fold):
                        ins = None
                        cnt = 0
                        nmm = 3 * FC
                        for c in range(FC):
                            if act16 is None:
                                ah = xh3(c, tl * 128, (tl + 1) * 128)
                                al = xl3(c, tl * 128, (tl + 1) * 128)
                            else:
                                ah = act16[:, c * 128:(c + 1) * 128]
                                al = act16[:, 1024 + c * 128:1024 + (c + 1) * 128]
                            wh = wph(c, jd * 512, (jd + 1) * 512)
                            wl_ = wpl(c, jd * 512, (jd + 1) * 512)
                            for lhsT, rhs in ((ah, wh), (al, wh), (ah, wl_)):
                                ins = e.matmul(pb[jd][:], lhsT, rhs,
                                               start=(cnt == 0),
                                               stop=(not fold and cnt == nmm - 1))
                                cnt += 1
                        if fold:
                            ins = e.matmul(pb[jd][:], ones1[0:1, :],
                                           brow[0:1, jd * 512:(jd + 1) * 512],
                                           start=False, stop=True)
                        return ins
                    op("tensor", pmm, inc=s_mm,
                       waits=[(s_sc, wsc), (s_ve, s_ve.n)])
                    if tag == "v":
                        for hh in range(jd * 8, (jd + 1) * 8):
                            op("vector", lambda e, tl=tl, jd=jd, hh=hh: e.tensor_scalar(
                                qt[:, hh * DH:(hh + 1) * DH],
                                pb[jd][:, (hh - jd * 8) * DH:(hh - jd * 8 + 1) * DH],
                                gk_tm[:, tl * H + hh:tl * H + hh + 1], 0.5,
                                AL.mult, AL.mult),
                               waits=([(s_mm, s_mm.n), (s_sst, s_sst.n)]
                                      if hh == jd * 8 else []),
                               inc=(s_ve if hh == (jd + 1) * 8 - 1 else None))
                    else:
                        op("vector", lambda e, jd=jd: e.tensor_copy(
                            qt[:, jd * 512:(jd + 1) * 512], pb[jd][:]),
                           waits=[(s_mm, s_mm.n),
                                  ((s_sst, s_sst.n) if s_sst.n else None)],
                           inc=s_ve)
                tile_ready = s_ve.n
                if do_norm:
                    op("scalar", lambda e: e.activation(sqt, qt[:], AF.Square),
                       waits=[(s_ve, tile_ready)], inc=s_sc)
                    op("vector", lambda e: e.reduce_sum(
                        nhead[:, 0:H], sqt.rearrange("p (h d) -> p h d", d=DH),
                        axis=AX.X), waits=[(s_sc, s_sc.n)], inc=s_ve)
                    op("scalar", lambda e: e.activation(nhead[:, H:2 * H],
                                                        nhead[:, 0:H], AF.Sqrt),
                       waits=[(s_ve, s_ve.n)], inc=s_sc)
                    op("vector", lambda e: e.tensor_scalar_max(
                        nhead[:, 2 * H:3 * H], nhead[:, H:2 * H], NORM_EPS),
                       waits=[(s_sc, s_sc.n)])
                    drain("vector")
                    op("vector", lambda e: e.reciprocal(nhead[:, 3 * H:4 * H],
                                                        nhead[:, 2 * H:3 * H]))
                    drain("vector")
                    for h in range(H):
                        op("vector", lambda e, h=h: e.tensor_scalar(
                            qt[:, h * DH:(h + 1) * DH], qt[:, h * DH:(h + 1) * DH],
                            nhead[:, 3 * H + h:3 * H + h + 1], None, AL.mult))
                    drain("vector")
                    for h in range(H):
                        op("vector", lambda e, h=h, grep=grep: e.scalar_tensor_tensor(
                            h1g[:, 0:DH], qt[:, h * DH:(h + 1) * DH], 1.0, grep,
                            AL.mult, AL.mult, accum_out=nhead[:, h:h + 1]))
                    drain("vector")
                    op("vector", lambda e: e.engine_nop(), inc=s_ve)
                    op("scalar", lambda e: e.activation(nhead[:, H:2 * H],
                                                        nhead[:, 0:H], AF.Sigmoid),
                       waits=[(s_ve, s_ve.n)], inc=s_sc)
                    op("vector", lambda e, g_tm=g_tm, tl=tl: e.tensor_scalar(
                        g_tm[:, tl * H:(tl + 1) * H], nhead[:, H:2 * H], 0.75, 0.25,
                        AL.mult, AL.add),
                       waits=[(s_sc, s_sc.n)], inc=s_ve)
                    def mk_ev_b(half, pbk):
                        op("vector", lambda e, half=half, pbk=pbk: e.tensor_copy(
                            s2b[:, half * 512:(half + 1) * 512], pbk[:]),
                           waits=[(s_mm, s_mm.n),
                                  ((s_sst, s_sst.n) if s_sst.n else None)],
                           inc=s_ve)
                    transpose8(qt[:], mk_ev_b)
                    xt16 = xt[:].bitcast(BF16)
                    drain("vector")
                    op("vector", lambda e, xt16=xt16: e.tensor_copy(
                        xt16[:, 0:1024], s2b[:]), waits=[(s_ve, s_ve.n)])
                    drain("vector")
                    op("vector", lambda e, xt16=xt16: e.tensor_tensor(
                        xt16[:, 1024:2048], s2b[:], xt16[:, 0:1024], AL.subtract),
                       inc=s_ve)
                    sdh, sdl = scratch_dram
                    op("scalar", lambda e, tl=tl, sd=sdh, xt16=xt16: e.dma_start(
                        out=sd[:, :, tl * 128:(tl + 1) * 128].rearrange(
                            "c p t -> p c t"),
                        in_=xt16[:, 0:1024].rearrange("p (c t) -> p c t", t=128)),
                       waits=[(s_ve, s_ve.n)], inc=s_sst, k=16)
                    op("scalar", lambda e, tl=tl, sd=sdl, xt16=xt16: e.dma_start(
                        out=sd[:, :, tl * 128:(tl + 1) * 128].rearrange(
                            "c p t -> p c t"),
                        in_=xt16[:, 1024:2048].rearrange("p (c t) -> p c t", t=128)),
                       inc=s_sst, k=16)
                    if debug:
                        dk = "dbg_q" if tag == "q" else "dbg_k"
                        op("scalar", lambda e, tl=tl, dk=dk: e.dma_start(
                            out=dbg[dk][tl * 128:(tl + 1) * 128, :], in_=qt[:]),
                           inc=s_sst, k=16)
                else:
                    xt16 = xt[:].bitcast(BF16)
                    drain("vector")
                    op("vector", lambda e, xt16=xt16: e.tensor_copy(
                        xt16[:, 0:1024], qt[:]),
                       waits=[(s_ve, tile_ready),
                              ((s_sst, s_sst.n) if s_sst.n else None)])
                    drain("vector")
                    op("vector", lambda e, xt16=xt16: e.tensor_tensor(
                        xt16[:, 1024:2048], qt[:], xt16[:, 0:1024], AL.subtract),
                       inc=s_ve)
                    op("scalar", lambda e, tl=tl, xt16=xt16: e.dma_start(
                        out=dvg_h[tl * 128:(tl + 1) * 128, :], in_=xt16[:, 0:1024]),
                       waits=[(s_ve, s_ve.n)], inc=s_sst, k=16)
                    op("scalar", lambda e, tl=tl, xt16=xt16: e.dma_start(
                        out=dvg_l[tl * 128:(tl + 1) * 128, :], in_=xt16[:, 1024:2048]),
                       inc=s_sst, k=16)
                    for jd in range(2):
                        op("tensor", lambda e, tl=tl, jd=jd: e.matmul(
                            pb[2 + jd][0:1, :], onesc[:, 0:1],
                            qt[:, jd * 512:(jd + 1) * 512],
                            start=(tl == 0), stop=(tl == NT - 1)),
                           inc=s_mm, waits=[(s_ve, tile_ready)])
                    if tl == NT - 1:
                        for jd in range(2):
                            op("vector", lambda e, jd=jd: e.tensor_copy(
                                csum[0:1, jd * 512:(jd + 1) * 512],
                                pb[2 + jd][0:1, :]),
                               waits=[(s_mm, s_mm.n)], inc=s_ve)

        proj_phase(dwk, True, gkrep[:], gk_tm, (dkn_h, dkn_l), True, "k")
        proj_phase(dwv, False, None, None, None, False, "v")
        proj_phase(dwq, True, gqrep[:], gq_tm, (dqn_h, dqn_l), True, "q")
        if debug:
            op("scalar", lambda e: e.dma_start(out=dbg["dbg_stage"], in_=xt[:]),
               waits=[(s_ve, s_ve.n), (s_sst, s_sst.n)], inc=s_sst, k=16)
            op("sync", lambda e: e.dma_start(
                out=dbg["dbg_rd"].bitcast(BF16)[:, 0:T], in_=dqn_h[1]),
               waits=[(s_sst, s_sst.n)], inc=s_ld, k=16)
            op("sync", lambda e: e.dma_start(
                out=dbg["dbg_rd"].bitcast(BF16)[:, 2 * T:3 * T], in_=dqn_l[1]),
               inc=s_ld, k=16)
            op("sync", lambda e: e.dma_start(
                out=dbg["dbg_rd"].bitcast(BF16)[:, T:T + NKC * DH].rearrange(
                    "p (kc d) -> p kc d", d=DH), 
                in_=dvg_h[:, (2 * FC - 2) * DH:(2 * FC - 2) * DH + DH].rearrange(
                    "(kc p) d -> p kc d", p=128)),
               inc=s_ld, k=16)
        if debug:
            op("scalar", lambda e: e.dma_start(
                out=dbg["dbg_gq"].rearrange("(tl p) h -> p tl h", p=128),
                in_=gq_tm[:].rearrange("p (tl h) -> p tl h", h=H)),
               waits=[(s_ve, s_ve.n)], inc=s_sst, k=16)
            op("scalar", lambda e: e.dma_start(
                out=dbg["dbg_gk"].rearrange("(tl p) h -> p tl h", p=128),
                in_=gk_tm[:].rearrange("p (tl h) -> p tl h", h=H)),
               inc=s_sst, k=16)
            op("scalar", lambda e: e.dma_start(out=dbg["dbg_csum"], in_=csum[:]),
               inc=s_sst, k=16)

        ve_at_D = s_ve.n
        sst_at_D = s_sst.n
        dt16 = dt_[:].bitcast(BF16)
        sq16x = h1g[:].bitcast(BF16)
        gqbc = xt[:, 0:512]
        gqrowA = xt[0:1, 512:1024]
        gqrowB = s2b[0:1, 0:512]
        tfs = (qt[:], s2a[:])
        sab_t = (pbA, pbB)
        sab_sc = [0, 0]
        sab_ve = [0, 0]
        fused_evac_ve = ve_at_D
        gq_ve = ve_at_D
        grow_ve = ve_at_D
        fusedT3 = fusedT[:].rearrange("p (c t) -> p c t", t=T)
        for hp in range(FC):
            for (dst, srcd) in ((qnh_hi, dqn_h), (qnh_lo, dqn_l),
                                (knh_hi, dkn_h), (knh_lo, dkn_l)):
                op("sync", lambda e, dst=dst, srcd=srcd, hp=hp: e.dma_start(
                    out=dst, in_=srcd[hp]),
                   inc=s_ld, k=16,
                   waits=([(s_mm, s_mm.n), (s_sst, s_sst.n)] if dst is qnh_hi else []))
            for (dst, srcd) in ((vgh_hi, dvg_h), (vgh_lo, dvg_l)):
                for h2 in (0, 1):
                    op("sync", lambda e, dst=dst, srcd=srcd, hp=hp, h2=h2: e.dma_start(
                        out=dst[:, h2 * NKC * DH:(h2 + 1) * NKC * DH].rearrange(
                            "p (kc d) -> p kc d", d=DH),
                        in_=srcd[:, (2 * hp + h2) * DH:(2 * hp + h2 + 1) * DH].rearrange(
                            "(kc p) d -> p kc d", p=128)),
                       inc=s_ld, k=16)
            hd_ld = s_ld.n
            for qc in range(NQC):
                ntl = QCW // 128
                def gqtA(e, hp=hp, qc=qc, ntl=ntl):
                    ins = None
                    for i in range(ntl):
                        tl = qc * ntl + i
                        ins = e.transpose(pbC[2][0:1, i * 128:(i + 1) * 128],
                                          gq_tm[:, tl * H + 2 * hp:tl * H + 2 * hp + 1],
                                          eyet[:])
                    return ins
                def gqtB(e, hp=hp, qc=qc, ntl=ntl):
                    ins = None
                    for i in range(ntl):
                        tl = qc * ntl + i
                        ins = e.transpose(pbC[3][0:1, i * 128:(i + 1) * 128],
                                          gq_tm[:, tl * H + 2 * hp + 1:tl * H + 2 * hp + 2],
                                          eyet[:])
                    return ins
                op("tensor", gqtA, waits=[(s_ve, grow_ve)])
                op("tensor", gqtB, inc=s_mm)
                op("vector", lambda e: e.tensor_copy(gqrowA, pbC[2][0:1, 0:QCW]),
                   waits=[(s_mm, s_mm.n),
                          ((s_sst, sst_at_D) if hp == 0 and qc == 0 else None)])
                op("vector", lambda e: e.tensor_copy(gqrowB, pbC[3][0:1, 0:QCW]),
                   inc=s_ve)
                grow_ve = s_ve.n
                def gbc(e):
                    e.matmul(pbC[1][0:64, 0:QCW], ones1[0:1, 0:64], gqrowA,
                             start=True, stop=True, tile_position=(0, 0))
                    return e.matmul(pbC[1][64:128, 0:QCW], ones1[0:1, 0:64], gqrowB,
                                    start=True, stop=True, tile_position=(0, 64))
                op("tensor", gbc, inc=s_mm, waits=[(s_ve, grow_ve), (s_ve, gq_ve)])
                op("vector", lambda e: e.tensor_copy(gqbc, pbC[1][:, 0:QCW]),
                   waits=[(s_mm, s_mm.n)], inc=s_ve)
                gq_ve = s_ve.n
                op("tensor", lambda e, hp=hp: e.matmul(
                    pbC[0][:, 0:QCW], csum[0:1, hp * 128:(hp + 1) * 128],
                    onesr[0:1, 0:QCW], start=True, stop=False),
                   inc=s_mm, waits=[(s_ve, fused_evac_ve), (s_ld, hd_ld)])
                for kc in range(NKC):
                    sl = kc % 2
                    sab = sab_t[sl]
                    def smm(e, kc=kc, qc=qc, sab=sab):
                        ins = None
                        ops = ((qnh_hi, knh_hi), (qnh_lo, knh_hi), (qnh_hi, knh_lo))
                        for i, (qb, kb) in enumerate(ops):
                            e.matmul(sab[:, 0:QCW],
                                     kb[0:64, kc * 128:(kc + 1) * 128],
                                     qb[0:64, qc * QCW:(qc + 1) * QCW],
                                     start=(i == 0), stop=(i == 2),
                                     tile_position=(0, 0))
                            ins = e.matmul(sab[:, 512:512 + QCW],
                                           kb[64:128, kc * 128:(kc + 1) * 128],
                                           qb[64:128, qc * QCW:(qc + 1) * QCW],
                                           start=(i == 0), stop=(i == 2),
                                           tile_position=(64, 0))
                        return ins
                    op("tensor", smm, inc=s_mm,
                       waits=[(s_ld, hd_ld),
                              ((s_sc, sab_sc[sl]) if sab_sc[sl] else (s_ve, ve_at_D))])
                    smm_n = s_mm.n
                    op("scalar", lambda e, sab=sab, sl=sl: e.activation(
                        tfs[sl], sab[:, 0:1024], AF.Tanh, bias=CM405, scale=C15),
                       waits=[(s_mm, smm_n)], inc=s_sc)
                    sab_sc[sl] = s_sc.n
                    op("vector", lambda e, sl=sl: e.tensor_copy(
                        dt16[:, sl * 1024:(sl + 1) * 1024], tfs[sl]),
                       waits=[(s_sc, s_sc.n), (s_mm, smm_n)])
                    drain("vector")
                    op("vector", lambda e, sl=sl: e.tensor_tensor(
                        sq16x[:, 2048 + sl * 1024:2048 + (sl + 1) * 1024], tfs[sl],
                        dt16[:, sl * 1024:(sl + 1) * 1024], AL.subtract),
                       inc=s_ve)
                    sab_ve[sl] = s_ve.n
                    def fmm(e, kc=kc, sl=sl):
                        ins = None
                        ops = ((vgh_hi, dt16), (vgh_lo, dt16), (vgh_hi, None))
                        for i, (vb, tb) in enumerate(ops):
                            toff = sl * 1024 if tb is not None else 2048 + sl * 1024
                            tbb = tb if tb is not None else sq16x
                            e.matmul(pbC[0][0:64, 0:QCW],
                                     vb[:, kc * DH:(kc + 1) * DH],
                                     tbb[:, toff:toff + QCW],
                                     start=False, stop=(kc == NKC - 1 and i == 2),
                                     tile_position=(0, 0))
                            ins = e.matmul(pbC[0][64:128, 0:QCW],
                                           vb[:, NKC * DH + kc * DH:
                                              NKC * DH + (kc + 1) * DH],
                                           tbb[:, toff + 512:toff + 512 + QCW],
                                           start=False,
                                           stop=(kc == NKC - 1 and i == 2),
                                           tile_position=(0, 64))
                        return ins
                    op("tensor", fmm, inc=s_mm, waits=[(s_ve, sab_ve[sl])])
                def fev(e, hp=hp, qc=qc):
                    dstv = fusedT3[:, hp, qc * QCW:(qc + 1) * QCW]
                    return e.tensor_tensor(dstv, pbC[0][:, 0:QCW], gqbc, AL.mult)
                op("vector", fev, waits=[(s_mm, s_mm.n), (s_ve, gq_ve)], inc=s_ve)
                fused_evac_ve = s_ve.n
        if debug:
            op("scalar", lambda e: e.dma_start(out=dbg["dbg_fused"], in_=fusedT[:]),
               waits=[(s_ve, s_ve.n)], inc=s_sst, k=16)
            op("scalar", lambda e: e.dma_start(out=dbg["dbg_gqbc"], in_=gqbc),
               inc=s_sst, k=16)
            op("scalar", lambda e: e.dma_start(out=dbg["dbg_t0"], in_=tfs[0]),
               inc=s_sst, k=16)
            op("scalar", lambda e: e.dma_start(out=dbg["dbg_t1"], in_=tfs[1]),
               inc=s_sst, k=16)
            op("scalar", lambda e: e.dma_start(
                out=dbg["dbg_qnh"].rearrange("p j -> p j").bitcast(BF16)[:, 0:T]
                if False else dbg["dbg_qnh"].bitcast(BF16)[:, 0:T], in_=qnh_hi),
               inc=s_sst, k=16)
            op("scalar", lambda e: e.dma_start(
                out=dbg["dbg_vgh"].bitcast(BF16)[:, 0:2 * NKC * DH], in_=vgh_hi),
               inc=s_sst, k=16)

        def lwo(e):
            dstv = Wbuf[:].rearrange("p (c j) -> p c j", j=D)
            return e.dma_start(out=dstv, in_=dwo.rearrange("(c p) j -> p c j", p=128))
        op("sync", lwo, inc=s_ld, k=16,
           waits=[(s_mm, s_mm.n), ((s_sst, s_sst.n) if s_sst.n else None)])
        op("sync", lambda e: e.dma_start(out=brow[0:1, :], in_=dbo), inc=s_ld, k=16)
        wo_ld = s_ld.n
        sc_at_E = s_sc.n
        pe_evac = {0: 0, 1: 0}
        for tl in range(NT):
            op("sync", lambda e, tl=tl: e.dma_start(
                out=xt[:], in_=dx[tl * 128:(tl + 1) * 128, :]),
               inc=s_ld, k=16, waits=[(s_ve, s_ve.n)])
            x_ld = s_ld.n
            for jd in range(2):
                def omm(e, tl=tl, jd=jd):
                    ins = None
                    for c in range(FC):
                        lhsT = fusedT[:].rearrange("p (c t) -> p c t", t=T)[
                            :, c, tl * 128:(tl + 1) * 128]
                        ins = e.matmul(pb[jd][:], lhsT,
                                       Wbuf[:, c * D + jd * 512:
                                            c * D + (jd + 1) * 512],
                                       start=(c == 0), stop=False)
                    ins = e.matmul(pb[jd][:], ones1[0:1, :],
                                   brow[0:1, jd * 512:(jd + 1) * 512],
                                   start=False, stop=True)
                    return ins
                op("tensor", omm, inc=s_mm,
                   waits=[(s_ld, wo_ld), (s_ve, s_ve.n),
                          ((s_ve, pe_evac[jd]) if pe_evac[jd] else None),
                          ((s_sc, sc_at_E) if tl == 0 else None)])
                op("vector", lambda e, tl=tl, jd=jd: e.scalar_tensor_tensor(
                    s2a[:, jd * 512:(jd + 1) * 512], pb[jd][:], 1.0,
                    yb[:, tl * D + jd * 512:tl * D + (jd + 1) * 512],
                    AL.bypass, AL.mult),
                   waits=[(s_mm, s_mm.n), (s_sc, g_done_sc),
                          ((s_out, s_out.n) if s_out.n else None)])
                drain("vector")
                op("vector", lambda e, jd=jd: e.tensor_tensor(
                    s2a[:, jd * 512:(jd + 1) * 512],
                    s2a[:, jd * 512:(jd + 1) * 512],
                    xt[:, jd * 512:(jd + 1) * 512], AL.add),
                   waits=[(s_ld, x_ld)], inc=s_ve)
                pe_evac[jd] = s_ve.n
            op("scalar", lambda e, tl=tl: e.dma_start(
                out=dout[tl * 128:(tl + 1) * 128, :], in_=s2a[:]),
               waits=[(s_ve, s_ve.n)], inc=s_out, k=16)
        plan["scalar"].append(lambda e, v=s_out.n: e.wait_ge(s_out.sem, v))

        with nc.Block() as block:
            block.sync(lambda e: [f(e) for f in plan["sync"]])
            block.vector(lambda e: [f(e) for f in plan["vector"]])
            block.scalar(lambda e: [f(e) for f in plan["scalar"]])
            block.tensor(lambda e: [f(e) for f in plan["tensor"]])
            block.gpsimd(lambda e: [f(e) for f in plan["gpsimd"]])
    return nc


def make_inputs(xc, wq, wk, wv, g_q, g_k, wo, b_o, gamma, beta, Wr, br,
                W1, b1, W2, b2):
    return {
        "x": np.ascontiguousarray(xc),
        "wq": np.ascontiguousarray(wq),
        "wk": np.ascontiguousarray(wk),
        "wv": np.ascontiguousarray(wv),
        "wo": np.ascontiguousarray(wo),
        "bo": np.ascontiguousarray(b_o.reshape(1, D)),
        "gamma_fm": np.ascontiguousarray(gamma.reshape(FC, 128).T),
        "beta_fm": np.ascontiguousarray(beta.reshape(FC, 128).T),
        "gq_rep": np.ascontiguousarray(np.broadcast_to(g_q, (128, DH))),
        "gk_rep": np.ascontiguousarray(np.broadcast_to(g_k, (128, DH))),
        "wr_sb": np.ascontiguousarray(
            Wr.reshape(FC, 128, E).transpose(1, 0, 2).reshape(128, FC * E)),
        "br": np.ascontiguousarray(br.reshape(1, E)),
        "W1": np.ascontiguousarray(W1),
        "b1_fm": np.ascontiguousarray(
            b1.reshape(E, DFF // 128, 128).transpose(2, 0, 1).reshape(128, -1)),
        "W2": np.ascontiguousarray(W2),
        "b2": np.ascontiguousarray(b2.reshape(1, E * D)),
        "eye": np.eye(128, dtype=np.float32),
    }


_NC_CACHE = {}


def kernel(L0_attn_x, L0_attn_w_q, L0_attn_w_k, L0_attn_w_v, L0_attn_g_q,
           L0_attn_g_k, L0_attn_w_o, L0_attn_b_o, L0_attn_gamma, L0_attn_beta,
           L0_ffn_Wr, L0_ffn_br, L0_ffn_W1, L0_ffn_b1, L0_ffn_W2, L0_ffn_b2):
    x = np.asarray(L0_attn_x, dtype=np.float32)
    B, T, _ = x.shape
    if (T,) not in _NC_CACHE:
        _NC_CACHE[(T,)] = build(T=T)
    nc = _NC_CACHE[(T,)]
    args = [np.asarray(a, dtype=np.float32) for a in
            (L0_attn_w_q, L0_attn_w_k, L0_attn_w_v, L0_attn_g_q, L0_attn_g_k,
             L0_attn_w_o, L0_attn_b_o, L0_attn_gamma, L0_attn_beta,
             L0_ffn_Wr, L0_ffn_br, L0_ffn_W1, L0_ffn_b1, L0_ffn_W2, L0_ffn_b2)]
    in_maps = [make_inputs(x[c], *args) for c in range(B)]
    res = run_bass_kernel_spmd(nc, in_maps, core_ids=list(range(B)))
    return np.stack([np.asarray(r["out"]) for r in res.results], axis=0)


# version 04
# speedup vs baseline: 1.6406x; 1.2441x over previous
import numpy as np
from contextlib import ExitStack

import concourse.bass as bass
import concourse.mybir as mybir
from concourse.bass_utils import run_bass_kernel_spmd

F32 = mybir.dt.float32
AF = mybir.ActivationFunctionType
AL = mybir.AluOpType
AX = mybir.AxisListType

D = 1024
H = 16
DH = 64
E = 8
DFF = 4096
FC = D // 128
GDFF = 256
NG = DFF // GDFF
LN_EPS = 1e-5
NORM_EPS = 1e-12


class Ctr:
    def __init__(self, sem):
        self.sem = sem
        self.n = 0


def build(T=2048, debug=False):
    NT = T // 128
    FTT = min(512, T)
    NFT = T // FTT
    NQC = max(1, T // 512)
    QCW = min(512, T)
    NKC = T // 128
    NH1 = GDFF // 128

    nc = bass.Bass("TRN2", target_bir_lowering=False)

    dx = nc.dram_tensor("x", [T, D], F32, kind="ExternalInput").ap()
    dwq = nc.dram_tensor("wq", [D, D], F32, kind="ExternalInput").ap()
    dwk = nc.dram_tensor("wk", [D, D], F32, kind="ExternalInput").ap()
    dwv = nc.dram_tensor("wv", [D, D], F32, kind="ExternalInput").ap()
    dwo = nc.dram_tensor("wo", [D, D], F32, kind="ExternalInput").ap()
    dbo = nc.dram_tensor("bo", [1, D], F32, kind="ExternalInput").ap()
    dgamma = nc.dram_tensor("gamma_fm", [128, FC], F32, kind="ExternalInput").ap()
    dbeta = nc.dram_tensor("beta_fm", [128, FC], F32, kind="ExternalInput").ap()
    dgq = nc.dram_tensor("gq_rep", [128, DH], F32, kind="ExternalInput").ap()
    dgk = nc.dram_tensor("gk_rep", [128, DH], F32, kind="ExternalInput").ap()
    dwr = nc.dram_tensor("wr_sb", [128, FC * E], F32, kind="ExternalInput").ap()
    dbr = nc.dram_tensor("br", [1, E], F32, kind="ExternalInput").ap()
    dW1 = nc.dram_tensor("W1", [E, D, DFF], F32, kind="ExternalInput").ap()
    db1 = nc.dram_tensor("b1_fm", [128, E * (DFF // 128)], F32, kind="ExternalInput").ap()
    dW2 = nc.dram_tensor("W2", [E, DFF, D], F32, kind="ExternalInput").ap()
    db2 = nc.dram_tensor("b2", [1, E * D], F32, kind="ExternalInput").ap()
    deye = nc.dram_tensor("eye", [128, 128], F32, kind="ExternalInput").ap()
    dout = nc.dram_tensor("out", [T, D], F32, kind="ExternalOutput").ap()
    BF16 = mybir.dt.bfloat16
    dqn_h = nc.dram_tensor("qn_sh", [FC, 128, T], BF16).ap()
    dqn_l = nc.dram_tensor("qn_sl", [FC, 128, T], BF16).ap()
    dkn_h = nc.dram_tensor("kn_sh", [FC, 128, T], BF16).ap()
    dkn_l = nc.dram_tensor("kn_sl", [FC, 128, T], BF16).ap()
    dvg_h = nc.dram_tensor("vg_sh", [T, D], BF16).ap()
    dvg_l = nc.dram_tensor("vg_sl", [T, D], BF16).ap()
    dbg = {}
    if debug:
        for name, shp in [("dbg_g", [T, D]), ("dbg_oh", [T, E]), ("dbg_gate", [T, 1]),
                          ("dbg_q", [T, D]), ("dbg_k", [T, D]),
                          ("dbg_gq", [T, H]), ("dbg_gk", [T, H]),
                          ("dbg_fused", [128, FC * T]), ("dbg_csum", [1, D]),
                          ("dbg_xn", [T, D]), ("dbg_gqbc", [128, 512]),
                          ("dbg_stage", [128, 1024]), ("dbg_rd", [128, 2 * T]),
                          ("dbg_praw", [128, 1024]),
                          ("dbg_t0", [128, 1024]), ("dbg_t1", [128, 1024]),
                          ("dbg_qnh", [128, 2048]), ("dbg_vgh", [128, 2048])]:
            dbg[name] = nc.dram_tensor(name, shp, F32, kind="ExternalOutput").ap()

    plan = {k: [] for k in ("sync", "vector", "scalar", "tensor", "gpsimd")}

    def op(eng, fn, inc=None, k=1, waits=()):
        waits = [w for w in waits if w is not None]
        for (c, v) in waits:
            assert v <= c.n, f"wait on future value: {v} > {c.n}"
        if inc is not None:
            inc.n += k
        def run(e, fn=fn, inc=inc, k=k, waits=tuple(waits)):
            for (c, v) in waits:
                if v > 0:
                    e.wait_ge(c.sem, v)
            ins = fn(e)
            if inc is not None:
                ins.then_inc(inc.sem, k)
        plan[eng].append(run)

    def drain(eng):
        plan[eng].append(lambda e: e.drain())

    ctx = ExitStack()
    with ctx:
        sb = lambda n, s: ctx.enter_context(nc.sbuf_tensor(n, s, F32))
        xnT = sb("xnT", [128, FC * T])
        yb = sb("yb", [128, NT * D])
        Wbuf = sb("Wbuf", [128, 8 * D])
        xt = sb("xt", [128, D])
        dt_ = sb("dt_", [128, D])
        qt = sb("qt", [128, D])
        s2a = sb("s2a", [128, D])
        s2b = sb("s2b", [128, D])
        h1g = sb("h1g", [128, 2048])
        sqt = h1g[:, 0:1024]
        stats = sb("stats", [128, 32])
        nhead = sb("nhead", [128, 4 * H])
        gq_tm = sb("gq_tm", [128, NT * H])
        gk_tm = sb("gk_tm", [128, NT * H])
        oh_tm = sb("oh_tm", [128, NT * E])
        gatev = sb("gatev", [128, NT])
        gammat = sb("gammat", [128, FC])
        betat = sb("betat", [128, FC])
        gqrep = sb("gqrep", [128, DH])
        gkrep = sb("gkrep", [128, DH])
        wr_sb = sb("wr_sbuf", [128, FC * E])
        wrhl = sb("wrhl", [128, FC * E])
        b1_sb = sb("b1_sbuf", [128, E * (DFF // 128)])
        eyet = sb("eyet", [128, 128])
        cst = sb("cst", [128, 8])
        onesr = sb("onesr", [1, 512])
        ones1 = sb("ones128", [1, 128])
        onesc = sb("onescol", [128, 1])
        brow = sb("brow", [1, D])
        br_sb = sb("br_sbuf", [1, E])
        csum = sb("csum_row", [1, D])
        Wb16 = Wbuf[:].bitcast(BF16)
        xnT16 = xnT[:].bitcast(BF16)
        XLO = FC * T
        def xh3(c, a, b):
            return xnT16[:, c * T + a:c * T + b]
        def xl3(c, a, b):
            return xnT16[:, XLO + c * T + a:XLO + c * T + b]
        def wph(c, a, b):
            return Wb16[:, c * D + a:c * D + b]
        def wpl(c, a, b):
            return Wb16[:, 8192 + c * D + a:8192 + c * D + b]
        qnh_hi = Wb16[:, 0:T]
        qnh_lo = Wb16[:, T:2 * T]
        knh_hi = Wb16[:, 2 * T:3 * T]
        knh_lo = Wb16[:, 3 * T:4 * T]
        VGO = 4 * T
        vgh_hi = Wb16[:, VGO:VGO + 2 * NKC * DH]
        vgh_lo = Wb16[:, VGO + 2 * NKC * DH:VGO + 4 * NKC * DH]
        pbA = ctx.enter_context(nc.psum_tensor("pbA", [128, 1024], F32))
        pbB = ctx.enter_context(nc.psum_tensor("pbB", [128, 1024], F32))
        pbC = [ctx.enter_context(nc.psum_tensor(f"pbC{i}", [128, 512], F32))
               for i in range(4)]
        pb = {0: pbA[:, 0:512], 1: pbA[:, 512:1024],
              4: pbB[:, 0:512], 5: pbB[:, 512:1024],
              2: pbC[0][:], 3: pbC[1][:], 6: pbC[2][:], 7: pbC[3][:]}

        s_ld = Ctr(ctx.enter_context(nc.semaphore("s_ld")))
        s_mm = Ctr(ctx.enter_context(nc.semaphore("s_mm")))
        s_ve = Ctr(ctx.enter_context(nc.semaphore("s_ve")))
        s_sc = Ctr(ctx.enter_context(nc.semaphore("s_sc")))
        s_sst = Ctr(ctx.enter_context(nc.semaphore("s_sst")))
        s_out = Ctr(ctx.enter_context(nc.semaphore("s_out")))

        CI1024 = cst[:, 0:1]
        CEPS = cst[:, 1:2]
        C15 = cst[:, 2:3]
        CM405 = cst[:, 3:4]
        fusedT = xnT

        xnT3 = xnT[:].rearrange("p (c t) -> p c t", t=T)

        def ldi(dst, src):
            op("sync", lambda e, d=dst, s=src: e.dma_start(out=d, in_=s),
               inc=s_ld, k=16)

        ldi(gammat[:], dgamma)
        ldi(betat[:], dbeta)
        ldi(gqrep[:], dgq)
        ldi(gkrep[:], dgk)
        ldi(wr_sb[:], dwr)
        ldi(b1_sb[:], db1)
        ldi(eyet[:], deye)
        ldi(br_sb[:], dbr)
        for aptgt, val in [(cst[:, 0:1], 1.0 / D), (cst[:, 1:2], LN_EPS),
                           (cst[:, 2:3], 15.0), (cst[:, 3:4], -4.05),
                           (onesr[:], 1.0), (ones1[:], 1.0), (onesc[:], 1.0)]:
            op("vector", lambda e, a=aptgt, v=val: e.memset(a, v), inc=s_ve)
        wr16 = wrhl[:].bitcast(BF16)
        op("vector", lambda e: e.tensor_copy(wr16[:, 0:FC * E], wr_sb[:]),
           waits=[(s_ld, s_ld.n)])
        drain("vector")
        op("vector", lambda e: e.tensor_tensor(wr16[:, FC * E:2 * FC * E], wr_sb[:],
                                               wr16[:, 0:FC * E], AL.subtract))
        op("vector", lambda e: e.memset(yb[:], 0.0), inc=s_ve)
        setup_ve = s_ve.n

        def ln_tile(src, out, ld_wait=None):
            drain("vector")
            op("vector", lambda e: e.reduce_sum(stats[:, 0:1], src, axis=AX.X),
               waits=[ld_wait] if ld_wait else [])
            drain("vector")
            op("vector", lambda e: e.tensor_scalar_mul(stats[:, 1:2], stats[:, 0:1], 1.0 / D))
            drain("vector")
            op("vector", lambda e: e.tensor_scalar(dt_[:], src, stats[:, 1:2],
                                                   None, AL.subtract), inc=s_ve)
            op("scalar", lambda e: e.activation(sqt, dt_[:], AF.Square,
                                                accum_out=stats[:, 2:3]),
               waits=[(s_ve, s_ve.n)], inc=s_sc)
            drain("scalar")
            op("scalar", lambda e: e.activation(stats[:, 3:4], stats[:, 2:3],
                                                AF.Sqrt, scale=CI1024, bias=CEPS),
               inc=s_sc)
            op("vector", lambda e: e.reciprocal(stats[:, 4:5], stats[:, 3:4]),
               waits=[(s_sc, s_sc.n)])
            drain("vector")
            op("vector", lambda e, out=out: e.tensor_scalar(out, dt_[:], stats[:, 4:5],
                                                            None, AL.mult), inc=s_ve)

        tr_evac = {6: 0, 7: 0}

        def transpose8(src, evac_fn_for_half):
            src_ready = s_ve.n
            for half in (0, 1):
                pbi = 6 + half
                def tmf(e, half=half, pbi=pbi, src=src):
                    ins = None
                    for c in range(4):
                        cc = half * 4 + c
                        ins = e.transpose(pb[pbi][:, c * 128:(c + 1) * 128],
                                          src[:, cc * 128:(cc + 1) * 128], eyet[:])
                    return ins
                op("tensor", tmf, inc=s_mm,
                   waits=[(s_ve, src_ready),
                          ((s_ve, tr_evac[pbi]) if tr_evac[pbi] else None)])
                evac_fn_for_half(half, pb[pbi])
                tr_evac[pbi] = s_ve.n

        for tl in range(NT):
            op("sync", lambda e, tl=tl: e.dma_start(out=xt[:],
                                                    in_=dx[tl * 128:(tl + 1) * 128, :]),
               inc=s_ld, k=16, waits=[(s_ve, s_ve.n if tl else setup_ve)])
            ln_tile(xt[:], qt[:], ld_wait=(s_ld, s_ld.n))
            def mk_evac(half, pbk, tl=tl):
                hi3 = xnT16[:, 0:FC * T].rearrange("p (c t) -> p c t", t=T)[
                    :, half * 4:(half + 1) * 4, tl * 128:(tl + 1) * 128]
                lo3 = xnT16[:, XLO:XLO + FC * T].rearrange("p (c t) -> p c t", t=T)[
                    :, half * 4:(half + 1) * 4, tl * 128:(tl + 1) * 128]
                p3 = pbk[:].rearrange("p (c t) -> p c t", t=128)
                op("vector", lambda e, hi3=hi3, p3=p3: e.tensor_copy(hi3, p3),
                   waits=[(s_mm, s_mm.n)])
                drain("vector")
                op("vector", lambda e, hi3=hi3, lo3=lo3, p3=p3: e.tensor_tensor(
                    lo3, p3, hi3, AL.subtract), inc=s_ve)
            transpose8(qt[:], mk_evac)
        xnT_ready = s_ve.n
        if debug:
            op("scalar", lambda e: e.dma_start(
                out=dbg["dbg_xn"].rearrange("(a p) j -> p a j", p=128),
                in_=xnT[:].rearrange("p (a j) -> p a j", j=D)),
               waits=[(s_ve, s_ve.n)], inc=s_sst, k=16)

        for tl in range(NT):
            def rmm(e, tl=tl):
                ins = None
                first = True
                for c in range(FC):
                    xh = xh3(c, tl * 128, (tl + 1) * 128)
                    xl = xl3(c, tl * 128, (tl + 1) * 128)
                    wh = wr16[:, c * E:(c + 1) * E]
                    wl = wr16[:, FC * E + c * E:FC * E + (c + 1) * E]
                    for lhsT, rhs in ((xh, wh), (xl, wh), (xh, wl)):
                        ins = e.matmul(pb[5][:, 0:E], lhsT, rhs,
                                       start=first, stop=False)
                        first = False
                ins = e.matmul(pb[5][:, 0:E], ones1[0:1, :], br_sb[0:1, :],
                               start=False, stop=True)
                return ins
            op("tensor", rmm, inc=s_mm, waits=[(s_ve, s_ve.n), (s_ld, s_ld.n)])
            op("vector", lambda e: e.tensor_copy(stats[:, 8:8 + E], pb[5][:, 0:E]),
               waits=[(s_mm, s_mm.n), (s_sc, s_sc.n)], inc=s_ve)
            drain("vector")
            op("vector", lambda e: e.reduce_max(stats[:, 16:17], stats[:, 8:8 + E],
                                                axis=AX.X, negate=True))
            drain("vector")
            op("vector", lambda e: e.tensor_scalar_mul(stats[:, 17:18],
                                                       stats[:, 16:17], -1.0))
            drain("vector")
            op("vector", lambda e, tl=tl: e.tensor_scalar(oh_tm[:, tl * E:(tl + 1) * E],
                                                          stats[:, 8:8 + E],
                                                          stats[:, 17:18], None,
                                                          AL.is_equal), inc=s_ve)
            op("scalar", lambda e: e.activation(stats[:, 18:18 + E], stats[:, 8:8 + E],
                                                AF.Exp, bias=stats[:, 16:17],
                                                accum_out=stats[:, 26:27]),
               waits=[(s_ve, s_ve.n)], inc=s_sc)
            op("vector", lambda e, tl=tl: e.reciprocal(gatev[:, tl:tl + 1],
                                                       stats[:, 26:27]),
               waits=[(s_sc, s_sc.n)], inc=s_ve)
        router_ve = s_ve.n

        NCH = DFF // 128
        h1g16 = h1g[:].bitcast(BF16)
        NGEN = E * NG
        wsplit_sc = [0, 0]
        wsplit_ve = [0, 0]
        wuse_mm = [0, 0]
        stage_sc = [0, 0, 0, 0]
        stage_ve = [0, 0, 0, 0]
        h1_used = [0, 0]
        pb_h_evac = {0: 0, 1: 0}
        pb_y_evac = {4: 0, 5: router_ve}
        b2_lds = {}

        def w_load_split(gen):
            e_i, g = divmod(gen, NG)
            p = gen % 2
            base = p * 8192
            stgs = (xt, qt, s2a, s2b)
            if e_i not in b2_lds:
                op("sync", lambda e, e_i=e_i: e.dma_start(
                    out=brow[0:1, :], in_=db2[0:1, e_i * D:(e_i + 1) * D]),
                   inc=s_ld, k=16,
                   waits=[(s_mm, wuse_mm[p]) if wuse_mm[p] else None])
                b2_lds[e_i] = s_ld.n
            def l1a(e, e_i=e_i, g=g):
                return e.dma_start(
                    out=xt[:].rearrange("p (c f) -> p c f", f=GDFF),
                    in_=dW1[e_i, 0:512, g * GDFF:(g + 1) * GDFF].rearrange(
                        "(c p) f -> p c f", p=128))
            def l1b(e, e_i=e_i, g=g):
                return e.dma_start(
                    out=qt[:].rearrange("p (c f) -> p c f", f=GDFF),
                    in_=dW1[e_i, 512:1024, g * GDFF:(g + 1) * GDFF].rearrange(
                        "(c p) f -> p c f", p=128))
            def l2a(e, e_i=e_i, g=g):
                return e.dma_start(out=s2a[:],
                                   in_=dW2[e_i, g * GDFF:g * GDFF + 128, :])
            def l2b(e, e_i=e_i, g=g):
                return e.dma_start(out=s2b[:],
                                   in_=dW2[e_i, g * GDFF + 128:g * GDFF + 256, :])
            lds = []
            for i, fn in enumerate((l1a, l1b, l2a, l2b)):
                op("sync", fn, inc=s_ld, k=16,
                   waits=[((s_sc, stage_sc[i]) if stage_sc[i] else None),
                          ((s_ve, stage_ve[i]) if stage_ve[i] else None),
                          ((s_mm, s_mm.n) if gen == 0 else None),
                          ((s_ve, s_ve.n) if gen == 0 else None)])
                lds.append(s_ld.n)
            hidst = (Wb16[:, base + 0:base + 1024],
                     Wb16[:, base + 1024:base + 2048],
                     Wb16[:, base + 4096:base + 5120],
                     Wb16[:, base + 5120:base + 6144])
            lodst = (Wb16[:, base + 2048:base + 3072],
                     Wb16[:, base + 3072:base + 4096],
                     Wb16[:, base + 6144:base + 7168],
                     Wb16[:, base + 7168:base + 8192])
            for i, stg in enumerate(stgs):
                op("scalar", lambda e, d=hidst[i], s=stg: e.copy(d, s[:]),
                   waits=[(s_ld, lds[i]),
                          ((s_mm, wuse_mm[p]) if wuse_mm[p] else None)],
                   inc=s_sc)
                stage_sc[i] = s_sc.n
                op("vector", lambda e, d=lodst[i], s=stg, h=hidst[i]:
                   e.tensor_tensor(d, s[:], h, AL.subtract),
                   waits=[(s_sc, s_sc.n),
                          ((s_mm, wuse_mm[p]) if wuse_mm[p] else None)],
                   inc=s_ve)
                stage_ve[i] = s_ve.n
            wsplit_sc[p] = s_sc.n
            wsplit_ve[p] = s_ve.n

        w_load_split(0)
        pending_ymm = [None]

        def emit_h1_block(gen, tt):
            e_i, g = divmod(gen, NG)
            p = gen % 2
            base = p * 8192
            gp = (gen * NFT + tt) % 2
            for fc in range(2):
                pbk = fc % 2
                def h1mm(e, g=g, tt=tt, fc=fc, pbk=pbk, base=base):
                    ins = None
                    cnt = 0
                    for c in range(FC):
                        w1h = Wb16[:, base + (c * GDFF + fc * 128):
                                   base + (c * GDFF + fc * 128) + 128]
                        w1l = Wb16[:, base + 2048 + (c * GDFF + fc * 128):
                                   base + 2048 + (c * GDFF + fc * 128) + 128]
                        xh = xh3(c, tt * FTT, (tt + 1) * FTT)
                        xl = xl3(c, tt * FTT, (tt + 1) * FTT)
                        for lhsT, rhs in ((w1h, xh), (w1l, xh), (w1h, xl)):
                            ins = e.matmul(pb[pbk][:, 0:FTT], lhsT, rhs,
                                           start=(cnt == 0),
                                           stop=(cnt == 3 * FC - 1))
                            cnt += 1
                    return ins
                op("tensor", h1mm, inc=s_mm,
                   waits=[(s_sc, wsplit_sc[p]), (s_ve, wsplit_ve[p]),
                          ((s_ve, pb_h_evac[pbk]) if pb_h_evac[pbk] else None),
                          ((s_mm, h1_used[gp]) if fc == 0 and h1_used[gp] else None)])
                bidx = e_i * NCH + g * 2 + fc
                op("vector", lambda e, fc=fc, bidx=bidx, pbk=pbk: e.tensor_scalar(
                    dt_[:, (fc % 2) * 512:(fc % 2 + 1) * 512], pb[pbk][:, 0:FTT],
                    b1_sb[:, bidx:bidx + 1], 0.0, AL.add, AL.max),
                   waits=[(s_mm, s_mm.n)], inc=s_ve)
                pb_h_evac[pbk] = s_ve.n
                op("scalar", lambda e, fc=fc, gp=gp: e.copy(
                    h1g16[:, gp * 2048 + fc * 512:gp * 2048 + (fc + 1) * 512],
                    dt_[:, (fc % 2) * 512:(fc % 2 + 1) * 512]),
                   waits=[(s_ve, s_ve.n),
                          ((s_mm, h1_used[gp]) if fc == 0 and h1_used[gp] else None)],
                   inc=s_sc)
                op("vector", lambda e, fc=fc, gp=gp: e.tensor_tensor(
                    h1g16[:, gp * 2048 + 1024 + fc * 512:
                          gp * 2048 + 1024 + (fc + 1) * 512],
                    dt_[:, (fc % 2) * 512:(fc % 2 + 1) * 512],
                    h1g16[:, gp * 2048 + fc * 512:gp * 2048 + (fc + 1) * 512],
                    AL.subtract),
                   waits=[(s_sc, s_sc.n)], inc=s_ve)
            return s_sc.n, s_ve.n

        def emit_ymm_block(gen, tt, h1_all_sc, h1_all_ve):
            e_i, g = divmod(gen, NG)
            p = gen % 2
            base = p * 8192
            gp = (gen * NFT + tt) % 2
            b2_ld = b2_lds[e_i]
            for ts in range(FTT // 128):
                for jd in range(2):
                    pbk = 4 + (ts * 2 + jd) % 2
                    def ymm(e, g=g, ts=ts, jd=jd, pbk=pbk, base=base, gp=gp):
                        ins = None
                        cnt = 0
                        nmm = 6
                        for fc in range(2):
                            h1h = h1g16[:, gp * 2048 + fc * 512 + ts * 128:
                                        gp * 2048 + fc * 512 + (ts + 1) * 128]
                            h1l = h1g16[:, gp * 2048 + 1024 + fc * 512 + ts * 128:
                                        gp * 2048 + 1024 + fc * 512 + (ts + 1) * 128]
                            w2h = Wb16[:, base + 4096 + fc * D + jd * 512:
                                       base + 4096 + fc * D + (jd + 1) * 512]
                            w2l = Wb16[:, base + 6144 + fc * D + jd * 512:
                                       base + 6144 + fc * D + (jd + 1) * 512]
                            for lhsT, rhs in ((h1h, w2h), (h1l, w2h), (h1h, w2l)):
                                ins = e.matmul(pb[pbk][:], lhsT, rhs,
                                               start=(cnt == 0),
                                               stop=(g > 0 and cnt == nmm - 1))
                                cnt += 1
                        if g == 0:
                            ins = e.matmul(pb[pbk][:], ones1[0:1, :],
                                           brow[0:1, jd * 512:(jd + 1) * 512],
                                           start=False, stop=True)
                        return ins
                    op("tensor", ymm, inc=s_mm,
                       waits=[(s_sc, h1_all_sc), (s_ve, h1_all_ve),
                              (s_ld, b2_ld),
                              ((s_ve, pb_y_evac[pbk]) if pb_y_evac[pbk] else None)])
                    tg = tt * (FTT // 128) + ts
                    op("vector", lambda e, e_i=e_i, tg=tg, jd=jd, pbk=pbk:
                       e.scalar_tensor_tensor(
                           yb[:, tg * D + jd * 512:tg * D + (jd + 1) * 512],
                           pb[pbk][:],
                           oh_tm[:, tg * E + e_i:tg * E + e_i + 1],
                           yb[:, tg * D + jd * 512:tg * D + (jd + 1) * 512],
                           AL.mult, AL.add),
                       waits=[(s_mm, s_mm.n)], inc=s_ve)
                    pb_y_evac[pbk] = s_ve.n
            h1_used[gp] = s_mm.n
            if tt == NFT - 1:
                wuse_mm[p] = s_mm.n

        for gen in range(NGEN):
            for tt in range(NFT):
                sc_v, ve_v = emit_h1_block(gen, tt)
                if pending_ymm[0] is not None:
                    pending_ymm[0]()
                pending_ymm[0] = (lambda gen=gen, tt=tt, sc_v=sc_v, ve_v=ve_v:
                                  emit_ymm_block(gen, tt, sc_v, ve_v))
                if tt == 0 and gen + 1 < NGEN:
                    w_load_split(gen + 1)
        pending_ymm[0]()
        for tl in range(NT):
            op("vector", lambda e, tl=tl: e.tensor_scalar(
                yb[:, tl * D:(tl + 1) * D], yb[:, tl * D:(tl + 1) * D],
                gatev[:, tl:tl + 1], None, AL.mult), inc=s_ve)
            op("scalar", lambda e, tl=tl: e.activation(
                yb[:, tl * D:(tl + 1) * D], yb[:, tl * D:(tl + 1) * D], AF.Silu),
               waits=[(s_ve, s_ve.n)], inc=s_sc)
        g_done_sc = s_sc.n
        if debug:
            op("scalar", lambda e: e.dma_start(
                out=dbg["dbg_g"].rearrange("(tl p) j -> p tl j", p=128),
                in_=yb[:].rearrange("p (tl j) -> p tl j", j=D)),
               inc=s_sst, k=16, waits=[(s_sc, g_done_sc)])
            op("scalar", lambda e: e.dma_start(
                out=dbg["dbg_oh"].rearrange("(tl p) j -> p tl j", p=128),
                in_=oh_tm[:].rearrange("p (tl j) -> p tl j", j=E)),
               inc=s_sst, k=16)
            op("scalar", lambda e: e.dma_start(
                out=dbg["dbg_gate"].rearrange("(tl p) j -> p tl j", p=128),
                in_=gatev[:].rearrange("p (tl j) -> p tl j", j=1),
                allow_slow_non_contiguous=True),
               inc=s_sst, k=16)

        def wload_fold(dw, fold):
            st_sc = [0, 0, 0, 0]
            st_ve = [0, 0, 0, 0]
            stgs = (xt, qt, s2a, s2b)
            entry_mm = s_mm.n
            entry_sst = s_sst.n
            for c in range(FC):
                i = c % 4
                op("sync", lambda e, c=c, dw=dw: e.dma_start(
                    out=stgs[c % 4][:], in_=dw[c * 128:(c + 1) * 128, :]),
                   inc=s_ld, k=16,
                   waits=[((s_sc, st_sc[i]) if st_sc[i] else None),
                          ((s_ve, st_ve[i]) if st_ve[i] else None),
                          ((s_mm, entry_mm) if c < 4 else None),
                          ((s_sst, entry_sst) if c < 4 and entry_sst else None)])
                wld = s_ld.n
                if fold:
                    for jd in range(2):
                        op("tensor", lambda e, c=c, jd=jd, i=i: e.matmul(
                            pbC[jd][0:1, :], betat[:, c:c + 1],
                            stgs[i][:, jd * 512:(jd + 1) * 512],
                            start=(c == 0), stop=(c == FC - 1)),
                           inc=s_mm, waits=[(s_ld, wld),
                                            ((s_ve, s_ve.n) if c == 0 and jd == 0 else None)])
                    op("vector", lambda e, c=c, i=i: e.tensor_scalar(
                        stgs[i][:], stgs[i][:], gammat[:, c:c + 1], None, AL.mult),
                       waits=[(s_mm, s_mm.n)], inc=s_ve)
                op("scalar", lambda e, c=c, i=i: e.copy(
                    wph(c, 0, 1024), stgs[i][:]),
                   waits=[((s_ve, s_ve.n) if fold else (s_ld, wld)),
                          ((s_mm, entry_mm) if c < 2 else None)], inc=s_sc)
                st_sc[i] = s_sc.n
                op("vector", lambda e, c=c, i=i: e.tensor_tensor(
                    wpl(c, 0, 1024), stgs[i][:], wph(c, 0, 1024), AL.subtract),
                   waits=[(s_sc, s_sc.n)], inc=s_ve)
                st_ve[i] = s_ve.n
            if fold:
                for jd in range(2):
                    op("vector", lambda e, jd=jd: e.tensor_copy(
                        brow[0:1, jd * 512:(jd + 1) * 512], pbC[jd][0:1, :]),
                       waits=[(s_mm, s_mm.n)], inc=s_ve)
            return s_sc.n, s_ve.n

        def proj_phase(dw, fold, grep, g_tm, scratch_dram, do_norm, tag):
            wsc, wve = wload_fold(dw, fold)
            s2b16 = s2b[:].bitcast(BF16)
            for tl in range(NT):
                if tag != "v":
                    op("sync", lambda e, tl=tl: e.dma_start(
                        out=xt[:], in_=dx[tl * 128:(tl + 1) * 128, :]),
                       inc=s_ld, k=16,
                       waits=[(s_ve, s_ve.n),
                              ((s_sst, s_sst.n) if s_sst.n else None)])
                    ln_tile(xt[:], qt[:], ld_wait=(s_ld, s_ld.n))
                    ln_tile(qt[:], qt[:])
                    def mk_ev_a(half, pbk):
                        op("vector", lambda e, half=half, pbk=pbk: e.tensor_copy(
                            s2a[:, half * 512:(half + 1) * 512], pbk[:]),
                           waits=[(s_mm, s_mm.n)], inc=s_ve)
                    transpose8(qt[:], mk_ev_a)
                    drain("vector")
                    op("vector", lambda e: e.tensor_copy(s2b16[:, 0:1024], s2a[:]),
                       waits=[((s_sst, s_sst.n) if s_sst.n else None)])
                    drain("vector")
                    op("vector", lambda e: e.tensor_tensor(
                        s2b16[:, 1024:2048], s2a[:], s2b16[:, 0:1024], AL.subtract),
                       inc=s_ve)
                    act16 = s2b16
                else:
                    act16 = None
                for jd in range(2):
                    def pmm(e, tl=tl, jd=jd, act16=act16, fold=fold):
                        ins = None
                        cnt = 0
                        nmm = 3 * FC
                        for c in range(FC):
                            if act16 is None:
                                ah = xh3(c, tl * 128, (tl + 1) * 128)
                                al = xl3(c, tl * 128, (tl + 1) * 128)
                            else:
                                ah = act16[:, c * 128:(c + 1) * 128]
                                al = act16[:, 1024 + c * 128:1024 + (c + 1) * 128]
                            wh = wph(c, jd * 512, (jd + 1) * 512)
                            wl_ = wpl(c, jd * 512, (jd + 1) * 512)
                            for lhsT, rhs in ((ah, wh), (al, wh), (ah, wl_)):
                                ins = e.matmul(pb[jd][:], lhsT, rhs,
                                               start=(cnt == 0),
                                               stop=(not fold and cnt == nmm - 1))
                                cnt += 1
                        if fold:
                            ins = e.matmul(pb[jd][:], ones1[0:1, :],
                                           brow[0:1, jd * 512:(jd + 1) * 512],
                                           start=False, stop=True)
                        return ins
                    op("tensor", pmm, inc=s_mm,
                       waits=[(s_sc, wsc), (s_ve, s_ve.n)])
                    if tag == "v":
                        for hh in range(jd * 8, (jd + 1) * 8):
                            op("vector", lambda e, tl=tl, jd=jd, hh=hh: e.tensor_scalar(
                                qt[:, hh * DH:(hh + 1) * DH],
                                pb[jd][:, (hh - jd * 8) * DH:(hh - jd * 8 + 1) * DH],
                                gk_tm[:, tl * H + hh:tl * H + hh + 1], 0.5,
                                AL.mult, AL.mult),
                               waits=([(s_mm, s_mm.n), (s_sst, s_sst.n)]
                                      if hh == jd * 8 else []),
                               inc=(s_ve if hh == (jd + 1) * 8 - 1 else None))
                    else:
                        op("vector", lambda e, jd=jd: e.tensor_copy(
                            qt[:, jd * 512:(jd + 1) * 512], pb[jd][:]),
                           waits=[(s_mm, s_mm.n),
                                  ((s_sst, s_sst.n) if s_sst.n else None)],
                           inc=s_ve)
                tile_ready = s_ve.n
                if do_norm:
                    op("scalar", lambda e: e.activation(sqt, qt[:], AF.Square),
                       waits=[(s_ve, tile_ready)], inc=s_sc)
                    op("vector", lambda e: e.reduce_sum(
                        nhead[:, 0:H], sqt.rearrange("p (h d) -> p h d", d=DH),
                        axis=AX.X), waits=[(s_sc, s_sc.n)], inc=s_ve)
                    op("scalar", lambda e: e.activation(nhead[:, H:2 * H],
                                                        nhead[:, 0:H], AF.Sqrt),
                       waits=[(s_ve, s_ve.n)], inc=s_sc)
                    op("vector", lambda e: e.tensor_scalar_max(
                        nhead[:, 2 * H:3 * H], nhead[:, H:2 * H], NORM_EPS),
                       waits=[(s_sc, s_sc.n)])
                    drain("vector")
                    op("vector", lambda e: e.reciprocal(nhead[:, 3 * H:4 * H],
                                                        nhead[:, 2 * H:3 * H]))
                    drain("vector")
                    for h in range(H):
                        op("vector", lambda e, h=h: e.tensor_scalar(
                            qt[:, h * DH:(h + 1) * DH], qt[:, h * DH:(h + 1) * DH],
                            nhead[:, 3 * H + h:3 * H + h + 1], None, AL.mult))
                    drain("vector")
                    for h in range(H):
                        op("vector", lambda e, h=h, grep=grep: e.scalar_tensor_tensor(
                            h1g[:, 0:DH], qt[:, h * DH:(h + 1) * DH], 1.0, grep,
                            AL.mult, AL.mult, accum_out=nhead[:, h:h + 1]))
                    drain("vector")
                    op("vector", lambda e: e.engine_nop(), inc=s_ve)
                    op("scalar", lambda e: e.activation(nhead[:, H:2 * H],
                                                        nhead[:, 0:H], AF.Sigmoid),
                       waits=[(s_ve, s_ve.n)], inc=s_sc)
                    op("vector", lambda e, g_tm=g_tm, tl=tl: e.tensor_scalar(
                        g_tm[:, tl * H:(tl + 1) * H], nhead[:, H:2 * H], 0.75, 0.25,
                        AL.mult, AL.add),
                       waits=[(s_sc, s_sc.n)], inc=s_ve)
                    def mk_ev_b(half, pbk):
                        op("vector", lambda e, half=half, pbk=pbk: e.tensor_copy(
                            s2b[:, half * 512:(half + 1) * 512], pbk[:]),
                           waits=[(s_mm, s_mm.n),
                                  ((s_sst, s_sst.n) if s_sst.n else None)],
                           inc=s_ve)
                    transpose8(qt[:], mk_ev_b)
                    xt16 = xt[:].bitcast(BF16)
                    drain("vector")
                    op("vector", lambda e, xt16=xt16: e.tensor_copy(
                        xt16[:, 0:1024], s2b[:]), waits=[(s_ve, s_ve.n)])
                    drain("vector")
                    op("vector", lambda e, xt16=xt16: e.tensor_tensor(
                        xt16[:, 1024:2048], s2b[:], xt16[:, 0:1024], AL.subtract),
                       inc=s_ve)
                    sdh, sdl = scratch_dram
                    op("scalar", lambda e, tl=tl, sd=sdh, xt16=xt16: e.dma_start(
                        out=sd[:, :, tl * 128:(tl + 1) * 128].rearrange(
                            "c p t -> p c t"),
                        in_=xt16[:, 0:1024].rearrange("p (c t) -> p c t", t=128)),
                       waits=[(s_ve, s_ve.n)], inc=s_sst, k=16)
                    op("scalar", lambda e, tl=tl, sd=sdl, xt16=xt16: e.dma_start(
                        out=sd[:, :, tl * 128:(tl + 1) * 128].rearrange(
                            "c p t -> p c t"),
                        in_=xt16[:, 1024:2048].rearrange("p (c t) -> p c t", t=128)),
                       inc=s_sst, k=16)
                    if debug:
                        dk = "dbg_q" if tag == "q" else "dbg_k"
                        op("scalar", lambda e, tl=tl, dk=dk: e.dma_start(
                            out=dbg[dk][tl * 128:(tl + 1) * 128, :], in_=qt[:]),
                           inc=s_sst, k=16)
                else:
                    xt16 = xt[:].bitcast(BF16)
                    drain("vector")
                    op("vector", lambda e, xt16=xt16: e.tensor_copy(
                        xt16[:, 0:1024], qt[:]),
                       waits=[(s_ve, tile_ready),
                              ((s_sst, s_sst.n) if s_sst.n else None)])
                    drain("vector")
                    op("vector", lambda e, xt16=xt16: e.tensor_tensor(
                        xt16[:, 1024:2048], qt[:], xt16[:, 0:1024], AL.subtract),
                       inc=s_ve)
                    op("scalar", lambda e, tl=tl, xt16=xt16: e.dma_start(
                        out=dvg_h[tl * 128:(tl + 1) * 128, :], in_=xt16[:, 0:1024]),
                       waits=[(s_ve, s_ve.n)], inc=s_sst, k=16)
                    op("scalar", lambda e, tl=tl, xt16=xt16: e.dma_start(
                        out=dvg_l[tl * 128:(tl + 1) * 128, :], in_=xt16[:, 1024:2048]),
                       inc=s_sst, k=16)
                    for jd in range(2):
                        op("tensor", lambda e, tl=tl, jd=jd: e.matmul(
                            pb[2 + jd][0:1, :], onesc[:, 0:1],
                            qt[:, jd * 512:(jd + 1) * 512],
                            start=(tl == 0), stop=(tl == NT - 1)),
                           inc=s_mm, waits=[(s_ve, tile_ready)])
                    if tl == NT - 1:
                        for jd in range(2):
                            op("vector", lambda e, jd=jd: e.tensor_copy(
                                csum[0:1, jd * 512:(jd + 1) * 512],
                                pb[2 + jd][0:1, :]),
                               waits=[(s_mm, s_mm.n)], inc=s_ve)

        proj_phase(dwk, True, gkrep[:], gk_tm, (dkn_h, dkn_l), True, "k")
        proj_phase(dwv, False, None, None, None, False, "v")
        proj_phase(dwq, True, gqrep[:], gq_tm, (dqn_h, dqn_l), True, "q")
        if debug:
            op("scalar", lambda e: e.dma_start(out=dbg["dbg_stage"], in_=xt[:]),
               waits=[(s_ve, s_ve.n), (s_sst, s_sst.n)], inc=s_sst, k=16)
            op("sync", lambda e: e.dma_start(
                out=dbg["dbg_rd"].bitcast(BF16)[:, 0:T], in_=dqn_h[1]),
               waits=[(s_sst, s_sst.n)], inc=s_ld, k=16)
            op("sync", lambda e: e.dma_start(
                out=dbg["dbg_rd"].bitcast(BF16)[:, 2 * T:3 * T], in_=dqn_l[1]),
               inc=s_ld, k=16)
            op("sync", lambda e: e.dma_start(
                out=dbg["dbg_rd"].bitcast(BF16)[:, T:T + NKC * DH].rearrange(
                    "p (kc d) -> p kc d", d=DH), 
                in_=dvg_h[:, (2 * FC - 2) * DH:(2 * FC - 2) * DH + DH].rearrange(
                    "(kc p) d -> p kc d", p=128)),
               inc=s_ld, k=16)
        if debug:
            op("scalar", lambda e: e.dma_start(
                out=dbg["dbg_gq"].rearrange("(tl p) h -> p tl h", p=128),
                in_=gq_tm[:].rearrange("p (tl h) -> p tl h", h=H)),
               waits=[(s_ve, s_ve.n)], inc=s_sst, k=16)
            op("scalar", lambda e: e.dma_start(
                out=dbg["dbg_gk"].rearrange("(tl p) h -> p tl h", p=128),
                in_=gk_tm[:].rearrange("p (tl h) -> p tl h", h=H)),
               inc=s_sst, k=16)
            op("scalar", lambda e: e.dma_start(out=dbg["dbg_csum"], in_=csum[:]),
               inc=s_sst, k=16)

        ve_at_D = s_ve.n
        sst_at_D = s_sst.n
        dt16 = dt_[:].bitcast(BF16)
        sq16x = h1g[:].bitcast(BF16)
        gqbc = xt[:, 0:512]
        gqrowA = xt[0:1, 512:1024]
        gqrowB = s2b[0:1, 0:512]
        tfs = (qt[:], s2a[:])
        sab_t = (pbA, pbB)
        sab_sc = [0, 0]
        sab_ve = [0, 0]
        fused_evac_ve = ve_at_D
        gq_ve = ve_at_D
        grow_ve = ve_at_D
        fusedT3 = fusedT[:].rearrange("p (c t) -> p c t", t=T)
        for hp in range(FC):
            for (dst, srcd) in ((qnh_hi, dqn_h), (qnh_lo, dqn_l),
                                (knh_hi, dkn_h), (knh_lo, dkn_l)):
                op("sync", lambda e, dst=dst, srcd=srcd, hp=hp: e.dma_start(
                    out=dst, in_=srcd[hp]),
                   inc=s_ld, k=16,
                   waits=([(s_mm, s_mm.n), (s_sst, s_sst.n)] if dst is qnh_hi else []))
            for (dst, srcd) in ((vgh_hi, dvg_h), (vgh_lo, dvg_l)):
                for h2 in (0, 1):
                    op("sync", lambda e, dst=dst, srcd=srcd, hp=hp, h2=h2: e.dma_start(
                        out=dst[:, h2 * NKC * DH:(h2 + 1) * NKC * DH].rearrange(
                            "p (kc d) -> p kc d", d=DH),
                        in_=srcd[:, (2 * hp + h2) * DH:(2 * hp + h2 + 1) * DH].rearrange(
                            "(kc p) d -> p kc d", p=128)),
                       inc=s_ld, k=16)
            hd_ld = s_ld.n
            for qc in range(NQC):
                ntl = QCW // 128
                def gqtA(e, hp=hp, qc=qc, ntl=ntl):
                    ins = None
                    for i in range(ntl):
                        tl = qc * ntl + i
                        ins = e.transpose(pbC[2][0:1, i * 128:(i + 1) * 128],
                                          gq_tm[:, tl * H + 2 * hp:tl * H + 2 * hp + 1],
                                          eyet[:])
                    return ins
                def gqtB(e, hp=hp, qc=qc, ntl=ntl):
                    ins = None
                    for i in range(ntl):
                        tl = qc * ntl + i
                        ins = e.transpose(pbC[3][0:1, i * 128:(i + 1) * 128],
                                          gq_tm[:, tl * H + 2 * hp + 1:tl * H + 2 * hp + 2],
                                          eyet[:])
                    return ins
                op("tensor", gqtA, waits=[(s_ve, grow_ve)])
                op("tensor", gqtB, inc=s_mm)
                op("vector", lambda e: e.tensor_copy(gqrowA, pbC[2][0:1, 0:QCW]),
                   waits=[(s_mm, s_mm.n),
                          ((s_sst, sst_at_D) if hp == 0 and qc == 0 else None)])
                op("vector", lambda e: e.tensor_copy(gqrowB, pbC[3][0:1, 0:QCW]),
                   inc=s_ve)
                grow_ve = s_ve.n
                def gbc(e):
                    e.matmul(pbC[1][0:64, 0:QCW], ones1[0:1, 0:64], gqrowA,
                             start=True, stop=True, tile_position=(0, 0))
                    return e.matmul(pbC[1][64:128, 0:QCW], ones1[0:1, 0:64], gqrowB,
                                    start=True, stop=True, tile_position=(0, 64))
                op("tensor", gbc, inc=s_mm, waits=[(s_ve, grow_ve), (s_ve, gq_ve)])
                op("vector", lambda e: e.tensor_copy(gqbc, pbC[1][:, 0:QCW]),
                   waits=[(s_mm, s_mm.n)], inc=s_ve)
                gq_ve = s_ve.n
                op("tensor", lambda e, hp=hp: e.matmul(
                    pbC[0][:, 0:QCW], csum[0:1, hp * 128:(hp + 1) * 128],
                    onesr[0:1, 0:QCW], start=True, stop=False),
                   inc=s_mm, waits=[(s_ve, fused_evac_ve), (s_ld, hd_ld)])
                def emit_smm(kc):
                    sl = kc % 2
                    sab = sab_t[sl]
                    def smm(e, kc=kc, qc=qc, sab=sab):
                        ins = None
                        ops = ((qnh_hi, knh_hi), (qnh_lo, knh_hi), (qnh_hi, knh_lo))
                        for i, (qb, kb) in enumerate(ops):
                            e.matmul(sab[:, 0:QCW],
                                     kb[0:64, kc * 128:(kc + 1) * 128],
                                     qb[0:64, qc * QCW:(qc + 1) * QCW],
                                     start=(i == 0), stop=(i == 2),
                                     tile_position=(0, 0))
                            ins = e.matmul(sab[:, 512:512 + QCW],
                                           kb[64:128, kc * 128:(kc + 1) * 128],
                                           qb[64:128, qc * QCW:(qc + 1) * QCW],
                                           start=(i == 0), stop=(i == 2),
                                           tile_position=(64, 0))
                        return ins
                    op("tensor", smm, inc=s_mm,
                       waits=[(s_ld, hd_ld),
                              ((s_sc, sab_sc[sl]) if sab_sc[sl] else (s_ve, ve_at_D))])
                    return s_mm.n
                smm_n_of = {}
                smm_n_of[0] = emit_smm(0)
                for kc in range(NKC):
                    sl = kc % 2
                    sab = sab_t[sl]
                    if kc + 1 < NKC:
                        smm_n_of[kc + 1] = emit_smm(kc + 1)
                    smm_n = smm_n_of[kc]
                    op("scalar", lambda e, sab=sab, sl=sl: e.activation(
                        tfs[sl], sab[:, 0:1024], AF.Tanh, bias=CM405, scale=C15),
                       waits=[(s_mm, smm_n)], inc=s_sc)
                    sab_sc[sl] = s_sc.n
                    op("vector", lambda e, sl=sl: e.tensor_copy(
                        dt16[:, sl * 1024:(sl + 1) * 1024], tfs[sl]),
                       waits=[(s_sc, s_sc.n), (s_mm, smm_n)])
                    drain("vector")
                    op("vector", lambda e, sl=sl: e.tensor_tensor(
                        sq16x[:, 2048 + sl * 1024:2048 + (sl + 1) * 1024], tfs[sl],
                        dt16[:, sl * 1024:(sl + 1) * 1024], AL.subtract),
                       inc=s_ve)
                    sab_ve[sl] = s_ve.n
                    def fmm(e, kc=kc, sl=sl):
                        ins = None
                        ops = ((vgh_hi, dt16), (vgh_lo, dt16), (vgh_hi, None))
                        for i, (vb, tb) in enumerate(ops):
                            toff = sl * 1024 if tb is not None else 2048 + sl * 1024
                            tbb = tb if tb is not None else sq16x
                            e.matmul(pbC[0][0:64, 0:QCW],
                                     vb[:, kc * DH:(kc + 1) * DH],
                                     tbb[:, toff:toff + QCW],
                                     start=False, stop=(kc == NKC - 1 and i == 2),
                                     tile_position=(0, 0))
                            ins = e.matmul(pbC[0][64:128, 0:QCW],
                                           vb[:, NKC * DH + kc * DH:
                                              NKC * DH + (kc + 1) * DH],
                                           tbb[:, toff + 512:toff + 512 + QCW],
                                           start=False,
                                           stop=(kc == NKC - 1 and i == 2),
                                           tile_position=(0, 64))
                        return ins
                    op("tensor", fmm, inc=s_mm, waits=[(s_ve, sab_ve[sl])])
                def fev(e, hp=hp, qc=qc):
                    dstv = fusedT3[:, hp, qc * QCW:(qc + 1) * QCW]
                    return e.tensor_tensor(dstv, pbC[0][:, 0:QCW], gqbc, AL.mult)
                op("vector", fev, waits=[(s_mm, s_mm.n), (s_ve, gq_ve)], inc=s_ve)
                fused_evac_ve = s_ve.n
        if debug:
            op("scalar", lambda e: e.dma_start(out=dbg["dbg_fused"], in_=fusedT[:]),
               waits=[(s_ve, s_ve.n)], inc=s_sst, k=16)
            op("scalar", lambda e: e.dma_start(out=dbg["dbg_gqbc"], in_=gqbc),
               inc=s_sst, k=16)
            op("scalar", lambda e: e.dma_start(out=dbg["dbg_t0"], in_=tfs[0]),
               inc=s_sst, k=16)
            op("scalar", lambda e: e.dma_start(out=dbg["dbg_t1"], in_=tfs[1]),
               inc=s_sst, k=16)
            op("scalar", lambda e: e.dma_start(
                out=dbg["dbg_qnh"].rearrange("p j -> p j").bitcast(BF16)[:, 0:T]
                if False else dbg["dbg_qnh"].bitcast(BF16)[:, 0:T], in_=qnh_hi),
               inc=s_sst, k=16)
            op("scalar", lambda e: e.dma_start(
                out=dbg["dbg_vgh"].bitcast(BF16)[:, 0:2 * NKC * DH], in_=vgh_hi),
               inc=s_sst, k=16)

        def lwo(e):
            dstv = Wbuf[:].rearrange("p (c j) -> p c j", j=D)
            return e.dma_start(out=dstv, in_=dwo.rearrange("(c p) j -> p c j", p=128))
        op("sync", lwo, inc=s_ld, k=16,
           waits=[(s_mm, s_mm.n), ((s_sst, s_sst.n) if s_sst.n else None)])
        op("sync", lambda e: e.dma_start(out=brow[0:1, :], in_=dbo), inc=s_ld, k=16)
        wo_ld = s_ld.n
        sc_at_E = s_sc.n
        pe_evac = {0: 0, 1: 0}
        for tl in range(NT):
            op("sync", lambda e, tl=tl: e.dma_start(
                out=xt[:], in_=dx[tl * 128:(tl + 1) * 128, :]),
               inc=s_ld, k=16, waits=[(s_ve, s_ve.n)])
            x_ld = s_ld.n
            for jd in range(2):
                def omm(e, tl=tl, jd=jd):
                    ins = None
                    for c in range(FC):
                        lhsT = fusedT[:].rearrange("p (c t) -> p c t", t=T)[
                            :, c, tl * 128:(tl + 1) * 128]
                        ins = e.matmul(pb[jd][:], lhsT,
                                       Wbuf[:, c * D + jd * 512:
                                            c * D + (jd + 1) * 512],
                                       start=(c == 0), stop=False)
                    ins = e.matmul(pb[jd][:], ones1[0:1, :],
                                   brow[0:1, jd * 512:(jd + 1) * 512],
                                   start=False, stop=True)
                    return ins
                op("tensor", omm, inc=s_mm,
                   waits=[(s_ld, wo_ld), (s_ve, s_ve.n),
                          ((s_ve, pe_evac[jd]) if pe_evac[jd] else None),
                          ((s_sc, sc_at_E) if tl == 0 else None)])
                op("vector", lambda e, tl=tl, jd=jd: e.scalar_tensor_tensor(
                    s2a[:, jd * 512:(jd + 1) * 512], pb[jd][:], 1.0,
                    yb[:, tl * D + jd * 512:tl * D + (jd + 1) * 512],
                    AL.bypass, AL.mult),
                   waits=[(s_mm, s_mm.n), (s_sc, g_done_sc),
                          ((s_out, s_out.n) if s_out.n else None)])
                drain("vector")
                op("vector", lambda e, jd=jd: e.tensor_tensor(
                    s2a[:, jd * 512:(jd + 1) * 512],
                    s2a[:, jd * 512:(jd + 1) * 512],
                    xt[:, jd * 512:(jd + 1) * 512], AL.add),
                   waits=[(s_ld, x_ld)], inc=s_ve)
                pe_evac[jd] = s_ve.n
            op("scalar", lambda e, tl=tl: e.dma_start(
                out=dout[tl * 128:(tl + 1) * 128, :], in_=s2a[:]),
               waits=[(s_ve, s_ve.n)], inc=s_out, k=16)
        plan["scalar"].append(lambda e, v=s_out.n: e.wait_ge(s_out.sem, v))

        with nc.Block() as block:
            block.sync(lambda e: [f(e) for f in plan["sync"]])
            block.vector(lambda e: [f(e) for f in plan["vector"]])
            block.scalar(lambda e: [f(e) for f in plan["scalar"]])
            block.tensor(lambda e: [f(e) for f in plan["tensor"]])
            block.gpsimd(lambda e: [f(e) for f in plan["gpsimd"]])
    return nc


def make_inputs(xc, wq, wk, wv, g_q, g_k, wo, b_o, gamma, beta, Wr, br,
                W1, b1, W2, b2):
    return {
        "x": np.ascontiguousarray(xc),
        "wq": np.ascontiguousarray(wq),
        "wk": np.ascontiguousarray(wk),
        "wv": np.ascontiguousarray(wv),
        "wo": np.ascontiguousarray(wo),
        "bo": np.ascontiguousarray(b_o.reshape(1, D)),
        "gamma_fm": np.ascontiguousarray(gamma.reshape(FC, 128).T),
        "beta_fm": np.ascontiguousarray(beta.reshape(FC, 128).T),
        "gq_rep": np.ascontiguousarray(np.broadcast_to(g_q, (128, DH))),
        "gk_rep": np.ascontiguousarray(np.broadcast_to(g_k, (128, DH))),
        "wr_sb": np.ascontiguousarray(
            Wr.reshape(FC, 128, E).transpose(1, 0, 2).reshape(128, FC * E)),
        "br": np.ascontiguousarray(br.reshape(1, E)),
        "W1": np.ascontiguousarray(W1),
        "b1_fm": np.ascontiguousarray(
            b1.reshape(E, DFF // 128, 128).transpose(2, 0, 1).reshape(128, -1)),
        "W2": np.ascontiguousarray(W2),
        "b2": np.ascontiguousarray(b2.reshape(1, E * D)),
        "eye": np.eye(128, dtype=np.float32),
    }


_NC_CACHE = {}


def kernel(L0_attn_x, L0_attn_w_q, L0_attn_w_k, L0_attn_w_v, L0_attn_g_q,
           L0_attn_g_k, L0_attn_w_o, L0_attn_b_o, L0_attn_gamma, L0_attn_beta,
           L0_ffn_Wr, L0_ffn_br, L0_ffn_W1, L0_ffn_b1, L0_ffn_W2, L0_ffn_b2):
    x = np.asarray(L0_attn_x, dtype=np.float32)
    B, T, _ = x.shape
    if (T,) not in _NC_CACHE:
        _NC_CACHE[(T,)] = build(T=T)
    nc = _NC_CACHE[(T,)]
    args = [np.asarray(a, dtype=np.float32) for a in
            (L0_attn_w_q, L0_attn_w_k, L0_attn_w_v, L0_attn_g_q, L0_attn_g_k,
             L0_attn_w_o, L0_attn_b_o, L0_attn_gamma, L0_attn_beta,
             L0_ffn_Wr, L0_ffn_br, L0_ffn_W1, L0_ffn_b1, L0_ffn_W2, L0_ffn_b2)]
    in_maps = [make_inputs(x[c], *args) for c in range(B)]
    res = run_bass_kernel_spmd(nc, in_maps, core_ids=list(range(B)))
    return np.stack([np.asarray(r["out"]) for r in res.results], axis=0)


# version 05
# speedup vs baseline: 1.6459x; 1.0032x over previous
import numpy as np
from contextlib import ExitStack

import concourse.bass as bass
import concourse.mybir as mybir
from concourse.bass_utils import run_bass_kernel_spmd

F32 = mybir.dt.float32
AF = mybir.ActivationFunctionType
AL = mybir.AluOpType
AX = mybir.AxisListType

D = 1024
H = 16
DH = 64
E = 8
DFF = 4096
FC = D // 128
GDFF = 256
NG = DFF // GDFF
LN_EPS = 1e-5
NORM_EPS = 1e-12


class Ctr:
    def __init__(self, sem):
        self.sem = sem
        self.n = 0


def build(T=2048, debug=False):
    NT = T // 128
    FTT = min(512, T)
    NFT = T // FTT
    NQC = max(1, T // 512)
    QCW = min(512, T)
    NKC = T // 128
    NH1 = GDFF // 128

    nc = bass.Bass("TRN2", target_bir_lowering=False)

    dx = nc.dram_tensor("x", [T, D], F32, kind="ExternalInput").ap()
    dwq = nc.dram_tensor("wq", [D, D], F32, kind="ExternalInput").ap()
    dwk = nc.dram_tensor("wk", [D, D], F32, kind="ExternalInput").ap()
    dwv = nc.dram_tensor("wv", [D, D], F32, kind="ExternalInput").ap()
    dwo = nc.dram_tensor("wo", [D, D], F32, kind="ExternalInput").ap()
    dbo = nc.dram_tensor("bo", [1, D], F32, kind="ExternalInput").ap()
    dgamma = nc.dram_tensor("gamma_fm", [128, FC], F32, kind="ExternalInput").ap()
    dbeta = nc.dram_tensor("beta_fm", [128, FC], F32, kind="ExternalInput").ap()
    dgq = nc.dram_tensor("gq_rep", [128, DH], F32, kind="ExternalInput").ap()
    dgk = nc.dram_tensor("gk_rep", [128, DH], F32, kind="ExternalInput").ap()
    dwr = nc.dram_tensor("wr_sb", [128, FC * E], F32, kind="ExternalInput").ap()
    dbr = nc.dram_tensor("br", [1, E], F32, kind="ExternalInput").ap()
    dW1 = nc.dram_tensor("W1", [E, D, DFF], F32, kind="ExternalInput").ap()
    db1 = nc.dram_tensor("b1_fm", [128, E * (DFF // 128)], F32, kind="ExternalInput").ap()
    dW2 = nc.dram_tensor("W2", [E, DFF, D], F32, kind="ExternalInput").ap()
    db2 = nc.dram_tensor("b2", [1, E * D], F32, kind="ExternalInput").ap()
    deye = nc.dram_tensor("eye", [128, 128], F32, kind="ExternalInput").ap()
    dout = nc.dram_tensor("out", [T, D], F32, kind="ExternalOutput").ap()
    BF16 = mybir.dt.bfloat16
    dqn_h = nc.dram_tensor("qn_sh", [FC, 128, T], BF16).ap()
    dqn_l = nc.dram_tensor("qn_sl", [FC, 128, T], BF16).ap()
    dkn_h = nc.dram_tensor("kn_sh", [FC, 128, T], BF16).ap()
    dkn_l = nc.dram_tensor("kn_sl", [FC, 128, T], BF16).ap()
    dvg_h = nc.dram_tensor("vg_sh", [T, D], BF16).ap()
    dvg_l = nc.dram_tensor("vg_sl", [T, D], BF16).ap()
    dbg = {}
    if debug:
        for name, shp in [("dbg_g", [T, D]), ("dbg_oh", [T, E]), ("dbg_gate", [T, 1]),
                          ("dbg_q", [T, D]), ("dbg_k", [T, D]),
                          ("dbg_gq", [T, H]), ("dbg_gk", [T, H]),
                          ("dbg_fused", [128, FC * T]), ("dbg_csum", [1, D]),
                          ("dbg_xn", [T, D]), ("dbg_gqbc", [128, 512]),
                          ("dbg_stage", [128, 1024]), ("dbg_rd", [128, 2 * T]),
                          ("dbg_praw", [128, 1024]),
                          ("dbg_t0", [128, 1024]), ("dbg_t1", [128, 1024]),
                          ("dbg_qnh", [128, 2048]), ("dbg_vgh", [128, 2048])]:
            dbg[name] = nc.dram_tensor(name, shp, F32, kind="ExternalOutput").ap()

    plan = {k: [] for k in ("sync", "vector", "scalar", "tensor", "gpsimd")}

    def op(eng, fn, inc=None, k=1, waits=()):
        waits = [w for w in waits if w is not None]
        for (c, v) in waits:
            assert v <= c.n, f"wait on future value: {v} > {c.n}"
        if inc is not None:
            inc.n += k
        def run(e, fn=fn, inc=inc, k=k, waits=tuple(waits)):
            for (c, v) in waits:
                if v > 0:
                    e.wait_ge(c.sem, v)
            ins = fn(e)
            if inc is not None:
                ins.then_inc(inc.sem, k)
        plan[eng].append(run)

    def drain(eng):
        plan[eng].append(lambda e: e.drain())

    ctx = ExitStack()
    with ctx:
        sb = lambda n, s: ctx.enter_context(nc.sbuf_tensor(n, s, F32))
        xnT = sb("xnT", [128, FC * T])
        yb = sb("yb", [128, NT * D])
        Wbuf = sb("Wbuf", [128, 8 * D])
        xt = sb("xt", [128, D])
        dt_ = sb("dt_", [128, D])
        qt = sb("qt", [128, D])
        s2a = sb("s2a", [128, D])
        s2b = sb("s2b", [128, D])
        h1g = sb("h1g", [128, 2048])
        sqt = h1g[:, 0:1024]
        stats = sb("stats", [128, 32])
        nhead = sb("nhead", [128, 4 * H])
        gq_tm = sb("gq_tm", [128, NT * H])
        gk_tm = sb("gk_tm", [128, NT * H])
        oh_tm = sb("oh_tm", [128, NT * E])
        gatev = sb("gatev", [128, NT])
        gammat = sb("gammat", [128, FC])
        betat = sb("betat", [128, FC])
        gqrep = sb("gqrep", [128, DH])
        gkrep = sb("gkrep", [128, DH])
        wr_sb = sb("wr_sbuf", [128, FC * E])
        wrhl = sb("wrhl", [128, FC * E])
        b1_sb = sb("b1_sbuf", [128, E * (DFF // 128)])
        eyet = sb("eyet", [128, 128])
        cst = sb("cst", [128, 8])
        onesr = sb("onesr", [1, 512])
        ones1 = sb("ones128", [1, 128])
        onesc = sb("onescol", [128, 1])
        brow = sb("brow", [1, D])
        br_sb = sb("br_sbuf", [1, E])
        csum = sb("csum_row", [1, D])
        Wb16 = Wbuf[:].bitcast(BF16)
        xnT16 = xnT[:].bitcast(BF16)
        XLO = FC * T
        def xh3(c, a, b):
            return xnT16[:, c * T + a:c * T + b]
        def xl3(c, a, b):
            return xnT16[:, XLO + c * T + a:XLO + c * T + b]
        def wph(c, a, b):
            return Wb16[:, c * D + a:c * D + b]
        def wpl(c, a, b):
            return Wb16[:, 8192 + c * D + a:8192 + c * D + b]
        qnh_hi = Wb16[:, 0:T]
        qnh_lo = Wb16[:, T:2 * T]
        knh_hi = Wb16[:, 2 * T:3 * T]
        knh_lo = Wb16[:, 3 * T:4 * T]
        VGO = 4 * T
        vgh_hi = Wb16[:, VGO:VGO + 2 * NKC * DH]
        vgh_lo = Wb16[:, VGO + 2 * NKC * DH:VGO + 4 * NKC * DH]
        pbA = ctx.enter_context(nc.psum_tensor("pbA", [128, 1024], F32))
        pbB = ctx.enter_context(nc.psum_tensor("pbB", [128, 1024], F32))
        pbC = [ctx.enter_context(nc.psum_tensor(f"pbC{i}", [128, 512], F32))
               for i in range(4)]
        pb = {0: pbA[:, 0:512], 1: pbA[:, 512:1024],
              4: pbB[:, 0:512], 5: pbB[:, 512:1024],
              2: pbC[0][:], 3: pbC[1][:], 6: pbC[2][:], 7: pbC[3][:]}

        s_ld = Ctr(ctx.enter_context(nc.semaphore("s_ld")))
        s_mm = Ctr(ctx.enter_context(nc.semaphore("s_mm")))
        s_ve = Ctr(ctx.enter_context(nc.semaphore("s_ve")))
        s_sc = Ctr(ctx.enter_context(nc.semaphore("s_sc")))
        s_sst = Ctr(ctx.enter_context(nc.semaphore("s_sst")))
        s_out = Ctr(ctx.enter_context(nc.semaphore("s_out")))

        CI1024 = cst[:, 0:1]
        CEPS = cst[:, 1:2]
        C15 = cst[:, 2:3]
        CM405 = cst[:, 3:4]
        fusedT = xnT

        xnT3 = xnT[:].rearrange("p (c t) -> p c t", t=T)

        def ldi(dst, src):
            op("sync", lambda e, d=dst, s=src: e.dma_start(out=d, in_=s),
               inc=s_ld, k=16)

        ldi(gammat[:], dgamma)
        ldi(betat[:], dbeta)
        ldi(gqrep[:], dgq)
        ldi(gkrep[:], dgk)
        ldi(wr_sb[:], dwr)
        ldi(b1_sb[:], db1)
        ldi(eyet[:], deye)
        ldi(br_sb[:], dbr)
        for aptgt, val in [(cst[:, 0:1], 1.0 / D), (cst[:, 1:2], LN_EPS),
                           (cst[:, 2:3], 15.0), (cst[:, 3:4], -4.05),
                           (onesr[:], 1.0), (ones1[:], 1.0), (onesc[:], 1.0)]:
            op("vector", lambda e, a=aptgt, v=val: e.memset(a, v), inc=s_ve)
        wr16 = wrhl[:].bitcast(BF16)
        op("vector", lambda e: e.tensor_copy(wr16[:, 0:FC * E], wr_sb[:]),
           waits=[(s_ld, s_ld.n)])
        drain("vector")
        op("vector", lambda e: e.tensor_tensor(wr16[:, FC * E:2 * FC * E], wr_sb[:],
                                               wr16[:, 0:FC * E], AL.subtract))
        op("vector", lambda e: e.memset(yb[:], 0.0), inc=s_ve)
        setup_ve = s_ve.n

        def ln_tile(src, out, ld_wait=None):
            drain("vector")
            op("vector", lambda e: e.reduce_sum(stats[:, 0:1], src, axis=AX.X),
               waits=[ld_wait] if ld_wait else [])
            drain("vector")
            op("vector", lambda e: e.tensor_scalar_mul(stats[:, 1:2], stats[:, 0:1], 1.0 / D))
            drain("vector")
            op("vector", lambda e: e.tensor_scalar(dt_[:], src, stats[:, 1:2],
                                                   None, AL.subtract), inc=s_ve)
            op("scalar", lambda e: e.activation(sqt, dt_[:], AF.Square,
                                                accum_out=stats[:, 2:3]),
               waits=[(s_ve, s_ve.n)], inc=s_sc)
            drain("scalar")
            op("scalar", lambda e: e.activation(stats[:, 3:4], stats[:, 2:3],
                                                AF.Sqrt, scale=CI1024, bias=CEPS),
               inc=s_sc)
            op("vector", lambda e: e.reciprocal(stats[:, 4:5], stats[:, 3:4]),
               waits=[(s_sc, s_sc.n)])
            drain("vector")
            op("vector", lambda e, out=out: e.tensor_scalar(out, dt_[:], stats[:, 4:5],
                                                            None, AL.mult), inc=s_ve)

        tr_evac = {6: 0, 7: 0}

        def transpose8(src, evac_fn_for_half):
            src_ready = s_ve.n
            for half in (0, 1):
                pbi = 6 + half
                def tmf(e, half=half, pbi=pbi, src=src):
                    ins = None
                    for c in range(4):
                        cc = half * 4 + c
                        ins = e.transpose(pb[pbi][:, c * 128:(c + 1) * 128],
                                          src[:, cc * 128:(cc + 1) * 128], eyet[:])
                    return ins
                op("tensor", tmf, inc=s_mm,
                   waits=[(s_ve, src_ready),
                          ((s_ve, tr_evac[pbi]) if tr_evac[pbi] else None)])
                evac_fn_for_half(half, pb[pbi])
                tr_evac[pbi] = s_ve.n

        for tl in range(NT):
            op("sync", lambda e, tl=tl: e.dma_start(out=xt[:],
                                                    in_=dx[tl * 128:(tl + 1) * 128, :]),
               inc=s_ld, k=16, waits=[(s_ve, s_ve.n if tl else setup_ve)])
            ln_tile(xt[:], qt[:], ld_wait=(s_ld, s_ld.n))
            def mk_evac(half, pbk, tl=tl):
                hi3 = xnT16[:, 0:FC * T].rearrange("p (c t) -> p c t", t=T)[
                    :, half * 4:(half + 1) * 4, tl * 128:(tl + 1) * 128]
                lo3 = xnT16[:, XLO:XLO + FC * T].rearrange("p (c t) -> p c t", t=T)[
                    :, half * 4:(half + 1) * 4, tl * 128:(tl + 1) * 128]
                p3 = pbk[:].rearrange("p (c t) -> p c t", t=128)
                op("vector", lambda e, hi3=hi3, p3=p3: e.tensor_copy(hi3, p3),
                   waits=[(s_mm, s_mm.n)])
                drain("vector")
                op("vector", lambda e, hi3=hi3, lo3=lo3, p3=p3: e.tensor_tensor(
                    lo3, p3, hi3, AL.subtract), inc=s_ve)
            transpose8(qt[:], mk_evac)
        xnT_ready = s_ve.n
        if debug:
            op("scalar", lambda e: e.dma_start(
                out=dbg["dbg_xn"].rearrange("(a p) j -> p a j", p=128),
                in_=xnT[:].rearrange("p (a j) -> p a j", j=D)),
               waits=[(s_ve, s_ve.n)], inc=s_sst, k=16)

        for tl in range(NT):
            def rmm(e, tl=tl):
                ins = None
                first = True
                for c in range(FC):
                    xh = xh3(c, tl * 128, (tl + 1) * 128)
                    xl = xl3(c, tl * 128, (tl + 1) * 128)
                    wh = wr16[:, c * E:(c + 1) * E]
                    wl = wr16[:, FC * E + c * E:FC * E + (c + 1) * E]
                    for lhsT, rhs in ((xh, wh), (xl, wh), (xh, wl)):
                        ins = e.matmul(pb[5][:, 0:E], lhsT, rhs,
                                       start=first, stop=False)
                        first = False
                ins = e.matmul(pb[5][:, 0:E], ones1[0:1, :], br_sb[0:1, :],
                               start=False, stop=True)
                return ins
            op("tensor", rmm, inc=s_mm, waits=[(s_ve, s_ve.n), (s_ld, s_ld.n)])
            op("vector", lambda e: e.tensor_copy(stats[:, 8:8 + E], pb[5][:, 0:E]),
               waits=[(s_mm, s_mm.n), (s_sc, s_sc.n)], inc=s_ve)
            drain("vector")
            op("vector", lambda e: e.reduce_max(stats[:, 16:17], stats[:, 8:8 + E],
                                                axis=AX.X, negate=True))
            drain("vector")
            op("vector", lambda e: e.tensor_scalar_mul(stats[:, 17:18],
                                                       stats[:, 16:17], -1.0))
            drain("vector")
            op("vector", lambda e, tl=tl: e.tensor_scalar(oh_tm[:, tl * E:(tl + 1) * E],
                                                          stats[:, 8:8 + E],
                                                          stats[:, 17:18], None,
                                                          AL.is_equal), inc=s_ve)
            op("scalar", lambda e: e.activation(stats[:, 18:18 + E], stats[:, 8:8 + E],
                                                AF.Exp, bias=stats[:, 16:17],
                                                accum_out=stats[:, 26:27]),
               waits=[(s_ve, s_ve.n)], inc=s_sc)
            op("vector", lambda e, tl=tl: e.reciprocal(gatev[:, tl:tl + 1],
                                                       stats[:, 26:27]),
               waits=[(s_sc, s_sc.n)], inc=s_ve)
        router_ve = s_ve.n

        NCH = DFF // 128
        h1g16 = h1g[:].bitcast(BF16)
        NGEN = E * NG
        wsplit_sc = [0, 0]
        wsplit_ve = [0, 0]
        wuse_mm = [0, 0]
        stage_sc = [0, 0, 0, 0]
        stage_ve = [0, 0, 0, 0]
        h1_used = [0, 0]
        pb_h_evac = {0: 0, 1: 0}
        pb_y_evac = {4: 0, 5: router_ve}
        b2_lds = {}

        def w_load_split(gen):
            e_i, g = divmod(gen, NG)
            p = gen % 2
            base = p * 8192
            stgs = (xt, qt, s2a, s2b)
            if e_i not in b2_lds:
                op("sync", lambda e, e_i=e_i: e.dma_start(
                    out=brow[0:1, :], in_=db2[0:1, e_i * D:(e_i + 1) * D]),
                   inc=s_ld, k=16,
                   waits=[(s_mm, wuse_mm[p]) if wuse_mm[p] else None])
                b2_lds[e_i] = s_ld.n
            def l1a(e, e_i=e_i, g=g):
                return e.dma_start(
                    out=xt[:].rearrange("p (c f) -> p c f", f=GDFF),
                    in_=dW1[e_i, 0:512, g * GDFF:(g + 1) * GDFF].rearrange(
                        "(c p) f -> p c f", p=128))
            def l1b(e, e_i=e_i, g=g):
                return e.dma_start(
                    out=qt[:].rearrange("p (c f) -> p c f", f=GDFF),
                    in_=dW1[e_i, 512:1024, g * GDFF:(g + 1) * GDFF].rearrange(
                        "(c p) f -> p c f", p=128))
            def l2a(e, e_i=e_i, g=g):
                return e.dma_start(out=s2a[:],
                                   in_=dW2[e_i, g * GDFF:g * GDFF + 128, :])
            def l2b(e, e_i=e_i, g=g):
                return e.dma_start(out=s2b[:],
                                   in_=dW2[e_i, g * GDFF + 128:g * GDFF + 256, :])
            lds = []
            for i, fn in enumerate((l1a, l1b, l2a, l2b)):
                op("sync", fn, inc=s_ld, k=16,
                   waits=[((s_sc, stage_sc[i]) if stage_sc[i] else None),
                          ((s_ve, stage_ve[i]) if stage_ve[i] else None),
                          ((s_mm, s_mm.n) if gen == 0 else None),
                          ((s_ve, s_ve.n) if gen == 0 else None)])
                lds.append(s_ld.n)
            hidst = (Wb16[:, base + 0:base + 1024],
                     Wb16[:, base + 1024:base + 2048],
                     Wb16[:, base + 4096:base + 5120],
                     Wb16[:, base + 5120:base + 6144])
            lodst = (Wb16[:, base + 2048:base + 3072],
                     Wb16[:, base + 3072:base + 4096],
                     Wb16[:, base + 6144:base + 7168],
                     Wb16[:, base + 7168:base + 8192])
            for i, stg in enumerate(stgs):
                op("scalar", lambda e, d=hidst[i], s=stg: e.copy(d, s[:]),
                   waits=[(s_ld, lds[i]),
                          ((s_mm, wuse_mm[p]) if wuse_mm[p] else None)],
                   inc=s_sc)
                stage_sc[i] = s_sc.n
                op("vector", lambda e, d=lodst[i], s=stg, h=hidst[i]:
                   e.tensor_tensor(d, s[:], h, AL.subtract),
                   waits=[(s_sc, s_sc.n),
                          ((s_mm, wuse_mm[p]) if wuse_mm[p] else None)],
                   inc=s_ve)
                stage_ve[i] = s_ve.n
            wsplit_sc[p] = s_sc.n
            wsplit_ve[p] = s_ve.n

        w_load_split(0)
        pending_ymm = [None]
        dtslot_sc = [0, 0]

        def emit_h1_block(gen, tt):
            e_i, g = divmod(gen, NG)
            p = gen % 2
            base = p * 8192
            gp = (gen * NFT + tt) % 2
            for fc in range(2):
                pbk = fc % 2
                def h1mm(e, g=g, tt=tt, fc=fc, pbk=pbk, base=base):
                    ins = None
                    cnt = 0
                    for c in range(FC):
                        w1h = Wb16[:, base + (c * GDFF + fc * 128):
                                   base + (c * GDFF + fc * 128) + 128]
                        w1l = Wb16[:, base + 2048 + (c * GDFF + fc * 128):
                                   base + 2048 + (c * GDFF + fc * 128) + 128]
                        xh = xh3(c, tt * FTT, (tt + 1) * FTT)
                        xl = xl3(c, tt * FTT, (tt + 1) * FTT)
                        for lhsT, rhs in ((w1h, xh), (w1l, xh), (w1h, xl)):
                            ins = e.matmul(pb[pbk][:, 0:FTT], lhsT, rhs,
                                           start=(cnt == 0),
                                           stop=(cnt == 3 * FC - 1))
                            cnt += 1
                    return ins
                op("tensor", h1mm, inc=s_mm,
                   waits=[(s_sc, wsplit_sc[p]), (s_ve, wsplit_ve[p]),
                          ((s_ve, pb_h_evac[pbk]) if pb_h_evac[pbk] else None),
                          ((s_mm, h1_used[gp]) if fc == 0 and h1_used[gp] else None)])
                bidx = e_i * NCH + g * 2 + fc
                op("vector", lambda e, fc=fc, bidx=bidx, pbk=pbk: e.tensor_scalar(
                    dt_[:, (fc % 2) * 512:(fc % 2 + 1) * 512], pb[pbk][:, 0:FTT],
                    b1_sb[:, bidx:bidx + 1], 0.0, AL.add, AL.max),
                   waits=[(s_mm, s_mm.n),
                          ((s_sc, dtslot_sc[fc % 2]) if dtslot_sc[fc % 2] else None)],
                   inc=s_ve)
                pb_h_evac[pbk] = s_ve.n
                op("scalar", lambda e, fc=fc, gp=gp: e.copy(
                    h1g16[:, gp * 2048 + fc * 512:gp * 2048 + (fc + 1) * 512],
                    dt_[:, (fc % 2) * 512:(fc % 2 + 1) * 512]),
                   waits=[(s_ve, s_ve.n),
                          ((s_mm, h1_used[gp]) if fc == 0 and h1_used[gp] else None)],
                   inc=s_sc)
                dtslot_sc[fc % 2] = s_sc.n
                op("vector", lambda e, fc=fc, gp=gp: e.tensor_tensor(
                    h1g16[:, gp * 2048 + 1024 + fc * 512:
                          gp * 2048 + 1024 + (fc + 1) * 512],
                    dt_[:, (fc % 2) * 512:(fc % 2 + 1) * 512],
                    h1g16[:, gp * 2048 + fc * 512:gp * 2048 + (fc + 1) * 512],
                    AL.subtract),
                   waits=[(s_sc, s_sc.n)], inc=s_ve)
            return s_sc.n, s_ve.n

        def emit_ymm_block(gen, tt, h1_all_sc, h1_all_ve):
            e_i, g = divmod(gen, NG)
            p = gen % 2
            base = p * 8192
            gp = (gen * NFT + tt) % 2
            b2_ld = b2_lds[e_i]
            for ts in range(FTT // 128):
                for jd in range(2):
                    pbk = 4 + (ts * 2 + jd) % 2
                    def ymm(e, g=g, ts=ts, jd=jd, pbk=pbk, base=base, gp=gp):
                        ins = None
                        cnt = 0
                        nmm = 6
                        for fc in range(2):
                            h1h = h1g16[:, gp * 2048 + fc * 512 + ts * 128:
                                        gp * 2048 + fc * 512 + (ts + 1) * 128]
                            h1l = h1g16[:, gp * 2048 + 1024 + fc * 512 + ts * 128:
                                        gp * 2048 + 1024 + fc * 512 + (ts + 1) * 128]
                            w2h = Wb16[:, base + 4096 + fc * D + jd * 512:
                                       base + 4096 + fc * D + (jd + 1) * 512]
                            w2l = Wb16[:, base + 6144 + fc * D + jd * 512:
                                       base + 6144 + fc * D + (jd + 1) * 512]
                            for lhsT, rhs in ((h1h, w2h), (h1l, w2h), (h1h, w2l)):
                                ins = e.matmul(pb[pbk][:], lhsT, rhs,
                                               start=(cnt == 0),
                                               stop=(g > 0 and cnt == nmm - 1))
                                cnt += 1
                        if g == 0:
                            ins = e.matmul(pb[pbk][:], ones1[0:1, :],
                                           brow[0:1, jd * 512:(jd + 1) * 512],
                                           start=False, stop=True)
                        return ins
                    op("tensor", ymm, inc=s_mm,
                       waits=[(s_sc, h1_all_sc), (s_ve, h1_all_ve),
                              (s_ld, b2_ld),
                              ((s_ve, pb_y_evac[pbk]) if pb_y_evac[pbk] else None)])
                    tg = tt * (FTT // 128) + ts
                    op("vector", lambda e, e_i=e_i, tg=tg, jd=jd, pbk=pbk:
                       e.scalar_tensor_tensor(
                           yb[:, tg * D + jd * 512:tg * D + (jd + 1) * 512],
                           pb[pbk][:],
                           oh_tm[:, tg * E + e_i:tg * E + e_i + 1],
                           yb[:, tg * D + jd * 512:tg * D + (jd + 1) * 512],
                           AL.mult, AL.add),
                       waits=[(s_mm, s_mm.n)], inc=s_ve)
                    pb_y_evac[pbk] = s_ve.n
            h1_used[gp] = s_mm.n
            if tt == NFT - 1:
                wuse_mm[p] = s_mm.n

        for gen in range(NGEN):
            for tt in range(NFT):
                sc_v, ve_v = emit_h1_block(gen, tt)
                if pending_ymm[0] is not None:
                    pending_ymm[0]()
                pending_ymm[0] = (lambda gen=gen, tt=tt, sc_v=sc_v, ve_v=ve_v:
                                  emit_ymm_block(gen, tt, sc_v, ve_v))
                if tt == 0 and gen + 1 < NGEN:
                    w_load_split(gen + 1)
        pending_ymm[0]()
        for tl in range(NT):
            op("vector", lambda e, tl=tl: e.tensor_scalar(
                yb[:, tl * D:(tl + 1) * D], yb[:, tl * D:(tl + 1) * D],
                gatev[:, tl:tl + 1], None, AL.mult), inc=s_ve)
            op("scalar", lambda e, tl=tl: e.activation(
                yb[:, tl * D:(tl + 1) * D], yb[:, tl * D:(tl + 1) * D], AF.Silu),
               waits=[(s_ve, s_ve.n)], inc=s_sc)
        g_done_sc = s_sc.n
        if debug:
            op("scalar", lambda e: e.dma_start(
                out=dbg["dbg_g"].rearrange("(tl p) j -> p tl j", p=128),
                in_=yb[:].rearrange("p (tl j) -> p tl j", j=D)),
               inc=s_sst, k=16, waits=[(s_sc, g_done_sc)])
            op("scalar", lambda e: e.dma_start(
                out=dbg["dbg_oh"].rearrange("(tl p) j -> p tl j", p=128),
                in_=oh_tm[:].rearrange("p (tl j) -> p tl j", j=E)),
               inc=s_sst, k=16)
            op("scalar", lambda e: e.dma_start(
                out=dbg["dbg_gate"].rearrange("(tl p) j -> p tl j", p=128),
                in_=gatev[:].rearrange("p (tl j) -> p tl j", j=1),
                allow_slow_non_contiguous=True),
               inc=s_sst, k=16)

        def wload_fold(dw, fold):
            st_sc = [0, 0, 0, 0]
            st_ve = [0, 0, 0, 0]
            stgs = (xt, qt, s2a, s2b)
            entry_mm = s_mm.n
            entry_sst = s_sst.n
            for c in range(FC):
                i = c % 4
                op("sync", lambda e, c=c, dw=dw: e.dma_start(
                    out=stgs[c % 4][:], in_=dw[c * 128:(c + 1) * 128, :]),
                   inc=s_ld, k=16,
                   waits=[((s_sc, st_sc[i]) if st_sc[i] else None),
                          ((s_ve, st_ve[i]) if st_ve[i] else None),
                          ((s_mm, entry_mm) if c < 4 else None),
                          ((s_sst, entry_sst) if c < 4 and entry_sst else None)])
                wld = s_ld.n
                if fold:
                    for jd in range(2):
                        op("tensor", lambda e, c=c, jd=jd, i=i: e.matmul(
                            pbC[jd][0:1, :], betat[:, c:c + 1],
                            stgs[i][:, jd * 512:(jd + 1) * 512],
                            start=(c == 0), stop=(c == FC - 1)),
                           inc=s_mm, waits=[(s_ld, wld),
                                            ((s_ve, s_ve.n) if c == 0 and jd == 0 else None)])
                    op("vector", lambda e, c=c, i=i: e.tensor_scalar(
                        stgs[i][:], stgs[i][:], gammat[:, c:c + 1], None, AL.mult),
                       waits=[(s_mm, s_mm.n)], inc=s_ve)
                op("scalar", lambda e, c=c, i=i: e.copy(
                    wph(c, 0, 1024), stgs[i][:]),
                   waits=[((s_ve, s_ve.n) if fold else (s_ld, wld)),
                          ((s_mm, entry_mm) if c < 2 else None)], inc=s_sc)
                st_sc[i] = s_sc.n
                op("vector", lambda e, c=c, i=i: e.tensor_tensor(
                    wpl(c, 0, 1024), stgs[i][:], wph(c, 0, 1024), AL.subtract),
                   waits=[(s_sc, s_sc.n)], inc=s_ve)
                st_ve[i] = s_ve.n
            if fold:
                for jd in range(2):
                    op("vector", lambda e, jd=jd: e.tensor_copy(
                        brow[0:1, jd * 512:(jd + 1) * 512], pbC[jd][0:1, :]),
                       waits=[(s_mm, s_mm.n)], inc=s_ve)
            return s_sc.n, s_ve.n

        def proj_phase(dw, fold, grep, g_tm, scratch_dram, do_norm, tag):
            wsc, wve = wload_fold(dw, fold)
            s2b16 = s2b[:].bitcast(BF16)
            for tl in range(NT):
                if tag != "v":
                    op("sync", lambda e, tl=tl: e.dma_start(
                        out=xt[:], in_=dx[tl * 128:(tl + 1) * 128, :]),
                       inc=s_ld, k=16,
                       waits=[(s_ve, s_ve.n),
                              ((s_sst, s_sst.n) if s_sst.n else None)])
                    ln_tile(xt[:], qt[:], ld_wait=(s_ld, s_ld.n))
                    ln_tile(qt[:], qt[:])
                    def mk_ev_a(half, pbk):
                        op("vector", lambda e, half=half, pbk=pbk: e.tensor_copy(
                            s2a[:, half * 512:(half + 1) * 512], pbk[:]),
                           waits=[(s_mm, s_mm.n)], inc=s_ve)
                    transpose8(qt[:], mk_ev_a)
                    drain("vector")
                    op("vector", lambda e: e.tensor_copy(s2b16[:, 0:1024], s2a[:]),
                       waits=[((s_sst, s_sst.n) if s_sst.n else None)])
                    drain("vector")
                    op("vector", lambda e: e.tensor_tensor(
                        s2b16[:, 1024:2048], s2a[:], s2b16[:, 0:1024], AL.subtract),
                       inc=s_ve)
                    act16 = s2b16
                else:
                    act16 = None
                for jd in range(2):
                    def pmm(e, tl=tl, jd=jd, act16=act16, fold=fold):
                        ins = None
                        cnt = 0
                        nmm = 3 * FC
                        for c in range(FC):
                            if act16 is None:
                                ah = xh3(c, tl * 128, (tl + 1) * 128)
                                al = xl3(c, tl * 128, (tl + 1) * 128)
                            else:
                                ah = act16[:, c * 128:(c + 1) * 128]
                                al = act16[:, 1024 + c * 128:1024 + (c + 1) * 128]
                            wh = wph(c, jd * 512, (jd + 1) * 512)
                            wl_ = wpl(c, jd * 512, (jd + 1) * 512)
                            for lhsT, rhs in ((ah, wh), (al, wh), (ah, wl_)):
                                ins = e.matmul(pb[jd][:], lhsT, rhs,
                                               start=(cnt == 0),
                                               stop=(not fold and cnt == nmm - 1))
                                cnt += 1
                        if fold:
                            ins = e.matmul(pb[jd][:], ones1[0:1, :],
                                           brow[0:1, jd * 512:(jd + 1) * 512],
                                           start=False, stop=True)
                        return ins
                    op("tensor", pmm, inc=s_mm,
                       waits=[(s_sc, wsc), (s_ve, s_ve.n)])
                    if tag == "v":
                        for hh in range(jd * 8, (jd + 1) * 8):
                            op("vector", lambda e, tl=tl, jd=jd, hh=hh: e.tensor_scalar(
                                qt[:, hh * DH:(hh + 1) * DH],
                                pb[jd][:, (hh - jd * 8) * DH:(hh - jd * 8 + 1) * DH],
                                gk_tm[:, tl * H + hh:tl * H + hh + 1], 0.5,
                                AL.mult, AL.mult),
                               waits=([(s_mm, s_mm.n), (s_sst, s_sst.n)]
                                      if hh == jd * 8 else []),
                               inc=(s_ve if hh == (jd + 1) * 8 - 1 else None))
                    else:
                        op("vector", lambda e, jd=jd: e.tensor_copy(
                            qt[:, jd * 512:(jd + 1) * 512], pb[jd][:]),
                           waits=[(s_mm, s_mm.n),
                                  ((s_sst, s_sst.n) if s_sst.n else None)],
                           inc=s_ve)
                tile_ready = s_ve.n
                if do_norm:
                    op("scalar", lambda e: e.activation(sqt, qt[:], AF.Square),
                       waits=[(s_ve, tile_ready)], inc=s_sc)
                    op("vector", lambda e: e.reduce_sum(
                        nhead[:, 0:H], sqt.rearrange("p (h d) -> p h d", d=DH),
                        axis=AX.X), waits=[(s_sc, s_sc.n)], inc=s_ve)
                    op("scalar", lambda e: e.activation(nhead[:, H:2 * H],
                                                        nhead[:, 0:H], AF.Sqrt),
                       waits=[(s_ve, s_ve.n)], inc=s_sc)
                    op("vector", lambda e: e.tensor_scalar_max(
                        nhead[:, 2 * H:3 * H], nhead[:, H:2 * H], NORM_EPS),
                       waits=[(s_sc, s_sc.n)])
                    drain("vector")
                    op("vector", lambda e: e.reciprocal(nhead[:, 3 * H:4 * H],
                                                        nhead[:, 2 * H:3 * H]))
                    drain("vector")
                    for h in range(H):
                        op("vector", lambda e, h=h: e.tensor_scalar(
                            qt[:, h * DH:(h + 1) * DH], qt[:, h * DH:(h + 1) * DH],
                            nhead[:, 3 * H + h:3 * H + h + 1], None, AL.mult))
                    drain("vector")
                    for h in range(H):
                        op("vector", lambda e, h=h, grep=grep: e.scalar_tensor_tensor(
                            h1g[:, 0:DH], qt[:, h * DH:(h + 1) * DH], 1.0, grep,
                            AL.mult, AL.mult, accum_out=nhead[:, h:h + 1]))
                    drain("vector")
                    op("vector", lambda e: e.engine_nop(), inc=s_ve)
                    op("scalar", lambda e: e.activation(nhead[:, H:2 * H],
                                                        nhead[:, 0:H], AF.Sigmoid),
                       waits=[(s_ve, s_ve.n)], inc=s_sc)
                    op("vector", lambda e, g_tm=g_tm, tl=tl: e.tensor_scalar(
                        g_tm[:, tl * H:(tl + 1) * H], nhead[:, H:2 * H], 0.75, 0.25,
                        AL.mult, AL.add),
                       waits=[(s_sc, s_sc.n)], inc=s_ve)
                    def mk_ev_b(half, pbk):
                        op("vector", lambda e, half=half, pbk=pbk: e.tensor_copy(
                            s2b[:, half * 512:(half + 1) * 512], pbk[:]),
                           waits=[(s_mm, s_mm.n),
                                  ((s_sst, s_sst.n) if s_sst.n else None)],
                           inc=s_ve)
                    transpose8(qt[:], mk_ev_b)
                    xt16 = xt[:].bitcast(BF16)
                    drain("vector")
                    op("vector", lambda e, xt16=xt16: e.tensor_copy(
                        xt16[:, 0:1024], s2b[:]), waits=[(s_ve, s_ve.n)])
                    drain("vector")
                    op("vector", lambda e, xt16=xt16: e.tensor_tensor(
                        xt16[:, 1024:2048], s2b[:], xt16[:, 0:1024], AL.subtract),
                       inc=s_ve)
                    sdh, sdl = scratch_dram
                    op("scalar", lambda e, tl=tl, sd=sdh, xt16=xt16: e.dma_start(
                        out=sd[:, :, tl * 128:(tl + 1) * 128].rearrange(
                            "c p t -> p c t"),
                        in_=xt16[:, 0:1024].rearrange("p (c t) -> p c t", t=128)),
                       waits=[(s_ve, s_ve.n)], inc=s_sst, k=16)
                    op("scalar", lambda e, tl=tl, sd=sdl, xt16=xt16: e.dma_start(
                        out=sd[:, :, tl * 128:(tl + 1) * 128].rearrange(
                            "c p t -> p c t"),
                        in_=xt16[:, 1024:2048].rearrange("p (c t) -> p c t", t=128)),
                       inc=s_sst, k=16)
                    if debug:
                        dk = "dbg_q" if tag == "q" else "dbg_k"
                        op("scalar", lambda e, tl=tl, dk=dk: e.dma_start(
                            out=dbg[dk][tl * 128:(tl + 1) * 128, :], in_=qt[:]),
                           inc=s_sst, k=16)
                else:
                    xt16 = xt[:].bitcast(BF16)
                    drain("vector")
                    op("vector", lambda e, xt16=xt16: e.tensor_copy(
                        xt16[:, 0:1024], qt[:]),
                       waits=[(s_ve, tile_ready),
                              ((s_sst, s_sst.n) if s_sst.n else None)])
                    drain("vector")
                    op("vector", lambda e, xt16=xt16: e.tensor_tensor(
                        xt16[:, 1024:2048], qt[:], xt16[:, 0:1024], AL.subtract),
                       inc=s_ve)
                    op("scalar", lambda e, tl=tl, xt16=xt16: e.dma_start(
                        out=dvg_h[tl * 128:(tl + 1) * 128, :], in_=xt16[:, 0:1024]),
                       waits=[(s_ve, s_ve.n)], inc=s_sst, k=16)
                    op("scalar", lambda e, tl=tl, xt16=xt16: e.dma_start(
                        out=dvg_l[tl * 128:(tl + 1) * 128, :], in_=xt16[:, 1024:2048]),
                       inc=s_sst, k=16)
                    for jd in range(2):
                        op("tensor", lambda e, tl=tl, jd=jd: e.matmul(
                            pb[2 + jd][0:1, :], onesc[:, 0:1],
                            qt[:, jd * 512:(jd + 1) * 512],
                            start=(tl == 0), stop=(tl == NT - 1)),
                           inc=s_mm, waits=[(s_ve, tile_ready)])
                    if tl == NT - 1:
                        for jd in range(2):
                            op("vector", lambda e, jd=jd: e.tensor_copy(
                                csum[0:1, jd * 512:(jd + 1) * 512],
                                pb[2 + jd][0:1, :]),
                               waits=[(s_mm, s_mm.n)], inc=s_ve)

        proj_phase(dwk, True, gkrep[:], gk_tm, (dkn_h, dkn_l), True, "k")
        proj_phase(dwv, False, None, None, None, False, "v")
        proj_phase(dwq, True, gqrep[:], gq_tm, (dqn_h, dqn_l), True, "q")
        if debug:
            op("scalar", lambda e: e.dma_start(out=dbg["dbg_stage"], in_=xt[:]),
               waits=[(s_ve, s_ve.n), (s_sst, s_sst.n)], inc=s_sst, k=16)
            op("sync", lambda e: e.dma_start(
                out=dbg["dbg_rd"].bitcast(BF16)[:, 0:T], in_=dqn_h[1]),
               waits=[(s_sst, s_sst.n)], inc=s_ld, k=16)
            op("sync", lambda e: e.dma_start(
                out=dbg["dbg_rd"].bitcast(BF16)[:, 2 * T:3 * T], in_=dqn_l[1]),
               inc=s_ld, k=16)
            op("sync", lambda e: e.dma_start(
                out=dbg["dbg_rd"].bitcast(BF16)[:, T:T + NKC * DH].rearrange(
                    "p (kc d) -> p kc d", d=DH), 
                in_=dvg_h[:, (2 * FC - 2) * DH:(2 * FC - 2) * DH + DH].rearrange(
                    "(kc p) d -> p kc d", p=128)),
               inc=s_ld, k=16)
        if debug:
            op("scalar", lambda e: e.dma_start(
                out=dbg["dbg_gq"].rearrange("(tl p) h -> p tl h", p=128),
                in_=gq_tm[:].rearrange("p (tl h) -> p tl h", h=H)),
               waits=[(s_ve, s_ve.n)], inc=s_sst, k=16)
            op("scalar", lambda e: e.dma_start(
                out=dbg["dbg_gk"].rearrange("(tl p) h -> p tl h", p=128),
                in_=gk_tm[:].rearrange("p (tl h) -> p tl h", h=H)),
               inc=s_sst, k=16)
            op("scalar", lambda e: e.dma_start(out=dbg["dbg_csum"], in_=csum[:]),
               inc=s_sst, k=16)

        ve_at_D = s_ve.n
        sst_at_D = s_sst.n
        dt16 = dt_[:].bitcast(BF16)
        sq16x = h1g[:].bitcast(BF16)
        gqbc = xt[:, 0:512]
        gqrowA = xt[0:1, 512:1024]
        gqrowB = s2b[0:1, 0:512]
        tfs = (qt[:], s2a[:])
        sab_t = (pbA, pbB)
        sab_sc = [0, 0]
        sab_ve = [0, 0]
        fused_evac_ve = ve_at_D
        gq_ve = ve_at_D
        grow_ve = ve_at_D
        fusedT3 = fusedT[:].rearrange("p (c t) -> p c t", t=T)
        for hp in range(FC):
            for (dst, srcd) in ((qnh_hi, dqn_h), (qnh_lo, dqn_l),
                                (knh_hi, dkn_h), (knh_lo, dkn_l)):
                op("sync", lambda e, dst=dst, srcd=srcd, hp=hp: e.dma_start(
                    out=dst, in_=srcd[hp]),
                   inc=s_ld, k=16,
                   waits=([(s_mm, s_mm.n), (s_sst, s_sst.n)] if dst is qnh_hi else []))
            for (dst, srcd) in ((vgh_hi, dvg_h), (vgh_lo, dvg_l)):
                for h2 in (0, 1):
                    op("sync", lambda e, dst=dst, srcd=srcd, hp=hp, h2=h2: e.dma_start(
                        out=dst[:, h2 * NKC * DH:(h2 + 1) * NKC * DH].rearrange(
                            "p (kc d) -> p kc d", d=DH),
                        in_=srcd[:, (2 * hp + h2) * DH:(2 * hp + h2 + 1) * DH].rearrange(
                            "(kc p) d -> p kc d", p=128)),
                       inc=s_ld, k=16)
            hd_ld = s_ld.n
            for qc in range(NQC):
                ntl = QCW // 128
                def gqtA(e, hp=hp, qc=qc, ntl=ntl):
                    ins = None
                    for i in range(ntl):
                        tl = qc * ntl + i
                        ins = e.transpose(pbC[2][0:1, i * 128:(i + 1) * 128],
                                          gq_tm[:, tl * H + 2 * hp:tl * H + 2 * hp + 1],
                                          eyet[:])
                    return ins
                def gqtB(e, hp=hp, qc=qc, ntl=ntl):
                    ins = None
                    for i in range(ntl):
                        tl = qc * ntl + i
                        ins = e.transpose(pbC[3][0:1, i * 128:(i + 1) * 128],
                                          gq_tm[:, tl * H + 2 * hp + 1:tl * H + 2 * hp + 2],
                                          eyet[:])
                    return ins
                op("tensor", gqtA, waits=[(s_ve, grow_ve)])
                op("tensor", gqtB, inc=s_mm)
                op("vector", lambda e: e.tensor_copy(gqrowA, pbC[2][0:1, 0:QCW]),
                   waits=[(s_mm, s_mm.n),
                          ((s_sst, sst_at_D) if hp == 0 and qc == 0 else None)])
                op("vector", lambda e: e.tensor_copy(gqrowB, pbC[3][0:1, 0:QCW]),
                   inc=s_ve)
                grow_ve = s_ve.n
                def gbc(e):
                    e.matmul(pbC[1][0:64, 0:QCW], ones1[0:1, 0:64], gqrowA,
                             start=True, stop=True, tile_position=(0, 0))
                    return e.matmul(pbC[1][64:128, 0:QCW], ones1[0:1, 0:64], gqrowB,
                                    start=True, stop=True, tile_position=(0, 64))
                op("tensor", gbc, inc=s_mm, waits=[(s_ve, grow_ve), (s_ve, gq_ve)])
                op("vector", lambda e: e.tensor_copy(gqbc, pbC[1][:, 0:QCW]),
                   waits=[(s_mm, s_mm.n)], inc=s_ve)
                gq_ve = s_ve.n
                op("tensor", lambda e, hp=hp: e.matmul(
                    pbC[0][:, 0:QCW], csum[0:1, hp * 128:(hp + 1) * 128],
                    onesr[0:1, 0:QCW], start=True, stop=False),
                   inc=s_mm, waits=[(s_ve, fused_evac_ve), (s_ld, hd_ld)])
                def emit_smm(kc):
                    sl = kc % 2
                    sab = sab_t[sl]
                    def smm(e, kc=kc, qc=qc, sab=sab):
                        ins = None
                        ops = ((qnh_hi, knh_hi), (qnh_lo, knh_hi), (qnh_hi, knh_lo))
                        for i, (qb, kb) in enumerate(ops):
                            e.matmul(sab[:, 0:QCW],
                                     kb[0:64, kc * 128:(kc + 1) * 128],
                                     qb[0:64, qc * QCW:(qc + 1) * QCW],
                                     start=(i == 0), stop=(i == 2),
                                     tile_position=(0, 0))
                            ins = e.matmul(sab[:, 512:512 + QCW],
                                           kb[64:128, kc * 128:(kc + 1) * 128],
                                           qb[64:128, qc * QCW:(qc + 1) * QCW],
                                           start=(i == 0), stop=(i == 2),
                                           tile_position=(64, 0))
                        return ins
                    op("tensor", smm, inc=s_mm,
                       waits=[(s_ld, hd_ld),
                              ((s_sc, sab_sc[sl]) if sab_sc[sl] else (s_ve, ve_at_D))])
                    return s_mm.n
                smm_n_of = {}
                smm_n_of[0] = emit_smm(0)
                for kc in range(NKC):
                    sl = kc % 2
                    sab = sab_t[sl]
                    if kc + 1 < NKC:
                        smm_n_of[kc + 1] = emit_smm(kc + 1)
                    smm_n = smm_n_of[kc]
                    op("scalar", lambda e, sab=sab, sl=sl: e.activation(
                        tfs[sl], sab[:, 0:1024], AF.Tanh, bias=CM405, scale=C15),
                       waits=[(s_mm, smm_n),
                              ((s_ve, sab_ve[sl]) if sab_ve[sl] else None)],
                       inc=s_sc)
                    sab_sc[sl] = s_sc.n
                    op("vector", lambda e, sl=sl: e.tensor_copy(
                        dt16[:, sl * 1024:(sl + 1) * 1024], tfs[sl]),
                       waits=[(s_sc, s_sc.n), (s_mm, smm_n)])
                    drain("vector")
                    op("vector", lambda e, sl=sl: e.tensor_tensor(
                        sq16x[:, 2048 + sl * 1024:2048 + (sl + 1) * 1024], tfs[sl],
                        dt16[:, sl * 1024:(sl + 1) * 1024], AL.subtract),
                       inc=s_ve)
                    sab_ve[sl] = s_ve.n
                    def fmm(e, kc=kc, sl=sl):
                        ins = None
                        ops = ((vgh_hi, dt16), (vgh_lo, dt16), (vgh_hi, None))
                        for i, (vb, tb) in enumerate(ops):
                            toff = sl * 1024 if tb is not None else 2048 + sl * 1024
                            tbb = tb if tb is not None else sq16x
                            e.matmul(pbC[0][0:64, 0:QCW],
                                     vb[:, kc * DH:(kc + 1) * DH],
                                     tbb[:, toff:toff + QCW],
                                     start=False, stop=(kc == NKC - 1 and i == 2),
                                     tile_position=(0, 0))
                            ins = e.matmul(pbC[0][64:128, 0:QCW],
                                           vb[:, NKC * DH + kc * DH:
                                              NKC * DH + (kc + 1) * DH],
                                           tbb[:, toff + 512:toff + 512 + QCW],
                                           start=False,
                                           stop=(kc == NKC - 1 and i == 2),
                                           tile_position=(0, 64))
                        return ins
                    op("tensor", fmm, inc=s_mm, waits=[(s_ve, sab_ve[sl])])
                def fev(e, hp=hp, qc=qc):
                    dstv = fusedT3[:, hp, qc * QCW:(qc + 1) * QCW]
                    return e.tensor_tensor(dstv, pbC[0][:, 0:QCW], gqbc, AL.mult)
                op("vector", fev, waits=[(s_mm, s_mm.n), (s_ve, gq_ve)], inc=s_ve)
                fused_evac_ve = s_ve.n
        if debug:
            op("scalar", lambda e: e.dma_start(out=dbg["dbg_fused"], in_=fusedT[:]),
               waits=[(s_ve, s_ve.n)], inc=s_sst, k=16)
            op("scalar", lambda e: e.dma_start(out=dbg["dbg_gqbc"], in_=gqbc),
               inc=s_sst, k=16)
            op("scalar", lambda e: e.dma_start(out=dbg["dbg_t0"], in_=tfs[0]),
               inc=s_sst, k=16)
            op("scalar", lambda e: e.dma_start(out=dbg["dbg_t1"], in_=tfs[1]),
               inc=s_sst, k=16)
            op("scalar", lambda e: e.dma_start(
                out=dbg["dbg_qnh"].rearrange("p j -> p j").bitcast(BF16)[:, 0:T]
                if False else dbg["dbg_qnh"].bitcast(BF16)[:, 0:T], in_=qnh_hi),
               inc=s_sst, k=16)
            op("scalar", lambda e: e.dma_start(
                out=dbg["dbg_vgh"].bitcast(BF16)[:, 0:2 * NKC * DH], in_=vgh_hi),
               inc=s_sst, k=16)

        def lwo(e):
            dstv = Wbuf[:].rearrange("p (c j) -> p c j", j=D)
            return e.dma_start(out=dstv, in_=dwo.rearrange("(c p) j -> p c j", p=128))
        op("sync", lwo, inc=s_ld, k=16,
           waits=[(s_mm, s_mm.n), ((s_sst, s_sst.n) if s_sst.n else None)])
        op("sync", lambda e: e.dma_start(out=brow[0:1, :], in_=dbo), inc=s_ld, k=16)
        wo_ld = s_ld.n
        sc_at_E = s_sc.n
        pe_evac = {0: 0, 1: 0}
        for tl in range(NT):
            op("sync", lambda e, tl=tl: e.dma_start(
                out=xt[:], in_=dx[tl * 128:(tl + 1) * 128, :]),
               inc=s_ld, k=16, waits=[(s_ve, s_ve.n)])
            x_ld = s_ld.n
            for jd in range(2):
                def omm(e, tl=tl, jd=jd):
                    ins = None
                    for c in range(FC):
                        lhsT = fusedT[:].rearrange("p (c t) -> p c t", t=T)[
                            :, c, tl * 128:(tl + 1) * 128]
                        ins = e.matmul(pb[jd][:], lhsT,
                                       Wbuf[:, c * D + jd * 512:
                                            c * D + (jd + 1) * 512],
                                       start=(c == 0), stop=False)
                    ins = e.matmul(pb[jd][:], ones1[0:1, :],
                                   brow[0:1, jd * 512:(jd + 1) * 512],
                                   start=False, stop=True)
                    return ins
                op("tensor", omm, inc=s_mm,
                   waits=[(s_ld, wo_ld), (s_ve, s_ve.n),
                          ((s_ve, pe_evac[jd]) if pe_evac[jd] else None),
                          ((s_sc, sc_at_E) if tl == 0 else None)])
                op("vector", lambda e, tl=tl, jd=jd: e.scalar_tensor_tensor(
                    s2a[:, jd * 512:(jd + 1) * 512], pb[jd][:], 1.0,
                    yb[:, tl * D + jd * 512:tl * D + (jd + 1) * 512],
                    AL.bypass, AL.mult),
                   waits=[(s_mm, s_mm.n), (s_sc, g_done_sc),
                          ((s_out, s_out.n) if s_out.n else None)])
                drain("vector")
                op("vector", lambda e, jd=jd: e.tensor_tensor(
                    s2a[:, jd * 512:(jd + 1) * 512],
                    s2a[:, jd * 512:(jd + 1) * 512],
                    xt[:, jd * 512:(jd + 1) * 512], AL.add),
                   waits=[(s_ld, x_ld)], inc=s_ve)
                pe_evac[jd] = s_ve.n
            op("scalar", lambda e, tl=tl: e.dma_start(
                out=dout[tl * 128:(tl + 1) * 128, :], in_=s2a[:]),
               waits=[(s_ve, s_ve.n)], inc=s_out, k=16)
        plan["scalar"].append(lambda e, v=s_out.n: e.wait_ge(s_out.sem, v))

        with nc.Block() as block:
            block.sync(lambda e: [f(e) for f in plan["sync"]])
            block.vector(lambda e: [f(e) for f in plan["vector"]])
            block.scalar(lambda e: [f(e) for f in plan["scalar"]])
            block.tensor(lambda e: [f(e) for f in plan["tensor"]])
            block.gpsimd(lambda e: [f(e) for f in plan["gpsimd"]])
    return nc


def make_inputs(xc, wq, wk, wv, g_q, g_k, wo, b_o, gamma, beta, Wr, br,
                W1, b1, W2, b2):
    return {
        "x": np.ascontiguousarray(xc),
        "wq": np.ascontiguousarray(wq),
        "wk": np.ascontiguousarray(wk),
        "wv": np.ascontiguousarray(wv),
        "wo": np.ascontiguousarray(wo),
        "bo": np.ascontiguousarray(b_o.reshape(1, D)),
        "gamma_fm": np.ascontiguousarray(gamma.reshape(FC, 128).T),
        "beta_fm": np.ascontiguousarray(beta.reshape(FC, 128).T),
        "gq_rep": np.ascontiguousarray(np.broadcast_to(g_q, (128, DH))),
        "gk_rep": np.ascontiguousarray(np.broadcast_to(g_k, (128, DH))),
        "wr_sb": np.ascontiguousarray(
            Wr.reshape(FC, 128, E).transpose(1, 0, 2).reshape(128, FC * E)),
        "br": np.ascontiguousarray(br.reshape(1, E)),
        "W1": np.ascontiguousarray(W1),
        "b1_fm": np.ascontiguousarray(
            b1.reshape(E, DFF // 128, 128).transpose(2, 0, 1).reshape(128, -1)),
        "W2": np.ascontiguousarray(W2),
        "b2": np.ascontiguousarray(b2.reshape(1, E * D)),
        "eye": np.eye(128, dtype=np.float32),
    }


_NC_CACHE = {}


def kernel(L0_attn_x, L0_attn_w_q, L0_attn_w_k, L0_attn_w_v, L0_attn_g_q,
           L0_attn_g_k, L0_attn_w_o, L0_attn_b_o, L0_attn_gamma, L0_attn_beta,
           L0_ffn_Wr, L0_ffn_br, L0_ffn_W1, L0_ffn_b1, L0_ffn_W2, L0_ffn_b2):
    x = np.asarray(L0_attn_x, dtype=np.float32)
    B, T, _ = x.shape
    if (T,) not in _NC_CACHE:
        _NC_CACHE[(T,)] = build(T=T)
    nc = _NC_CACHE[(T,)]
    args = [np.asarray(a, dtype=np.float32) for a in
            (L0_attn_w_q, L0_attn_w_k, L0_attn_w_v, L0_attn_g_q, L0_attn_g_k,
             L0_attn_w_o, L0_attn_b_o, L0_attn_gamma, L0_attn_beta,
             L0_ffn_Wr, L0_ffn_br, L0_ffn_W1, L0_ffn_b1, L0_ffn_W2, L0_ffn_b2)]
    in_maps = [make_inputs(x[c], *args) for c in range(B)]
    res = run_bass_kernel_spmd(nc, in_maps, core_ids=list(range(B)))
    return np.stack([np.asarray(r["out"]) for r in res.results], axis=0)
